# version 01
# baseline (speedup 1.0000x reference)
import numpy as np
import ml_dtypes
from contextlib import ExitStack

import concourse.bass as bass
import concourse.mybir as mybir
from concourse.bass_utils import run_bass_kernel_spmd

F32 = mybir.dt.float32
I32 = mybir.dt.int32
BF16 = mybir.dt.bfloat16
AF = mybir.ActivationFunctionType
ALU = mybir.AluOpType
AX = mybir.AxisListType

MAGIC = float(np.float32(12582912.0))
NEG_BIG = float(np.finfo(np.float32).min)
AMAX_FLOOR = float(np.float32(2.0 ** -100))
EPS_LN = 1e-5


class Cfg:
    def __init__(self, T=2048, D=2048, DFF=8192, DK0=512, debug=False):
        self.T, self.D, self.DFF, self.DK0 = T, D, DFF, DK0
        self.B = 4
        self.DH = 128
        self.H0 = D // self.DH
        self.HK0 = DK0 // self.DH
        self.KEY = D
        self.G = T // 128
        self.SL = self.G // 2
        self.TOK = self.SL * 128
        self.DCH = D // 128
        self.debug = debug


def g_of(j, r):
    return 2 * j + ((j + r) % 2)


def owner_rank(kt):
    return (kt // 2 + kt % 2) % 2


def owner_slot(kt):
    return kt // 2


def build(cfg):
    nc = bass.Bass("TRN2", target_bir_lowering=False)
    T, D, DFF, DK0 = cfg.T, cfg.D, cfg.DFF, cfg.DK0
    SL, TOK, DCH = cfg.SL, cfg.TOK, cfg.DCH
    H0, HK0, KEY = cfg.H0, cfg.HK0, cfg.KEY
    GRP = H0 // HK0
    FSUP = min(1024, DFF)
    NDM = D // 512
    SC1 = float(np.float32(1.0 / np.sqrt(np.float32(KEY))))

    def din(name, shape, dt=F32):
        return nc.dram_tensor(name, shape, dt, kind="ExternalInput").ap()

    x_own = din("x_own", [TOK, D])
    wqT = din("wqT", [D, D]); wkT = din("wkT", [D, DK0]); wvT = din("wvT", [D, DK0]); wpT = din("wpT", [D, D])
    w1n0 = din("w1n0", [DFF, D]); w2n0 = din("w2n0", [D, DFF])
    b1h0 = din("b1h0", [1, DFF]); b2h0 = din("b2h0", [1, D])
    lnwh = din("lnwh", [128, D]); lnbh = din("lnbh", [128, D])
    gamh = din("gamh", [128, D]); beth = din("beth", [128, D])
    qkvw = din("qkvw", [D, 3 * KEY]); oprw = din("oprw", [KEY, D])
    w1n1 = din("w1n1", [DFF, D]); w2n1 = din("w2n1", [D, DFF])
    b1h1 = din("b1h1", [1, DFF]); b2h1 = din("b2h1", [1, D])
    cosqh = din("cosqh", [128, SL * 64]); sinqh = din("sinqh", [128, SL * 64])
    coskh = din("coskh", [128, SL * 64]); sinkh = din("sinkh", [128, SL * 64])
    masksh = din("masksh", [128, SL * 256])
    identh = din("identh", [128, 128])
    y_own = nc.dram_tensor("y_own", [TOK, D], F32, kind="ExternalOutput").ap()
    dbg = {}
    if cfg.debug:
        for nm in ("dbg_s1", "dbg_s2", "dbg_s3"):
            dbg[nm] = nc.dram_tensor(nm, [TOK, D], F32, kind="ExternalOutput").ap()

    def dint(name, shape, dt=F32):
        return nc.dram_tensor(name, shape, dt).ap()

    stream_h = dint("stream_h", [TOK, D])
    xb_h = dint("xb_h", [TOK, D], BF16)
    qT_h = dint("qT_h", [max(D, KEY), TOK])
    oT_h = dint("oT_h", [D, TOK])
    kT0o = dint("kT0o", [DK0, TOK]); kT0g = dint("kT0g", [2 * DK0, TOK])
    v0o = dint("v0o", [TOK, DK0]); v0g = dint("v0g", [2 * TOK, DK0])
    kT1o = dint("kT1o", [KEY, TOK]); kT1g = dint("kT1g", [2 * KEY, TOK])
    v1o = dint("v1o", [TOK, KEY]); v1g = dint("v1g", [2 * TOK, KEY])
    w1qT0_h = dint("w1qT0_h", [D, DFF], BF16)
    w2qT0_h = dint("w2qT0_h", [DFF, D], BF16)
    w1qT1_h = dint("w1qT1_h", [D, DFF], BF16)
    w2qT1_h = dint("w2qT1_h", [DFF, D], BF16)

    es = ExitStack()
    sbuf = lambda name, shape, dt=F32: es.enter_context(nc.sbuf_tensor(name, shape, dt))

    class Sem:
        __slots__ = ("h", "n")
        def __init__(self, name):
            self.h = es.enter_context(nc.semaphore(name))
            self.n = 0
        def inc(self, instr, k=1):
            instr.then_inc(self.h, k)
            self.n += k
            return self.n

    dW = Sem("dW"); dX = Sem("dX"); dK = Sem("dK"); dG = Sem("dG")
    sV = Sem("sV"); sA = Sem("sA"); sP = Sem("sP"); sCC = Sem("sCC")
    sB = Sem("sB"); sG = Sem("sG")
    CSEMS = (sV, sA, sP, sCC)

    V, A, P, G_, SY = nc.vector, nc.scalar, nc.tensor, nc.gpsimd, nc.sync

    ident = sbuf("ident", [128, 128])
    identb = sbuf("identb", [128, 128], BF16)
    ones1 = sbuf("ones1", [1, 128])
    onesq = sbuf("onesq", [128, 128])
    masks = sbuf("masks", [128, SL * 256])
    stat = sbuf("stat", [128, 16])
    mu3 = sbuf("mu3", [128, SL]); rstd3 = sbuf("rstd3", [128, SL])
    inv0 = sbuf("inv0", [128, 1]); inv1 = sbuf("inv1", [128, 1])
    negm = sbuf("negm", [128, 1]); ssum = sbuf("ssum", [128, 1]); rsum = sbuf("rsum", [128, 1])
    bb1 = sbuf("bb1", [1, FSUP])
    bb2 = sbuf("bb2", [1, D])

    psBIG = es.enter_context(nc.psum_tensor("psBIG", [128, 2048], F32))
    psH0 = es.enter_context(nc.psum_tensor("psH0", [128, 512], F32))
    psH1 = es.enter_context(nc.psum_tensor("psH1", [128, 512], F32))
    psA_ = es.enter_context(nc.psum_tensor("psA_", [128, 512], F32))
    psO = es.enter_context(nc.psum_tensor("psO", [128, 512], F32))

    def barrier():
        SY.wait_ge(dW.h, dW.n); SY.wait_ge(dX.h, dX.n); SY.wait_ge(dK.h, dK.n)
        SY.sem_inc(sB.h, 1); sB.n += 1
        G_.wait_ge(dG.h, dG.n)
        G_.sem_inc(sB.h, 1); sB.n += 1
        for eng in (V, A, P, G_, SY):
            eng.wait_ge(sB.h, sB.n)
            for s in CSEMS:
                eng.wait_ge(s.h, s.n)

    def vsync():
        V.wait_ge(sV.h, sV.n)

    def vack():
        return sV.inc(V.tensor_copy(stat[:, 15:16], stat[:, 14:15]))

    def gack():
        G_.wait_ge(dG.h, dG.n)
        G_.sem_inc(sG.h, 1); sG.n += 1
        return sG.n

    def dma(q, qsem, out, in_):
        q.dma_start(out=out, in_=in_).then_inc(qsem.h, 16)
        qsem.n += 16
        return qsem.n

    def emit_ln(xt, ncols, dump, xn_out, mu_ap, rstd_ap):
        V.reduce_sum(stat[:, 0:1], xt, axis=AX.X)
        V.drain()
        V.tensor_single_scalar(mu_ap, stat[:, 0:1], 1.0 / ncols, op=ALU.mult)
        V.drain()
        n = sV.inc(V.tensor_scalar(xn_out, xt, mu_ap, None, op0=ALU.subtract))
        A.wait_ge(sV.h, n)
        n = sA.inc(A.activation(dump, xn_out, AF.Square, accum_out=stat[:, 1:2]))
        V.wait_ge(sA.h, n)
        n = sV.inc(V.tensor_scalar(stat[:, 2:3], stat[:, 1:2], 1.0 / ncols, EPS_LN,
                                   op0=ALU.mult, op1=ALU.add))
        A.wait_ge(sV.h, n)
        n = sA.inc(A.activation(stat[:, 3:4], stat[:, 2:3], AF.Sqrt))
        V.wait_ge(sA.h, n)
        V.reciprocal(rstd_ap, stat[:, 3:4])
        V.drain()
        n = sV.inc(V.tensor_scalar(xn_out, xn_out, rstd_ap, None, op0=ALU.mult))
        V.wait_ge(sV.h, n)
        return n

    def emit_transposes(src, soff, dst, doff, k, bf16):
        idt = identb if bf16 else ident
        pa = psA_[:].bitcast(BF16) if bf16 else psA_[:]
        done = 0
        while done < k:
            nb = min(4, k - done)
            P.wait_ge(sV.h, sV.n)
            for c in range(nb):
                r = P.transpose(pa[:, c * 128:(c + 1) * 128],
                                src[:, (soff + done + c) * 128:(soff + done + c + 1) * 128], idt[:])
            n = sP.inc(r)
            V.wait_ge(sP.h, n)
            sV.inc(V.tensor_copy(dst[:, (doff + done) * 128:(doff + done + nb) * 128],
                                 pa[:, 0:nb * 128]))
            done += nb

    def spill_T(src, k, dsth, row0_chunk, col0):
        dst = dsth.rearrange("(c p) t -> p c t", p=128)[:, row0_chunk:row0_chunk + k, col0:col0 + 128]
        return dma(G_, dG, dst, src.rearrange("p (c t) -> p c t", t=128))

    def emit_mx_quant(xin, xout_bf, ncols, amx, expb, scl, invs, vtmp):
        nb = ncols // 32
        x3 = xin.rearrange("p (b k) -> p b k", k=32)
        V.tensor_reduce(amx[:, 0:nb], x3, axis=AX.X, op=ALU.max, apply_absolute_value=True)
        V.drain()
        V.tensor_single_scalar(amx[:, 0:nb], amx[:, 0:nb], AMAX_FLOOR, op=ALU.max)
        V.drain()
        V.tensor_single_scalar(expb[:, 0:nb], amx[:, 0:nb].bitcast(I32), 0x7F800000, op=ALU.bitwise_and)
        V.drain()
        V.tensor_single_scalar(scl[:, 0:nb], expb[:, 0:nb].bitcast(F32),
                               float(np.float32(2.0 ** -6)), op=ALU.mult)
        V.drain()
        V.reciprocal(invs[:, 0:nb], scl[:, 0:nb])
        V.drain()
        inv3 = invs[:, 0:nb].unsqueeze(2).broadcast_to([128, nb, 32])
        v3 = vtmp.rearrange("p (b k) -> p b k", k=32)
        V.tensor_tensor(v3, x3, inv3, op=ALU.mult)
        V.drain()
        V.tensor_scalar(vtmp, vtmp, MAGIC, float(np.float32(MAGIC + 127.0)), op0=ALU.add, op1=ALU.min)
        V.drain()
        V.tensor_scalar(vtmp, vtmp, float(np.float32(MAGIC - 128.0)), -MAGIC, op0=ALU.max, op1=ALU.add)
        V.drain()
        sc3 = scl[:, 0:nb].unsqueeze(2).broadcast_to([128, nb, 32])
        o3 = xout_bf.rearrange("p (b k) -> p b k", k=32)
        n = sV.inc(V.tensor_tensor(o3, v3, sc3, op=ALU.mult))
        V.wait_ge(sV.h, n)
        return n

    dma(SY, dX, ident[:], identh)
    dma(SY, dX, masks[:], masksh)
    G_.memset(ones1[:], 1.0)
    V.memset(stat[:], 0.0)
    V.drain()
    n = sV.inc(G_.memset(onesq[:], 1.0))
    dma(G_, dG, stream_h, x_own)
    V.wait_ge(dX.h, dX.n)
    sV.inc(V.tensor_copy(identb[:], ident[:]))
    barrier()

    with nc.sbuf_tensor("xnT", [128, SL * D], F32) as xnT, \
         nc.sbuf_tensor("ta", [128, D], F32) as ta, \
         nc.sbuf_tensor("tb", [128, D], F32) as tb, \
         nc.sbuf_tensor("tc", [128, D], F32) as tc, \
         nc.sbuf_tensor("wb0", [128, DCH * 512], F32) as wb0, \
         nc.sbuf_tensor("wb1", [128, DCH * 512], F32) as wb1, \
         nc.sbuf_tensor("ev0", [128, 512], F32) as ev0, \
         nc.sbuf_tensor("ev1", [128, 512], F32) as ev1, \
         nc.sbuf_tensor("cq", [128, SL * 64], F32) as cq, \
         nc.sbuf_tensor("sq", [128, SL * 64], F32) as sq, \
         nc.sbuf_tensor("ck", [128, SL * 64], F32) as ck, \
         nc.sbuf_tensor("sk", [128, SL * 64], F32) as sk:
        dma(SY, dX, cq[:], cosqh); dma(SY, dX, sq[:], sinqh)
        dma(SY, dX, ck[:], coskh); dma(SY, dX, sk[:], sinkh)
        V.wait_ge(dX.h, dX.n)
        vack()
        for j in range(SL):
            SY.wait_ge(sV.h, sV.n)
            nld = dma(SY, dX, ta[:], stream_h[j * 128:(j + 1) * 128, :])
            V.wait_ge(dX.h, nld)
            emit_ln(ta[:], D, tb[:], tc[:], stat[:, 4:5], stat[:, 5:6])
            emit_transposes(tc[:], 0, xnT, j * DCH, DCH, False)

        def rope_spill(kind, oc, j):
            ct = (cq if kind == 'q' else ck)[:, j * 64:(j + 1) * 64]
            st_ = (sq if kind == 'q' else sk)[:, j * 64:(j + 1) * 64]
            cb = ct.unsqueeze(1).broadcast_to([128, 4, 64])
            sb_ = st_.unsqueeze(1).broadcast_to([128, 4, 64])
            e3 = ev0[:, 0:512].rearrange("p (h c) -> p h c", c=128)
            o3 = ev1[:, 0:512].rearrange("p (h c) -> p h c", c=128)
            x1 = e3[:, :, 0:64]; x2 = e3[:, :, 64:128]
            t1 = ta[:, 0:256].rearrange("p (h c) -> p h c", c=64)
            t2 = ta[:, 256:512].rearrange("p (h c) -> p h c", c=64)
            V.tensor_tensor(t1, x1, cb, op=ALU.mult)
            V.tensor_tensor(t2, x2, sb_, op=ALU.mult)
            V.drain()
            V.tensor_tensor(o3[:, :, 0:64], t1, t2, op=ALU.subtract)
            V.drain()
            V.tensor_tensor(t1, x2, cb, op=ALU.mult)
            V.tensor_tensor(t2, x1, sb_, op=ALU.mult)
            V.drain()
            n = sV.inc(V.tensor_tensor(o3[:, :, 64:128], t1, t2, op=ALU.add))
            P.wait_ge(sV.h, n)
            for c in range(4):
                r = P.transpose(psA_[:, c * 128:(c + 1) * 128],
                                ev1[:, c * 128:(c + 1) * 128], ident[:])
            n = sP.inc(r)
            V.wait_ge(sP.h, n)
            V.wait_ge(dG.h, dG.n)
            n = sV.inc(V.tensor_copy(ev0[:, 0:512], psA_[:]))
            G_.wait_ge(sV.h, n)
            spill_T(ev0[:, 0:512], 4, qT_h if kind == 'q' else kT0o, oc * 4, j * 128)

        wl = 0
        for wT_hbm, nout, kind in ((wqT, D, 'q'), (wkT, DK0, 'k'), (wvT, DK0, 'v')):
            for oc in range(nout // 512):
                wb = (wb0, wb1)[wl % 2]; wl += 1
                SY.wait_ge(sP.h, sP.n)
                src = wT_hbm.rearrange("(dc p) n -> p dc n", p=128)[:, :, oc * 512:(oc + 1) * 512]
                wsem = dma(SY, dW, wb[:].rearrange("p (dc n) -> p dc n", n=512), src)
                for j in range(SL):
                    psu = (psH0, psH1)[j % 2]
                    P.wait_ge(dW.h, wsem)
                    P.wait_ge(sV.h, sV.n)
                    for dc in range(DCH):
                        r = P.matmul(psu[:], xnT[:, (j * DCH + dc) * 128:(j * DCH + dc + 1) * 128],
                                     wb[:, dc * 512:(dc + 1) * 512],
                                     start=(dc == 0), stop=(dc == DCH - 1))
                    n = sP.inc(r)
                    V.wait_ge(sP.h, n)
                    V.wait_ge(dG.h, dG.n)
                    n = sV.inc(V.tensor_copy(ev0[:, 0:512], psu[:]))
                    if kind == 'v':
                        G_.wait_ge(sV.h, n)
                        dma(G_, dG, v0o[j * 128:(j + 1) * 128, oc * 512:(oc + 1) * 512], ev0[:, 0:512])
                    else:
                        V.wait_ge(sV.h, n)
                        rope_spill(kind, oc, j)
        barrier()

        sCC.inc(G_.collective_compute(
            "AllGather", ALU.bypass, replica_groups=[[0, 1], [2, 3], [4, 5], [6, 7]],
            ins=[kT0o.tensor.ap().opt()], outs=[kT0g.tensor.ap().opt()]))
        sCC.inc(G_.collective_compute(
            "AllGather", ALU.bypass, replica_groups=[[0, 1], [2, 3], [4, 5], [6, 7]],
            ins=[v0o.tensor.ap().opt()], outs=[v0g.tensor.ap().opt()]))
        G_.wait_ge(sCC.h, sCC.n)
        barrier()

    with nc.sbuf_tensor("kbuf", [128, HK0 * 2048], F32) as kbuf, \
         nc.sbuf_tensor("vbuf", [128, 16 * DK0], F32) as vbuf, \
         nc.sbuf_tensor("qTs", [128, D], F32) as qTs, \
         nc.sbuf_tensor("Pb", [128, 2048], F32) as Pb, \
         nc.sbuf_tensor("PTb", [128, 2048], F32) as PTb, \
         nc.sbuf_tensor("Ot", [128, D], F32) as Ot, \
         nc.sbuf_tensor("oTe", [128, 512], F32) as oTe:
        for j in range(SL):
            nkt = 2 * j + 2
            nk = nkt * 128
            SY.wait_ge(sP.h, sP.n)
            qsrc = qT_h.rearrange("(dc p) t -> p dc t", p=128)[:, 0:DCH, j * 128:(j + 1) * 128]
            qsem = dma(SY, dX, qTs[:].rearrange("p (dc t) -> p dc t", t=128), qsrc)
            for kt in range(nkt):
                r_ = owner_rank(kt); sl = owner_slot(kt)
                ksrc = kT0g.rearrange("(c p) t -> p c t", p=128)[
                    :, r_ * HK0:(r_ + 1) * HK0, sl * 128:(sl + 1) * 128]
                kdst = kbuf[:].rearrange("p (c n) -> p c n", n=2048)[:, :, kt * 128:(kt + 1) * 128]
                dma(SY, dK, kdst, ksrc)
                vsrc = v0g[(r_ * TOK + sl * 128):(r_ * TOK + sl * 128 + 128), :]
                dma(SY, dK, vbuf[:, kt * DK0:(kt + 1) * DK0], vsrc)
            ldsem = dK.n
            for h in range(H0):
                kvh = h // GRP
                P.wait_ge(dK.h, ldsem)
                P.wait_ge(dX.h, qsem)
                P.wait_ge(sA.h, sA.n)
                P.wait_ge(sV.h, sV.n)
                for c in range((nk + 511) // 512):
                    w = min(512, nk - c * 512)
                    r = P.matmul(psBIG[:, c * 512:c * 512 + w],
                                 qTs[:, h * 128:(h + 1) * 128],
                                 kbuf[:, kvh * 2048 + c * 512: kvh * 2048 + c * 512 + w],
                                 start=True, stop=True)
                npe = sP.inc(r)
                V.wait_ge(sP.h, npe)
                V.tensor_tensor(psBIG[:, nk - 256:nk], psBIG[:, nk - 256:nk],
                                masks[:, j * 256:(j + 1) * 256], op=ALU.add)
                V.drain()
                n = sV.inc(V.reduce_max(negm[:], psBIG[:, 0:nk], axis=AX.X, negate=True))
                A.wait_ge(sV.h, n)
                na = sA.inc(A.activation(Pb[:, 0:nk], psBIG[:, 0:nk], AF.Exp,
                                         bias=negm[:], accum_out=ssum[:]))
                V.wait_ge(sA.h, na)
                n = sV.inc(V.reciprocal(rsum[:], ssum[:]))
                emit_transposes(Pb[:], 0, PTb, 0, nkt, False)
                P.wait_ge(sV.h, sV.n)
                for kt in range(nkt):
                    r = P.matmul(psO[:, 0:128], PTb[:, kt * 128:(kt + 1) * 128],
                                 vbuf[:, kt * DK0 + kvh * 128: kt * DK0 + (kvh + 1) * 128],
                                 start=(kt == 0), stop=(kt == nkt - 1))
                n = sP.inc(r)
                V.wait_ge(sP.h, n)
                sV.inc(V.tensor_scalar(Ot[:, h * 128:(h + 1) * 128], psO[:, 0:128],
                                       rsum[:], None, op0=ALU.mult))
            done = 0
            while done < DCH:
                nb = min(4, DCH - done)
                P.wait_ge(sV.h, sV.n)
                for c in range(nb):
                    r = P.transpose(psA_[:, c * 128:(c + 1) * 128],
                                    Ot[:, (done + c) * 128:(done + c + 1) * 128], ident[:])
                n = sP.inc(r)
                V.wait_ge(sP.h, n)
                V.wait_ge(dG.h, dG.n)
                n = sV.inc(V.tensor_copy(oTe[:, 0:nb * 128], psA_[:, 0:nb * 128]))
                G_.wait_ge(sV.h, n)
                spill_T(oTe[:, 0:nb * 128], nb, oT_h, done, j * 128)
                done += nb
        barrier()

    _opctr = [0]

    def emit_out_proj(wT_hbm, nin_ch, resid_fn):
        _opctr[0] += 1
        tg = str(_opctr[0])
        with nc.sbuf_tensor("pwb0" + tg, [128, nin_ch * 512], F32) as pwb0, \
             nc.sbuf_tensor("pwb1" + tg, [128, nin_ch * 512], F32) as pwb1, \
             nc.sbuf_tensor("poT" + tg, [128, nin_ch * 128], F32) as poT, \
             nc.sbuf_tensor("prt0" + tg, [128, 512], F32) as prt0, \
             nc.sbuf_tensor("prt1" + tg, [128, 512], F32) as prt1, \
             nc.sbuf_tensor("prt2" + tg, [128, 512], F32) as prt2:
            for oc in range(D // 512):
                wb = (pwb0, pwb1)[oc % 2]
                SY.wait_ge(sP.h, sP.n)
                src = wT_hbm.rearrange("(dc p) n -> p dc n", p=128)[:, :, oc * 512:(oc + 1) * 512]
                wsem = dma(SY, dW, wb[:].rearrange("p (dc n) -> p dc n", n=512), src)
                for j in range(SL):
                    SY.wait_ge(sP.h, sP.n)
                    xsrc = oT_h.rearrange("(dc p) t -> p dc t", p=128)[:, 0:nin_ch, j * 128:(j + 1) * 128]
                    nx = dma(SY, dX, poT[:].rearrange("p (dc t) -> p dc t", t=128), xsrc)
                    psu = (psH0, psH1)[j % 2]
                    P.wait_ge(dW.h, wsem)
                    P.wait_ge(dX.h, nx)
                    P.wait_ge(sV.h, sV.n)
                    for dc in range(nin_ch):
                        r = P.matmul(psu[:], poT[:, dc * 128:(dc + 1) * 128],
                                     wb[:, dc * 512:(dc + 1) * 512],
                                     start=(dc == 0), stop=(dc == nin_ch - 1))
                    npe = sP.inc(r)
                    resid_fn(j, oc, psu, npe, prt0, prt1, prt2)
            barrier()

    def resid_l0(j, oc, psu, npe, rt0, rt1, rt2):
        G_.wait_ge(sV.h, sV.n)
        nl = dma(G_, dG, rt0[:], stream_h[j * 128:(j + 1) * 128, oc * 512:(oc + 1) * 512])
        V.wait_ge(dG.h, nl)
        V.wait_ge(sP.h, npe)
        n = sV.inc(V.tensor_tensor(rt1[:], rt0[:], psu[:], op=ALU.add))
        G_.wait_ge(sV.h, n)
        dma(G_, dG, stream_h[j * 128:(j + 1) * 128, oc * 512:(oc + 1) * 512], rt1[:])

    emit_out_proj(wpT, DCH, resid_l0)

    if cfg.debug:
        G_.wait_ge(dG.h, dG.n)
        dma(G_, dG, dbg["dbg_s1"], stream_h)
        barrier()

    def emit_quant_spill(w_hbm, wqT_hbm, nrows, ncols, ternary, inv_ap, tag):
        with nc.sbuf_tensor("qw0" + tag, [128, ncols], F32) as qw0, \
             nc.sbuf_tensor("qw1" + tag, [128, ncols], F32) as qw1, \
             nc.sbuf_tensor("qv" + tag, [128, ncols], F32) as qv, \
             nc.sbuf_tensor("qb" + tag, [128, ncols], BF16) as qb, \
             nc.sbuf_tensor("qe" + tag, [128, 512], BF16) as qe, \
             nc.sbuf_tensor("qam" + tag, [128, ncols // 32], F32) as qam, \
             nc.sbuf_tensor("qex" + tag, [128, ncols // 32], I32) as qex, \
             nc.sbuf_tensor("qsc" + tag, [128, ncols // 32], F32) as qsc, \
             nc.sbuf_tensor("qiv" + tag, [128, ncols // 32], F32) as qiv:
            for rc in range(nrows // 128):
                wbq = (qw0, qw1)[rc % 2]
                SY.wait_ge(sV.h, sV.n)
                nl = dma(SY, dW, wbq[:], w_hbm[rc * 128:(rc + 1) * 128, :])
                V.wait_ge(dW.h, nl)
                if ternary:
                    V.tensor_scalar(qv[:], wbq[:], inv_ap, None, op0=ALU.mult)
                    V.drain()
                    V.tensor_scalar(qv[:], qv[:], MAGIC, float(np.float32(MAGIC + 1.0)),
                                    op0=ALU.add, op1=ALU.min)
                    V.drain()
                    nq = sV.inc(V.tensor_scalar(qb[:], qv[:], float(np.float32(MAGIC - 1.0)), -MAGIC,
                                                 op0=ALU.max, op1=ALU.add))
                    V.wait_ge(sV.h, nq)
                else:
                    emit_mx_quant(wbq[:], qb[:], ncols, qam, qex, qsc, qiv, qv[:])
                done = 0
                while done < ncols // 128:
                    nb_ = min(4, ncols // 128 - done)
                    P.wait_ge(sV.h, sV.n)
                    pa = psA_[:].bitcast(BF16)
                    for c in range(nb_):
                        r = P.transpose(pa[:, c * 128:(c + 1) * 128],
                                        qb[:, (done + c) * 128:(done + c + 1) * 128], identb[:])
                    n = sP.inc(r)
                    V.wait_ge(sP.h, n)
                    V.wait_ge(dG.h, dG.n)
                    n = sV.inc(V.tensor_copy(qe[:, 0:nb_ * 128], pa[:, 0:nb_ * 128]))
                    G_.wait_ge(sV.h, n)
                    dst = wqT_hbm.rearrange("(fc p) d -> p fc d", p=128)[:, done:done + nb_, rc * 128:(rc + 1) * 128]
                    dma(G_, dG, dst, qe[:, 0:nb_ * 128].rearrange("p (c t) -> p c t", t=128))
                    done += nb_
            barrier()

    _ffnctr = [0]

    def emit_ffn(w1qT_hbm, w2qT_hbm, b1_hbm, b2_hbm, xT_list, l1, ntt, out_fn, fsup):
        nfs = DFF // fsup
        nfg = fsup // 512
        xis = (0, 1) if l1 else (0,)
        _ffnctr[0] += 1
        tg = str(_ffnctr[0])
        with nc.sbuf_tensor("fh2" + tg, [128, ntt * D], F32) as h2acc, \
             nc.sbuf_tensor("fw1" + tg, [128, DCH * fsup], BF16) as w1b, \
             nc.sbuf_tensor("fw2" + tg, [128, (fsup // 128) * D], BF16) as w2b, \
             nc.sbuf_tensor("fg0" + tg, [128, 512], BF16) as fg0, \
             nc.sbuf_tensor("fg1" + tg, [128, 512], BF16) as fg1, \
             nc.sbuf_tensor("fgf" + tg, [128, 512], F32) as fgf, \
             nc.sbuf_tensor("fgh" + tg, [128, 512], BF16) as fgh, \
             nc.sbuf_tensor("fgl" + tg, [128, 512], BF16) as fgl, \
             nc.sbuf_tensor("fq2" + tg, [128, 512], BF16) as fq2, \
             nc.sbuf_tensor("fqT" + tg, [128, 8 * 128], BF16) as fqT, \
             nc.sbuf_tensor("fam" + tg, [128, 16], F32) as fam, \
             nc.sbuf_tensor("fex" + tg, [128, 16], I32) as fex, \
             nc.sbuf_tensor("fsc" + tg, [128, 16], F32) as fsc, \
             nc.sbuf_tensor("fiv" + tg, [128, 16], F32) as fiv, \
             nc.sbuf_tensor("fvt" + tg, [128, 512], F32) as fvt:
            nbb2 = dma(SY, dX, bb2[0:1, :], b2_hbm[0:1, :])
            for fs in range(nfs):
                SY.wait_ge(sP.h, sP.n)
                w1src = w1qT_hbm.rearrange("(dc p) f -> p dc f", p=128)[:, :, fs * fsup:(fs + 1) * fsup]
                dma(SY, dW, w1b[:].rearrange("p (dc f) -> p dc f", f=fsup), w1src)
                w2src = w2qT_hbm.rearrange("(fc p) d -> p fc d", p=128)[
                    :, fs * (fsup // 128):(fs + 1) * (fsup // 128), :]
                dma(SY, dW, w2b[:].rearrange("p (fc d) -> p fc d", d=D), w2src)
                dma(SY, dW, bb1[0:1, 0:fsup], b1_hbm[0:1, fs * fsup:(fs + 1) * fsup])
                wsem = dW.n
                for tt in range(ntt):
                    P.wait_ge(dW.h, wsem)
                    P.wait_ge(dX.h, nbb2)
                    if fs == 0:
                        P.wait_ge(sV.h, sV.n)
                        for dmc in range(NDM):
                            P.matmul(psBIG[:, dmc * 512:(dmc + 1) * 512], ones1[:],
                                     bb2[0:1, dmc * 512:(dmc + 1) * 512], start=True, stop=False)
                    for fg in range(nfg):
                        psu = (psH0, psH1)[fg % 2]
                        P.wait_ge(sV.h, sV.n)
                        P.matmul(psu[:], ones1[:], bb1[0:1, fg * 512:(fg + 1) * 512],
                                 start=True, stop=False)
                        for dc in range(DCH):
                            for xi in xis:
                                r = P.matmul(
                                    psu[:],
                                    xT_list[xi][:, (tt * DCH + dc) * 128:(tt * DCH + dc + 1) * 128],
                                    w1b[:, dc * fsup + fg * 512: dc * fsup + fg * 512 + 512],
                                    start=False, stop=(dc == DCH - 1 and xi == xis[-1]))
                        npe = sP.inc(r)
                        V.wait_ge(sP.h, npe)
                        if not l1:
                            n = sV.inc(V.tensor_copy(fg0[:], psu[:]))
                            A.wait_ge(sV.h, n)
                            na = sA.inc(A.activation(fg1[:], fg0[:], AF.Gelu_apprx_tanh))
                            V.wait_ge(sA.h, na)
                            V.tensor_copy(fgf[:], fg1[:])
                            V.drain()
                            emit_mx_quant(fgf[:], fq2[:], 512, fam, fex, fsc, fiv, fvt[:])
                            emit_transposes(fq2[:], 0, fqT, 0, 4, True)
                        else:
                            n = sV.inc(V.tensor_copy(fgf[:], psu[:]))
                            A.wait_ge(sV.h, n)
                            na = sA.inc(A.activation(fgf[:], fgf[:], AF.Gelu))
                            V.wait_ge(sA.h, na)
                            V.tensor_copy(fgh[:], fgf[:])
                            V.drain()
                            V.tensor_tensor(fvt[:], fgf[:], fgh[:], op=ALU.subtract)
                            V.drain()
                            sV.inc(V.tensor_copy(fgl[:], fvt[:]))
                            emit_transposes(fgh[:], 0, fqT, 0, 4, True)
                            emit_transposes(fgl[:], 0, fqT, 4, 4, True)
                        P.wait_ge(sV.h, sV.n)
                        for dmc in range(NDM):
                            for sub in range(4):
                                fc = fg * 4 + sub
                                for xi in xis:
                                    first = (fs > 0 and fg == 0 and sub == 0 and xi == 0)
                                    last = (fg == nfg - 1 and sub == 3 and xi == xis[-1])
                                    r = P.matmul(
                                        psBIG[:, dmc * 512:(dmc + 1) * 512],
                                        fqT[:, (xi * 4 + sub) * 128:(xi * 4 + sub + 1) * 128],
                                        w2b[:, fc * D + dmc * 512: fc * D + dmc * 512 + 512],
                                        start=first, stop=last)
                        sP.inc(r)
                    V.wait_ge(sP.h, sP.n)
                    vsync()
                    if fs == 0:
                        n = sV.inc(V.tensor_copy(h2acc[:, tt * D:(tt + 1) * D], psBIG[:, 0:D]))
                    else:
                        n = sV.inc(V.tensor_tensor(h2acc[:, tt * D:(tt + 1) * D],
                                                   h2acc[:, tt * D:(tt + 1) * D],
                                                   psBIG[:, 0:D], op=ALU.add))
                    P.wait_ge(sV.h, n)
            for tt in range(ntt):
                out_fn(tt, h2acc)
            barrier()

    emit_quant_spill(w1n0, w1qT0_h, DFF, D, False, None, "a")
    emit_quant_spill(w2n0, w2qT0_h, D, DFF, False, None, "b")

    with nc.sbuf_tensor("q1T", [128, SL * D], BF16) as q1T:
        with nc.sbuf_tensor("f2a", [128, D], F32) as f2a, \
             nc.sbuf_tensor("f2b", [128, D], F32) as f2b, \
             nc.sbuf_tensor("f2c", [128, D], F32) as f2c, \
             nc.sbuf_tensor("xbt", [128, D], BF16) as xbt, \
             nc.sbuf_tensor("q1b", [128, D], BF16) as q1b, \
             nc.sbuf_tensor("lnw", [128, D], F32) as lnw, \
             nc.sbuf_tensor("lnb", [128, D], F32) as lnb, \
             nc.sbuf_tensor("am2", [128, D // 32], F32) as am2, \
             nc.sbuf_tensor("ex2", [128, D // 32], I32) as ex2, \
             nc.sbuf_tensor("sc2", [128, D // 32], F32) as sc2, \
             nc.sbuf_tensor("iv2", [128, D // 32], F32) as iv2:
            dma(SY, dX, lnw[:], lnwh)
            nl = dma(SY, dX, lnb[:], lnbh)
            V.wait_ge(dX.h, nl)
            vack()
            for j in range(SL):
                SY.wait_ge(sV.h, sV.n)
                nl = dma(SY, dX, f2a[:], stream_h[j * 128:(j + 1) * 128, :])
                V.wait_ge(dX.h, nl)
                emit_ln(f2a[:], D, f2b[:], f2c[:], stat[:, 4:5], stat[:, 5:6])
                V.wait_ge(dG.h, dG.n)
                V.tensor_copy(xbt[:], f2c[:])
                V.drain()
                n = sV.inc(V.tensor_copy(f2c[:], xbt[:]))
                V.wait_ge(sV.h, n)
                G_.wait_ge(sV.h, n)
                dma(G_, dG, xb_h[j * 128:(j + 1) * 128, :], xbt[:])
                emit_ln(f2c[:], D, f2a[:], f2b[:], stat[:, 6:7], stat[:, 7:8])
                V.tensor_tensor(f2b[:], f2b[:], lnw[:], op=ALU.mult)
                V.drain()
                V.tensor_tensor(f2b[:], f2b[:], lnb[:], op=ALU.add)
                V.drain()
                V.tensor_copy(q1b[:], f2b[:])
                V.drain()
                V.tensor_copy(f2b[:], q1b[:])
                V.drain()
                emit_mx_quant(f2b[:], q1b[:], D, am2, ex2, sc2, iv2, f2a[:])
                emit_transposes(q1b[:], 0, q1T, j * DCH, DCH, True)
            barrier()

        with nc.sbuf_tensor("o2a", [128, D], BF16) as o2a, \
             nc.sbuf_tensor("o2b", [128, D], BF16) as o2b, \
             nc.sbuf_tensor("o2c", [128, D], F32) as o2c:
            def out_l0(tt, h2acc):
                vsync()
                SY.wait_ge(sG.h, gack())
                nl = dma(SY, dX, o2a[:], xb_h[tt * 128:(tt + 1) * 128, :])
                nl2 = dma(SY, dX, o2c[:], stream_h[tt * 128:(tt + 1) * 128, :])
                V.wait_ge(dX.h, nl2)
                V.tensor_copy(o2b[:], h2acc[:, tt * D:(tt + 1) * D])
                V.drain()
                V.tensor_tensor(o2b[:], o2a[:], o2b[:], op=ALU.add)
                V.drain()
                n = sV.inc(V.tensor_tensor(o2c[:], o2c[:], o2b[:], op=ALU.add))
                G_.wait_ge(sV.h, n)
                dma(G_, dG, stream_h[tt * 128:(tt + 1) * 128, :], o2c[:])

            emit_ffn(w1qT0_h, w2qT0_h, b1h0, b2h0, [q1T], False, SL, out_l0, 512)

    if cfg.debug:
        G_.wait_ge(dG.h, dG.n)
        dma(G_, dG, dbg["dbg_s2"], stream_h)
        barrier()

    with nc.sbuf_tensor("xnT3", [128, SL * D], F32) as xnT3, \
         nc.sbuf_tensor("t3a", [128, D], F32) as t3a, \
         nc.sbuf_tensor("t3b", [128, D], F32) as t3b, \
         nc.sbuf_tensor("t3c", [128, D], F32) as t3c, \
         nc.sbuf_tensor("gam", [128, D], F32) as gam, \
         nc.sbuf_tensor("bet", [128, D], F32) as bet, \
         nc.sbuf_tensor("wb30", [128, DCH * 512], F32) as wb30, \
         nc.sbuf_tensor("wb31", [128, DCH * 512], F32) as wb31, \
         nc.sbuf_tensor("ev30", [128, 512], F32) as ev30, \
         nc.sbuf_tensor("ev31", [128, 512], F32) as ev31:
        dma(SY, dX, gam[:], gamh)
        nl = dma(SY, dX, bet[:], beth)
        V.wait_ge(dX.h, nl)
        vack()
        for j in range(SL):
            SY.wait_ge(sV.h, sV.n)
            nl = dma(SY, dX, t3a[:], stream_h[j * 128:(j + 1) * 128, :])
            V.wait_ge(dX.h, nl)
            emit_ln(t3a[:], D, t3b[:], t3c[:], mu3[:, j:j + 1], rstd3[:, j:j + 1])
            emit_ln(t3c[:], D, t3a[:], t3b[:], stat[:, 4:5], stat[:, 5:6])
            V.tensor_tensor(t3b[:], t3b[:], gam[:], op=ALU.mult)
            V.drain()
            n = sV.inc(V.tensor_tensor(t3b[:], t3b[:], bet[:], op=ALU.add))
            V.wait_ge(sV.h, n)
            emit_transposes(t3b[:], 0, xnT3, j * DCH, DCH, False)
        wl = 0
        for sec, base in (('q', 0), ('k', KEY), ('v', 2 * KEY)):
            for oc in range(KEY // 512):
                wb = (wb30, wb31)[wl % 2]; wl += 1
                SY.wait_ge(sP.h, sP.n)
                src = qkvw.rearrange("(dc p) n -> p dc n", p=128)[:, :, base + oc * 512: base + (oc + 1) * 512]
                wsem = dma(SY, dW, wb[:].rearrange("p (dc n) -> p dc n", n=512), src)
                for j in range(SL):
                    psu = (psH0, psH1)[j % 2]
                    P.wait_ge(dW.h, wsem)
                    P.wait_ge(sV.h, sV.n)
                    for dc in range(DCH):
                        r = P.matmul(psu[:], xnT3[:, (j * DCH + dc) * 128:(j * DCH + dc + 1) * 128],
                                     wb[:, dc * 512:(dc + 1) * 512],
                                     start=(dc == 0), stop=(dc == DCH - 1))
                    n = sP.inc(r)
                    V.wait_ge(sP.h, n)
                    V.wait_ge(dG.h, dG.n)
                    if sec == 'q':
                        n = sV.inc(V.tensor_single_scalar(ev30[:, 0:512], psu[:], SC1, op=ALU.mult))
                    else:
                        n = sV.inc(V.tensor_copy(ev30[:, 0:512], psu[:]))
                    if sec == 'v':
                        G_.wait_ge(sV.h, n)
                        dma(G_, dG, v1o[j * 128:(j + 1) * 128, oc * 512:(oc + 1) * 512], ev30[:, 0:512])
                    else:
                        V.wait_ge(sV.h, n)
                        emit_transposes(ev30[:, 0:512], 0, ev31, 0, 4, False)
                        G_.wait_ge(sV.h, sV.n)
                        spill_T(ev31[:, 0:512], 4, qT_h if sec == 'q' else kT1o, oc * 4, j * 128)
        barrier()
        sCC.inc(G_.collective_compute(
            "AllGather", ALU.bypass, replica_groups=[[0, 1], [2, 3], [4, 5], [6, 7]],
            ins=[kT1o.tensor.ap().opt()], outs=[kT1g.tensor.ap().opt()]))
        sCC.inc(G_.collective_compute(
            "AllGather", ALU.bypass, replica_groups=[[0, 1], [2, 3], [4, 5], [6, 7]],
            ins=[v1o.tensor.ap().opt()], outs=[v1g.tensor.ap().opt()]))
        G_.wait_ge(sCC.h, sCC.n)
        barrier()

    with nc.sbuf_tensor("kb3", [128, DCH * 512], F32) as kb3, \
         nc.sbuf_tensor("vb3", [128, 4 * KEY], F32) as vb3, \
         nc.sbuf_tensor("qT3", [128, D], F32) as qT3, \
         nc.sbuf_tensor("Pb3", [128, 2048], F32) as Pb3, \
         nc.sbuf_tensor("PT3", [128, 2048], F32) as PT3, \
         nc.sbuf_tensor("Ot3", [128, D], F32) as Ot3, \
         nc.sbuf_tensor("oTe3", [128, 512], F32) as oTe3:
        for j in range(SL):
            nkt = 2 * j + 2
            nk = nkt * 128
            SY.wait_ge(sP.h, sP.n)
            qsrc = qT_h.rearrange("(dc p) t -> p dc t", p=128)[:, 0:DCH, j * 128:(j + 1) * 128]
            qsem = dma(SY, dX, qT3[:].rearrange("p (dc t) -> p dc t", t=128), qsrc)
            for c in range((nk + 511) // 512):
                tiles = range(c * 4, min(c * 4 + 4, nkt))
                SY.wait_ge(sP.h, sP.n)
                for kt in tiles:
                    r_ = owner_rank(kt); sl = owner_slot(kt)
                    ksrc = kT1g.rearrange("(dc p) t -> p dc t", p=128)[
                        :, r_ * DCH:(r_ + 1) * DCH, sl * 128:(sl + 1) * 128]
                    kdst = kb3[:].rearrange("p (dc n) -> p dc n", n=512)[:, :, (kt - c * 4) * 128:(kt - c * 4 + 1) * 128]
                    dma(SY, dK, kdst, ksrc)
                w = min(512, nk - c * 512)
                P.wait_ge(dK.h, dK.n)
                P.wait_ge(dX.h, qsem)
                P.wait_ge(sA.h, sA.n)
                P.wait_ge(sV.h, sV.n)
                for dc in range(DCH):
                    r = P.matmul(psBIG[:, c * 512:c * 512 + w],
                                 qT3[:, dc * 128:(dc + 1) * 128],
                                 kb3[:, dc * 512:dc * 512 + w],
                                 start=(dc == 0), stop=(dc == DCH - 1))
                sP.inc(r)
            V.wait_ge(sP.h, sP.n)
            V.tensor_tensor(psBIG[:, nk - 256:nk], psBIG[:, nk - 256:nk],
                            masks[:, j * 256:(j + 1) * 256], op=ALU.add)
            V.drain()
            n = sV.inc(V.reduce_max(negm[:], psBIG[:, 0:nk], axis=AX.X, negate=True))
            A.wait_ge(sV.h, n)
            na = sA.inc(A.activation(Pb3[:, 0:nk], psBIG[:, 0:nk], AF.Exp,
                                     bias=negm[:], accum_out=ssum[:]))
            V.wait_ge(sA.h, na)
            n = sV.inc(V.reciprocal(rsum[:], ssum[:]))
            emit_transposes(Pb3[:], 0, PT3, 0, nkt, False)
            for c in range((nk + 511) // 512):
                tiles = list(range(c * 4, min(c * 4 + 4, nkt)))
                SY.wait_ge(sP.h, sP.n)
                for i, kt in enumerate(tiles):
                    r_ = owner_rank(kt); sl = owner_slot(kt)
                    vsrc = v1g[(r_ * TOK + sl * 128):(r_ * TOK + sl * 128 + 128), :]
                    dma(SY, dK, vb3[:, i * KEY:(i + 1) * KEY], vsrc)
                P.wait_ge(dK.h, dK.n)
                if c == 0:
                    P.wait_ge(sA.h, sA.n)
                    P.wait_ge(sV.h, sV.n)
                for dmc in range(NDM):
                    for i, kt in enumerate(tiles):
                        r = P.matmul(psBIG[:, dmc * 512:(dmc + 1) * 512],
                                     PT3[:, kt * 128:(kt + 1) * 128],
                                     vb3[:, i * KEY + dmc * 512: i * KEY + dmc * 512 + 512],
                                     start=(kt == 0), stop=(kt == nkt - 1))
                sP.inc(r)
            n = sP.n
            V.wait_ge(sP.h, n)
            for dmc in range(NDM):
                sV.inc(V.tensor_scalar(Ot3[:, dmc * 512:(dmc + 1) * 512],
                                       psBIG[:, dmc * 512:(dmc + 1) * 512],
                                       rsum[:], None, op0=ALU.mult))
            done = 0
            while done < DCH:
                nb = min(4, DCH - done)
                P.wait_ge(sV.h, sV.n)
                for c2 in range(nb):
                    r = P.transpose(psA_[:, c2 * 128:(c2 + 1) * 128],
                                    Ot3[:, (done + c2) * 128:(done + c2 + 1) * 128], ident[:])
                n = sP.inc(r)
                V.wait_ge(sP.h, n)
                V.wait_ge(dG.h, dG.n)
                n = sV.inc(V.tensor_copy(oTe3[:, 0:nb * 128], psA_[:, 0:nb * 128]))
                G_.wait_ge(sV.h, n)
                spill_T(oTe3[:, 0:nb * 128], nb, oT_h, done, j * 128)
                done += nb
        barrier()

    def resid_l1(j, oc, psu, npe, rt0, rt1, rt2):
        G_.wait_ge(sV.h, sV.n)
        nl = dma(G_, dG, rt0[:], stream_h[j * 128:(j + 1) * 128, oc * 512:(oc + 1) * 512])
        V.wait_ge(dG.h, nl)
        V.wait_ge(sP.h, npe)
        V.tensor_scalar(rt2[:], rt0[:], mu3[:, j:j + 1], None, op0=ALU.subtract)
        V.drain()
        V.tensor_scalar(rt2[:], rt2[:], rstd3[:, j:j + 1], None, op0=ALU.mult)
        V.drain()
        V.tensor_tensor(rt1[:], rt2[:], psu[:], op=ALU.add)
        V.drain()
        n = sV.inc(V.tensor_tensor(rt1[:], rt1[:], rt0[:], op=ALU.add))
        G_.wait_ge(sV.h, n)
        dma(G_, dG, stream_h[j * 128:(j + 1) * 128, oc * 512:(oc + 1) * 512], rt1[:])

    emit_out_proj(oprw, DCH, resid_l1)

    if cfg.debug:
        G_.wait_ge(dG.h, dG.n)
        dma(G_, dG, dbg["dbg_s3"], stream_h)
        barrier()

    def emit_absmean(w_hbm, nrows, ncols, inv_out, tag):
        with nc.sbuf_tensor("aw0" + tag, [128, ncols], F32) as aw0, \
             nc.sbuf_tensor("aw1" + tag, [128, ncols], F32) as aw1:
            for rc in range(nrows // 128):
                wb = (aw0, aw1)[rc % 2]
                SY.wait_ge(sV.h, sV.n)
                nl = dma(SY, dW, wb[:], w_hbm[rc * 128:(rc + 1) * 128, :])
                V.wait_ge(dW.h, nl)
                V.reduce_sum(stat[:, 9:10], wb[:], axis=AX.X, apply_absolute_value=True)
                V.drain()
                if rc == 0:
                    n = sV.inc(V.tensor_single_scalar(stat[:, 8:9], stat[:, 9:10], 1.0, op=ALU.mult))
                else:
                    n = sV.inc(V.tensor_tensor(stat[:, 8:9], stat[:, 8:9], stat[:, 9:10], op=ALU.add))
            P.wait_ge(sV.h, sV.n)
            n = sP.inc(P.matmul(psO[:, 0:1], onesq[:], stat[:, 8:9], start=True, stop=True))
            V.wait_ge(sP.h, n)
            V.tensor_single_scalar(stat[:, 10:11], psO[:, 0:1], 1.0 / (nrows * ncols), op=ALU.mult)
            V.drain()
            V.tensor_single_scalar(stat[:, 10:11], stat[:, 10:11], 1e-8, op=ALU.max)
            V.drain()
            n = sV.inc(V.reciprocal(inv_out, stat[:, 10:11]))
            barrier()

    emit_absmean(w1n1, DFF, D, inv0[:], "m1")
    emit_absmean(w2n1, D, DFF, inv1[:], "m2")
    emit_quant_spill(w1n1, w1qT1_h, DFF, D, True, inv0[:], "c")
    emit_quant_spill(w2n1, w2qT1_h, D, DFF, True, inv1[:], "d")

    HT = SL // 2 if SL >= 2 else 1
    for half in range((SL + HT - 1) // HT):
        tts = list(range(half * HT, min((half + 1) * HT, SL)))
        with nc.sbuf_tensor("xhiT%d" % half, [128, len(tts) * D], BF16) as xhiT, \
             nc.sbuf_tensor("xloT%d" % half, [128, len(tts) * D], BF16) as xloT:
            with nc.sbuf_tensor("f4a%d" % half, [128, D], F32) as f4a, \
                 nc.sbuf_tensor("f4b%d" % half, [128, D], F32) as f4b, \
                 nc.sbuf_tensor("f4c%d" % half, [128, D], F32) as f4c, \
                 nc.sbuf_tensor("xhib%d" % half, [128, D], BF16) as xhib, \
                 nc.sbuf_tensor("xlob%d" % half, [128, D], BF16) as xlob:
                for i, tt in enumerate(tts):
                    SY.wait_ge(sV.h, sV.n)
                    nl = dma(SY, dX, f4a[:], stream_h[tt * 128:(tt + 1) * 128, :])
                    V.wait_ge(dX.h, nl)
                    emit_ln(f4a[:], D, f4b[:], f4c[:], stat[:, 4:5], stat[:, 5:6])
                    V.tensor_copy(xhib[:], f4c[:])
                    V.drain()
                    V.tensor_tensor(f4b[:], f4c[:], xhib[:], op=ALU.subtract)
                    V.drain()
                    n = sV.inc(V.tensor_copy(xlob[:], f4b[:]))
                    V.wait_ge(sV.h, n)
                    emit_transposes(xhib[:], 0, xhiT, i * DCH, DCH, True)
                    emit_transposes(xlob[:], 0, xloT, i * DCH, DCH, True)
                barrier()

            with nc.sbuf_tensor("o4a%d" % half, [128, D], F32) as o4a:
                def out_l1(ttl, h2acc, _tts=tts):
                    tt = _tts[ttl]
                    vsync()
                    SY.wait_ge(sG.h, gack())
                    nl = dma(SY, dX, o4a[:], stream_h[tt * 128:(tt + 1) * 128, :])
                    V.wait_ge(dX.h, nl)
                    n = sV.inc(V.tensor_tensor(o4a[:], o4a[:], h2acc[:, ttl * D:(ttl + 1) * D], op=ALU.add))
                    G_.wait_ge(sV.h, n)
                    dma(G_, dG, y_own[tt * 128:(tt + 1) * 128, :], o4a[:])

                emit_ffn(w1qT1_h, w2qT1_h, b1h1, b2h1, [xhiT, xloT], True, len(tts), out_l1, min(1024, DFF))

    barrier()
    es.close()
    return nc



def _rope_tables(cfg, positions, scale):
    half = 64
    j64 = (np.arange(half, dtype=np.float32) / np.float32(half)).astype(np.float32)
    inv = (np.float32(1.0) / (np.float32(10000.0) ** j64)).astype(np.float32)
    ang = (positions[:, None].astype(np.float32) * inv[None, :]).astype(np.float32)
    cos = np.cos(ang.astype(np.float64)).astype(np.float32) * np.float32(scale)
    sin = np.sin(ang.astype(np.float64)).astype(np.float32) * np.float32(scale)
    return cos.astype(np.float32), sin.astype(np.float32)


def _host_inputs(cfg, I, rank, seq):
    T, D, DFF, DK0, SL = cfg.T, cfg.D, cfg.DFF, cfg.DK0, cfg.SL
    rows = np.concatenate([np.arange(g_of(j, rank) * 128, g_of(j, rank) * 128 + 128)
                           for j in range(SL)])
    x2 = np.asarray(I["L0_attn_x"]).reshape(cfg.B * T, D)
    m = {}
    m["x_own"] = np.ascontiguousarray(x2[seq * T + rows])
    m["wqT"] = np.ascontiguousarray(np.asarray(I["L0_attn_Wq"]).T)
    m["wkT"] = np.ascontiguousarray(np.asarray(I["L0_attn_Wk"]).T)
    m["wvT"] = np.ascontiguousarray(np.asarray(I["L0_attn_Wv"]).T)
    m["wpT"] = np.ascontiguousarray(np.asarray(I["L0_attn_Wp"]).T)
    m["w1n0"] = np.ascontiguousarray(np.asarray(I["L0_ffn_W1"]))
    m["w2n0"] = np.ascontiguousarray(np.asarray(I["L0_ffn_W2"]))
    m["b1h0"] = np.ascontiguousarray(np.asarray(I["L0_ffn_b1"]).reshape(1, DFF))
    m["b2h0"] = np.ascontiguousarray(np.asarray(I["L0_ffn_b2"]).reshape(1, D))
    m["lnwh"] = np.ascontiguousarray(np.broadcast_to(np.asarray(I["L0_ffn_ln_w"]), (128, D)))
    m["lnbh"] = np.ascontiguousarray(np.broadcast_to(np.asarray(I["L0_ffn_ln_b"]), (128, D)))
    m["gamh"] = np.ascontiguousarray(np.broadcast_to(np.asarray(I["L1_attn_gamma"]), (128, D)))
    m["beth"] = np.ascontiguousarray(np.broadcast_to(np.asarray(I["L1_attn_beta"]), (128, D)))
    m["qkvw"] = np.ascontiguousarray(np.asarray(I["L1_attn_qkv"]))
    m["oprw"] = np.ascontiguousarray(np.asarray(I["L1_attn_o_proj"]))
    m["w1n1"] = np.ascontiguousarray(np.asarray(I["L1_ffn_w1"]))
    m["w2n1"] = np.ascontiguousarray(np.asarray(I["L1_ffn_w2"]))
    m["b1h1"] = np.ascontiguousarray(np.asarray(I["L1_ffn_b1"]).reshape(1, DFF))
    m["b2h1"] = np.ascontiguousarray(np.asarray(I["L1_ffn_b2"]).reshape(1, D))
    qs = 1.0 / np.sqrt(np.float32(cfg.DH))
    cq = np.empty((128, SL * 64), np.float32); sq = np.empty((128, SL * 64), np.float32)
    ck = np.empty((128, SL * 64), np.float32); sk = np.empty((128, SL * 64), np.float32)
    for j in range(SL):
        pos = np.arange(g_of(j, rank) * 128, g_of(j, rank) * 128 + 128)
        c1, s1 = _rope_tables(cfg, pos, qs)
        c2, s2 = _rope_tables(cfg, pos, 1.0)
        cq[:, j * 64:(j + 1) * 64] = c1; sq[:, j * 64:(j + 1) * 64] = s1
        ck[:, j * 64:(j + 1) * 64] = c2; sk[:, j * 64:(j + 1) * 64] = s2
    m["cosqh"] = cq; m["sinqh"] = sq; m["coskh"] = ck; m["sinkh"] = sk
    tri = np.triu(np.full((128, 128), NEG_BIG, np.float32), 1)
    full = np.full((128, 128), NEG_BIG, np.float32)
    zero = np.zeros((128, 128), np.float32)
    mk = np.empty((128, SL * 256), np.float32)
    for j in range(SL):
        g = g_of(j, rank)
        if g == 2 * j:
            mk[:, j * 256: j * 256 + 128] = tri
            mk[:, j * 256 + 128: j * 256 + 256] = full
        else:
            mk[:, j * 256: j * 256 + 128] = zero
            mk[:, j * 256 + 128: j * 256 + 256] = tri
    m["masksh"] = mk
    m["identh"] = np.eye(128, dtype=np.float32)
    return m, rows


def kernel(**inputs):
    cfg = Cfg()
    nc = build(cfg)
    in_maps = []
    row_list = []
    for core in range(8):
        seq, rank = core // 2, core % 2
        m, rows = _host_inputs(cfg, inputs, rank, seq)
        in_maps.append(m)
        row_list.append((seq, rows))
    res = run_bass_kernel_spmd(nc, in_maps, core_ids=list(range(8)))
    out = np.empty((cfg.B, cfg.T, cfg.D), np.float32)
    for core in range(8):
        seq, rows = row_list[core]
        out.reshape(cfg.B * cfg.T, cfg.D)[seq * cfg.T + rows] = res.results[core]["y_own"]
    return out


# version 02
# speedup vs baseline: 1.1079x; 1.1079x over previous
import numpy as np
import ml_dtypes
from contextlib import ExitStack

import concourse.bass as bass
import concourse.mybir as mybir
from concourse.bass_utils import run_bass_kernel_spmd

F32 = mybir.dt.float32
I32 = mybir.dt.int32
BF16 = mybir.dt.bfloat16
AF = mybir.ActivationFunctionType
ALU = mybir.AluOpType
AX = mybir.AxisListType

MAGIC = float(np.float32(12582912.0))
NEG_BIG = float(np.finfo(np.float32).min)
AMAX_FLOOR = float(np.float32(2.0 ** -100))
EPS_LN = 1e-5


class Cfg:
    def __init__(self, T=2048, D=2048, DFF=8192, DK0=512, debug=False):
        self.T, self.D, self.DFF, self.DK0 = T, D, DFF, DK0
        self.B = 4
        self.DH = 128
        self.H0 = D // self.DH
        self.HK0 = DK0 // self.DH
        self.KEY = D
        self.G = T // 128
        self.SL = self.G // 2
        self.TOK = self.SL * 128
        self.DCH = D // 128
        self.debug = debug


def g_of(j, r):
    return 2 * j + ((j + r) % 2)


def owner_rank(kt):
    return (kt // 2 + kt % 2) % 2


def owner_slot(kt):
    return kt // 2


def build(cfg):
    nc = bass.Bass("TRN2", target_bir_lowering=False)
    T, D, DFF, DK0 = cfg.T, cfg.D, cfg.DFF, cfg.DK0
    SL, TOK, DCH = cfg.SL, cfg.TOK, cfg.DCH
    H0, HK0, KEY = cfg.H0, cfg.HK0, cfg.KEY
    GRP = H0 // HK0
    FSUP = min(1024, DFF)
    NDM = D // 512
    SC1 = float(np.float32(1.0 / np.sqrt(np.float32(KEY))))

    def din(name, shape, dt=F32):
        return nc.dram_tensor(name, shape, dt, kind="ExternalInput").ap()

    x_own = din("x_own", [TOK, D])
    wqT = din("wqT", [D, D]); wkT = din("wkT", [D, DK0]); wvT = din("wvT", [D, DK0]); wpT = din("wpT", [D, D])
    w1n0 = din("w1n0", [DFF, D]); w2n0 = din("w2n0", [D, DFF])
    b1h0 = din("b1h0", [1, DFF]); b2h0 = din("b2h0", [1, D])
    lnwh = din("lnwh", [128, D]); lnbh = din("lnbh", [128, D])
    gamh = din("gamh", [128, D]); beth = din("beth", [128, D])
    qkvw = din("qkvw", [D, 3 * KEY]); oprw = din("oprw", [KEY, D])
    w1n1 = din("w1n1", [DFF, D]); w2n1 = din("w2n1", [D, DFF])
    b1h1 = din("b1h1", [1, DFF]); b2h1 = din("b2h1", [1, D])
    cosqh = din("cosqh", [128, SL * 64]); sinqh = din("sinqh", [128, SL * 64])
    coskh = din("coskh", [128, SL * 64]); sinkh = din("sinkh", [128, SL * 64])
    masksh = din("masksh", [128, SL * 256])
    identh = din("identh", [128, 128])
    y_own = nc.dram_tensor("y_own", [TOK, D], F32, kind="ExternalOutput").ap()
    dbg = {}
    if cfg.debug:
        for nm in ("dbg_s1", "dbg_s2", "dbg_s3"):
            dbg[nm] = nc.dram_tensor(nm, [TOK, D], F32, kind="ExternalOutput").ap()

    def dint(name, shape, dt=F32):
        return nc.dram_tensor(name, shape, dt).ap()

    stream_h = dint("stream_h", [TOK, D])
    xb_h = dint("xb_h", [TOK, D], BF16)
    qT_h = dint("qT_h", [max(D, KEY), TOK])
    oT_h = dint("oT_h", [D, TOK])
    kT0o = dint("kT0o", [DK0, TOK]); kT0g = dint("kT0g", [2 * DK0, TOK])
    v0o = dint("v0o", [TOK, DK0]); v0g = dint("v0g", [2 * TOK, DK0])
    kT1o = dint("kT1o", [KEY, TOK]); kT1g = dint("kT1g", [2 * KEY, TOK])
    v1o = dint("v1o", [TOK, KEY]); v1g = dint("v1g", [2 * TOK, KEY])
    w1qT0_h = dint("w1qT0_h", [D, DFF], BF16)
    w2qT0_h = dint("w2qT0_h", [DFF, D], BF16)
    w1qT1_h = dint("w1qT1_h", [D, DFF], BF16)
    w2qT1_h = dint("w2qT1_h", [DFF, D], BF16)

    es = ExitStack()
    sbuf = lambda name, shape, dt=F32: es.enter_context(nc.sbuf_tensor(name, shape, dt))

    class Sem:
        __slots__ = ("h", "n")
        def __init__(self, name):
            self.h = es.enter_context(nc.semaphore(name))
            self.n = 0
        def inc(self, instr, k=1):
            instr.then_inc(self.h, k)
            self.n += k
            return self.n

    dW = Sem("dW"); dX = Sem("dX"); dK = Sem("dK"); dG = Sem("dG")
    dwA = Sem("dwA"); dwB = Sem("dwB"); dwF = Sem("dwF")
    sV = Sem("sV"); sA = Sem("sA"); sP = Sem("sP"); sCC = Sem("sCC")
    sB = Sem("sB"); sG = Sem("sG")
    CSEMS = (sV, sA, sP, sCC)

    V, A, P, G_, SY = nc.vector, nc.scalar, nc.tensor, nc.gpsimd, nc.sync

    ident = sbuf("ident", [128, 128])
    identb = sbuf("identb", [128, 128], BF16)
    ones1 = sbuf("ones1", [1, 128])
    onesq = sbuf("onesq", [128, 128])
    masks = sbuf("masks", [128, SL * 256])
    stat = sbuf("stat", [128, 16])
    mu3 = sbuf("mu3", [128, SL]); rstd3 = sbuf("rstd3", [128, SL])
    inv0 = sbuf("inv0", [128, 1]); inv1 = sbuf("inv1", [128, 1])
    negm = sbuf("negm", [128, 1]); ssum = sbuf("ssum", [128, 1]); rsum = sbuf("rsum", [128, 1])
    bb1 = sbuf("bb1", [1, FSUP])
    bb2 = sbuf("bb2", [1, D])

    psBIG = es.enter_context(nc.psum_tensor("psBIG", [128, 2048], F32))
    psH0 = es.enter_context(nc.psum_tensor("psH0", [128, 512], F32))
    psH1 = es.enter_context(nc.psum_tensor("psH1", [128, 512], F32))
    psA_ = es.enter_context(nc.psum_tensor("psA_", [128, 512], F32))
    psO = es.enter_context(nc.psum_tensor("psO", [128, 512], F32))

    def barrier():
        SY.wait_ge(dW.h, dW.n); SY.wait_ge(dX.h, dX.n); SY.wait_ge(dK.h, dK.n)
        SY.wait_ge(dwA.h, dwA.n); SY.wait_ge(dwB.h, dwB.n); SY.wait_ge(dwF.h, dwF.n)
        SY.sem_inc(sB.h, 1); sB.n += 1
        G_.wait_ge(dG.h, dG.n)
        G_.sem_inc(sB.h, 1); sB.n += 1
        for eng in (V, A, P, G_, SY):
            eng.wait_ge(sB.h, sB.n)
            for s in CSEMS:
                eng.wait_ge(s.h, s.n)

    pend = {}

    def note(name, sem, val):
        d = pend.setdefault(name, {})
        d[id(sem)] = (sem, max(val, d.get(id(sem), (None, -1))[1]))

    def guard(eng, name):
        for sem, v in pend.pop(name, {}).values():
            eng.wait_ge(sem.h, v)

    def vsync():
        V.wait_ge(sV.h, sV.n)

    def vack():
        return sV.inc(V.tensor_copy(stat[:, 15:16], stat[:, 14:15]))

    def gack():
        G_.wait_ge(dG.h, dG.n)
        G_.sem_inc(sG.h, 1); sG.n += 1
        return sG.n

    def dma(q, qsem, out, in_):
        q.dma_start(out=out, in_=in_).then_inc(qsem.h, 16)
        qsem.n += 16
        return qsem.n

    def emit_ln(xt, ncols, dump, xn_out, mu_ap, rstd_ap):
        V.reduce_sum(stat[:, 0:1], xt, axis=AX.X)
        V.drain()
        V.tensor_single_scalar(mu_ap, stat[:, 0:1], 1.0 / ncols, op=ALU.mult)
        V.drain()
        n = sV.inc(V.tensor_scalar(xn_out, xt, mu_ap, None, op0=ALU.subtract))
        A.wait_ge(sV.h, n)
        n = sA.inc(A.activation(dump, xn_out, AF.Square, accum_out=stat[:, 1:2]))
        V.wait_ge(sA.h, n)
        n = sV.inc(V.tensor_scalar(stat[:, 2:3], stat[:, 1:2], 1.0 / ncols, EPS_LN,
                                   op0=ALU.mult, op1=ALU.add))
        A.wait_ge(sV.h, n)
        n = sA.inc(A.activation(stat[:, 3:4], stat[:, 2:3], AF.Sqrt))
        V.wait_ge(sA.h, n)
        V.reciprocal(rstd_ap, stat[:, 3:4])
        V.drain()
        n = sV.inc(V.tensor_scalar(xn_out, xn_out, rstd_ap, None, op0=ALU.mult))
        V.wait_ge(sV.h, n)
        return n

    def emit_transposes(src, soff, dst, doff, k, bf16, nsrc=None):
        idt = identb if bf16 else ident
        pa = psA_[:].bitcast(BF16) if bf16 else psA_[:]
        if nsrc is not None:
            P.wait_ge(nsrc[0].h, nsrc[1])
        done = 0
        while done < k:
            nb = min(4, k - done)
            guard(P, "psA")
            for c in range(nb):
                n = sP.inc(P.transpose(pa[:, c * 128:(c + 1) * 128],
                                       src[:, (soff + done + c) * 128:(soff + done + c + 1) * 128], idt[:]))
            V.wait_ge(sP.h, n)
            nv = sV.inc(V.tensor_copy(dst[:, (doff + done) * 128:(doff + done + nb) * 128],
                                      pa[:, 0:nb * 128]))
            note("psA", sV, nv)
            done += nb

    def spill_T(src, k, dsth, row0_chunk, col0):
        dst = dsth.rearrange("(c p) t -> p c t", p=128)[:, row0_chunk:row0_chunk + k, col0:col0 + 128]
        return dma(G_, dG, dst, src.rearrange("p (c t) -> p c t", t=128))

    def emit_mx_quant(xin, xout_bf, ncols, amx, expb, scl, invs, vtmp):
        nb = ncols // 32
        x3 = xin.rearrange("p (b k) -> p b k", k=32)
        V.tensor_reduce(amx[:, 0:nb], x3, axis=AX.X, op=ALU.max, apply_absolute_value=True)
        V.drain()
        V.tensor_single_scalar(amx[:, 0:nb], amx[:, 0:nb], AMAX_FLOOR, op=ALU.max)
        V.drain()
        V.tensor_single_scalar(expb[:, 0:nb], amx[:, 0:nb].bitcast(I32), 0x7F800000, op=ALU.bitwise_and)
        V.drain()
        V.tensor_single_scalar(scl[:, 0:nb], expb[:, 0:nb].bitcast(F32),
                               float(np.float32(2.0 ** -6)), op=ALU.mult)
        V.drain()
        V.reciprocal(invs[:, 0:nb], scl[:, 0:nb])
        V.drain()
        inv3 = invs[:, 0:nb].unsqueeze(2).broadcast_to([128, nb, 32])
        v3 = vtmp.rearrange("p (b k) -> p b k", k=32)
        V.tensor_tensor(v3, x3, inv3, op=ALU.mult)
        V.drain()
        V.tensor_scalar(vtmp, vtmp, MAGIC, float(np.float32(MAGIC + 127.0)), op0=ALU.add, op1=ALU.min)
        V.drain()
        V.tensor_scalar(vtmp, vtmp, float(np.float32(MAGIC - 128.0)), -MAGIC, op0=ALU.max, op1=ALU.add)
        V.drain()
        sc3 = scl[:, 0:nb].unsqueeze(2).broadcast_to([128, nb, 32])
        o3 = xout_bf.rearrange("p (b k) -> p b k", k=32)
        n = sV.inc(V.tensor_tensor(o3, v3, sc3, op=ALU.mult))
        V.wait_ge(sV.h, n)
        return n

    dma(SY, dX, ident[:], identh)
    dma(SY, dX, masks[:], masksh)
    G_.memset(ones1[:], 1.0)
    V.memset(stat[:], 0.0)
    V.drain()
    n = sV.inc(G_.memset(onesq[:], 1.0))
    dma(G_, dG, stream_h, x_own)
    V.wait_ge(dX.h, dX.n)
    sV.inc(V.tensor_copy(identb[:], ident[:]))
    barrier()

    with nc.sbuf_tensor("xnT", [128, SL * D], F32) as xnT, \
         nc.sbuf_tensor("ta", [128, D], F32) as ta, \
         nc.sbuf_tensor("tb", [128, D], F32) as tb, \
         nc.sbuf_tensor("tc", [128, D], F32) as tc, \
         nc.sbuf_tensor("wb0", [128, DCH * 512], F32) as wb0, \
         nc.sbuf_tensor("wb1", [128, DCH * 512], F32) as wb1, \
         nc.sbuf_tensor("ev0", [128, 512], F32) as ev0, \
         nc.sbuf_tensor("ev1", [128, 512], F32) as ev1, \
         nc.sbuf_tensor("cq", [128, SL * 64], F32) as cq, \
         nc.sbuf_tensor("sq", [128, SL * 64], F32) as sq, \
         nc.sbuf_tensor("ck", [128, SL * 64], F32) as ck, \
         nc.sbuf_tensor("sk", [128, SL * 64], F32) as sk:
        dma(SY, dX, cq[:], cosqh); dma(SY, dX, sq[:], sinqh)
        dma(SY, dX, ck[:], coskh); dma(SY, dX, sk[:], sinkh)
        V.wait_ge(dX.h, dX.n)
        vack()
        for j in range(SL):
            SY.wait_ge(sV.h, sV.n)
            nld = dma(SY, dX, ta[:], stream_h[j * 128:(j + 1) * 128, :])
            V.wait_ge(dX.h, nld)
            nln = emit_ln(ta[:], D, tb[:], tc[:], stat[:, 4:5], stat[:, 5:6])
            emit_transposes(tc[:], 0, xnT, j * DCH, DCH, False, nsrc=(sV, nln))

        def rope_spill(kind, oc, j):
            ct = (cq if kind == 'q' else ck)[:, j * 64:(j + 1) * 64]
            st_ = (sq if kind == 'q' else sk)[:, j * 64:(j + 1) * 64]
            cb = ct.unsqueeze(1).broadcast_to([128, 4, 64])
            sb_ = st_.unsqueeze(1).broadcast_to([128, 4, 64])
            e3 = ev0[:, 0:512].rearrange("p (h c) -> p h c", c=128)
            o3 = ev1[:, 0:512].rearrange("p (h c) -> p h c", c=128)
            x1 = e3[:, :, 0:64]; x2 = e3[:, :, 64:128]
            t1 = ta[:, 0:256].rearrange("p (h c) -> p h c", c=64)
            t2 = ta[:, 256:512].rearrange("p (h c) -> p h c", c=64)
            guard(V, "ev1")
            V.tensor_tensor(t1, x1, cb, op=ALU.mult)
            V.tensor_tensor(t2, x2, sb_, op=ALU.mult)
            V.drain()
            V.tensor_tensor(o3[:, :, 0:64], t1, t2, op=ALU.subtract)
            V.drain()
            V.tensor_tensor(t1, x2, cb, op=ALU.mult)
            V.tensor_tensor(t2, x1, sb_, op=ALU.mult)
            V.drain()
            n = sV.inc(V.tensor_tensor(o3[:, :, 64:128], t1, t2, op=ALU.add))
            P.wait_ge(sV.h, n)
            guard(P, "psA")
            for c in range(4):
                r = sP.inc(P.transpose(psA_[:, c * 128:(c + 1) * 128],
                                       ev1[:, c * 128:(c + 1) * 128], ident[:]))
            n = r
            note("ev1", sP, n)
            V.wait_ge(sP.h, n)
            guard(V, "ev0")
            n = sV.inc(V.tensor_copy(ev0[:, 0:512], psA_[:]))
            note("psA", sV, n)
            G_.wait_ge(sV.h, n)
            nst = spill_T(ev0[:, 0:512], 4, qT_h if kind == 'q' else kT0o, oc * 4, j * 128)
            note("ev0", dG, nst)

        wl = 0
        for wT_hbm, nout, kind in ((wqT, D, 'q'), (wkT, DK0, 'k'), (wvT, DK0, 'v')):
            for oc in range(nout // 512):
                wb, wbn = ((wb0, "wb0"), (wb1, "wb1"))[wl % 2]
                dwp = (dwA, dwB)[wl % 2]; wl += 1
                guard(SY, wbn)
                src = wT_hbm.rearrange("(dc p) n -> p dc n", p=128)[:, :, oc * 512:(oc + 1) * 512]
                wsem = dma(SY, dwp, wb[:].rearrange("p (dc n) -> p dc n", n=512), src)
                for j in range(SL):
                    psu, psn = ((psH0, "psH0"), (psH1, "psH1"))[j % 2]
                    P.wait_ge(dwp.h, wsem)
                    guard(P, psn)
                    for dc in range(DCH):
                        r = P.matmul(psu[:], xnT[:, (j * DCH + dc) * 128:(j * DCH + dc + 1) * 128],
                                     wb[:, dc * 512:(dc + 1) * 512],
                                     start=(dc == 0), stop=(dc == DCH - 1))
                    n = sP.inc(r)
                    note(wbn, sP, n)
                    V.wait_ge(sP.h, n)
                    guard(V, "ev0")
                    n = sV.inc(V.tensor_copy(ev0[:, 0:512], psu[:]))
                    note(psn, sV, n)
                    if kind == 'v':
                        G_.wait_ge(sV.h, n)
                        nst = dma(G_, dG, v0o[j * 128:(j + 1) * 128, oc * 512:(oc + 1) * 512], ev0[:, 0:512])
                        note("ev0", dG, nst)
                    else:
                        V.wait_ge(sV.h, n)
                        rope_spill(kind, oc, j)
        barrier()

        sCC.inc(G_.collective_compute(
            "AllGather", ALU.bypass, replica_groups=[[0, 1], [2, 3], [4, 5], [6, 7]],
            ins=[kT0o.tensor.ap().opt()], outs=[kT0g.tensor.ap().opt()]))
        sCC.inc(G_.collective_compute(
            "AllGather", ALU.bypass, replica_groups=[[0, 1], [2, 3], [4, 5], [6, 7]],
            ins=[v0o.tensor.ap().opt()], outs=[v0g.tensor.ap().opt()]))
        G_.wait_ge(sCC.h, sCC.n)
        barrier()

    with nc.sbuf_tensor("kbuf", [128, HK0 * 2048], F32) as kbuf, \
         nc.sbuf_tensor("vbuf", [128, 16 * DK0], F32) as vbuf, \
         nc.sbuf_tensor("qTs", [128, D], F32) as qTs, \
         nc.sbuf_tensor("Pb", [128, 2048], F32) as Pb, \
         nc.sbuf_tensor("PTb", [128, 2048], F32) as PTb, \
         nc.sbuf_tensor("Ot", [128, D], F32) as Ot, \
         nc.sbuf_tensor("oTe", [128, 512], F32) as oTe:
        for j in range(SL):
            nkt = 2 * j + 2
            nk = nkt * 128
            SY.wait_ge(sP.h, sP.n)
            qsrc = qT_h.rearrange("(dc p) t -> p dc t", p=128)[:, 0:DCH, j * 128:(j + 1) * 128]
            qsem = dma(SY, dX, qTs[:].rearrange("p (dc t) -> p dc t", t=128), qsrc)
            for kt in range(nkt):
                r_ = owner_rank(kt); sl = owner_slot(kt)
                ksrc = kT0g.rearrange("(c p) t -> p c t", p=128)[
                    :, r_ * HK0:(r_ + 1) * HK0, sl * 128:(sl + 1) * 128]
                kdst = kbuf[:].rearrange("p (c n) -> p c n", n=2048)[:, :, kt * 128:(kt + 1) * 128]
                dma(SY, dK, kdst, ksrc)
                vsrc = v0g[(r_ * TOK + sl * 128):(r_ * TOK + sl * 128 + 128), :]
                dma(SY, dK, vbuf[:, kt * DK0:(kt + 1) * DK0], vsrc)
            ldsem = dK.n
            for h in range(H0):
                kvh = h // GRP
                P.wait_ge(dK.h, ldsem)
                P.wait_ge(dX.h, qsem)
                P.wait_ge(sA.h, sA.n)
                P.wait_ge(sV.h, sV.n)
                for c in range((nk + 511) // 512):
                    w = min(512, nk - c * 512)
                    r = P.matmul(psBIG[:, c * 512:c * 512 + w],
                                 qTs[:, h * 128:(h + 1) * 128],
                                 kbuf[:, kvh * 2048 + c * 512: kvh * 2048 + c * 512 + w],
                                 start=True, stop=True)
                npe = sP.inc(r)
                V.wait_ge(sP.h, npe)
                V.tensor_tensor(psBIG[:, nk - 256:nk], psBIG[:, nk - 256:nk],
                                masks[:, j * 256:(j + 1) * 256], op=ALU.add)
                V.drain()
                n = sV.inc(V.reduce_max(negm[:], psBIG[:, 0:nk], axis=AX.X, negate=True))
                A.wait_ge(sV.h, n)
                na = sA.inc(A.activation(Pb[:, 0:nk], psBIG[:, 0:nk], AF.Exp,
                                         bias=negm[:], accum_out=ssum[:]))
                V.wait_ge(sA.h, na)
                n = sV.inc(V.reciprocal(rsum[:], ssum[:]))
                emit_transposes(Pb[:], 0, PTb, 0, nkt, False, nsrc=(sA, na))
                P.wait_ge(sV.h, sV.n)
                for kt in range(nkt):
                    r = P.matmul(psO[:, 0:128], PTb[:, kt * 128:(kt + 1) * 128],
                                 vbuf[:, kt * DK0 + kvh * 128: kt * DK0 + (kvh + 1) * 128],
                                 start=(kt == 0), stop=(kt == nkt - 1))
                n = sP.inc(r)
                V.wait_ge(sP.h, n)
                sV.inc(V.tensor_scalar(Ot[:, h * 128:(h + 1) * 128], psO[:, 0:128],
                                       rsum[:], None, op0=ALU.mult))
            done = 0
            while done < DCH:
                nb = min(4, DCH - done)
                P.wait_ge(sV.h, sV.n)
                for c in range(nb):
                    r = P.transpose(psA_[:, c * 128:(c + 1) * 128],
                                    Ot[:, (done + c) * 128:(done + c + 1) * 128], ident[:])
                n = sP.inc(r)
                V.wait_ge(sP.h, n)
                V.wait_ge(dG.h, dG.n)
                n = sV.inc(V.tensor_copy(oTe[:, 0:nb * 128], psA_[:, 0:nb * 128]))
                G_.wait_ge(sV.h, n)
                spill_T(oTe[:, 0:nb * 128], nb, oT_h, done, j * 128)
                done += nb
        barrier()

    _opctr = [0]

    def emit_out_proj(wT_hbm, nin_ch, resid_fn):
        _opctr[0] += 1
        tg = str(_opctr[0])
        with nc.sbuf_tensor("pwb0" + tg, [128, nin_ch * 512], F32) as pwb0, \
             nc.sbuf_tensor("pwb1" + tg, [128, nin_ch * 512], F32) as pwb1, \
             nc.sbuf_tensor("poT" + tg, [128, nin_ch * 128], F32) as poT, \
             nc.sbuf_tensor("prt0" + tg, [128, 512], F32) as prt0, \
             nc.sbuf_tensor("prt1" + tg, [128, 512], F32) as prt1, \
             nc.sbuf_tensor("prt2" + tg, [128, 512], F32) as prt2:
            for oc in range(D // 512):
                wb, wbn = ((pwb0, "pwb0"), (pwb1, "pwb1"))[oc % 2]
                dwp = (dwA, dwB)[oc % 2]
                guard(SY, wbn)
                src = wT_hbm.rearrange("(dc p) n -> p dc n", p=128)[:, :, oc * 512:(oc + 1) * 512]
                wsem = dma(SY, dwp, wb[:].rearrange("p (dc n) -> p dc n", n=512), src)
                for j in range(SL):
                    guard(SY, "poT")
                    xsrc = oT_h.rearrange("(dc p) t -> p dc t", p=128)[:, 0:nin_ch, j * 128:(j + 1) * 128]
                    nx = dma(SY, dX, poT[:].rearrange("p (dc t) -> p dc t", t=128), xsrc)
                    psu, psn = ((psH0, "psH0"), (psH1, "psH1"))[j % 2]
                    P.wait_ge(dwp.h, wsem)
                    P.wait_ge(dX.h, nx)
                    guard(P, psn)
                    for dc in range(nin_ch):
                        r = P.matmul(psu[:], poT[:, dc * 128:(dc + 1) * 128],
                                     wb[:, dc * 512:(dc + 1) * 512],
                                     start=(dc == 0), stop=(dc == nin_ch - 1))
                    npe = sP.inc(r)
                    note(wbn, sP, npe)
                    note("poT", sP, npe)
                    resid_fn(j, oc, psu, npe, prt0, prt1, prt2)
            barrier()

    def resid_l0(j, oc, psu, npe, rt0, rt1, rt2):
        G_.wait_ge(sV.h, sV.n)
        nl = dma(G_, dG, rt0[:], stream_h[j * 128:(j + 1) * 128, oc * 512:(oc + 1) * 512])
        V.wait_ge(dG.h, nl)
        V.wait_ge(sP.h, npe)
        n = sV.inc(V.tensor_tensor(rt1[:], rt0[:], psu[:], op=ALU.add))
        note("psH0" if psu is psH0 else "psH1", sV, n)
        G_.wait_ge(sV.h, n)
        dma(G_, dG, stream_h[j * 128:(j + 1) * 128, oc * 512:(oc + 1) * 512], rt1[:])

    emit_out_proj(wpT, DCH, resid_l0)

    if cfg.debug:
        G_.wait_ge(dG.h, dG.n)
        dma(G_, dG, dbg["dbg_s1"], stream_h)
        barrier()

    def emit_quant_spill(w_hbm, wqT_hbm, nrows, ncols, ternary, inv_ap, tag):
        with nc.sbuf_tensor("qw0" + tag, [128, ncols], F32) as qw0, \
             nc.sbuf_tensor("qw1" + tag, [128, ncols], F32) as qw1, \
             nc.sbuf_tensor("qv" + tag, [128, ncols], F32) as qv, \
             nc.sbuf_tensor("qb" + tag, [128, ncols], BF16) as qb, \
             nc.sbuf_tensor("qe" + tag, [128, 512], BF16) as qe, \
             nc.sbuf_tensor("qam" + tag, [128, ncols // 32], F32) as qam, \
             nc.sbuf_tensor("qex" + tag, [128, ncols // 32], I32) as qex, \
             nc.sbuf_tensor("qsc" + tag, [128, ncols // 32], F32) as qsc, \
             nc.sbuf_tensor("qiv" + tag, [128, ncols // 32], F32) as qiv:
            for rc in range(nrows // 128):
                wbq, wbqn = ((qw0, "qw0"), (qw1, "qw1"))[rc % 2]
                dwp = (dwA, dwB)[rc % 2]
                guard(SY, wbqn)
                nl = dma(SY, dwp, wbq[:], w_hbm[rc * 128:(rc + 1) * 128, :])
                V.wait_ge(dwp.h, nl)
                guard(V, "qb")
                if ternary:
                    V.tensor_scalar(qv[:], wbq[:], inv_ap, None, op0=ALU.mult)
                    V.drain()
                    V.tensor_scalar(qv[:], qv[:], MAGIC, float(np.float32(MAGIC + 1.0)),
                                    op0=ALU.add, op1=ALU.min)
                    V.drain()
                    nq = sV.inc(V.tensor_scalar(qb[:], qv[:], float(np.float32(MAGIC - 1.0)), -MAGIC,
                                                 op0=ALU.max, op1=ALU.add))
                    V.wait_ge(sV.h, nq)
                else:
                    emit_mx_quant(wbq[:], qb[:], ncols, qam, qex, qsc, qiv, qv[:])
                note(wbqn, sV, sV.n)
                P.wait_ge(sV.h, sV.n)
                done = 0
                while done < ncols // 128:
                    nb_ = min(4, ncols // 128 - done)
                    guard(P, "psA")
                    pa = psA_[:].bitcast(BF16)
                    for c in range(nb_):
                        r = P.transpose(pa[:, c * 128:(c + 1) * 128],
                                        qb[:, (done + c) * 128:(done + c + 1) * 128], identb[:])
                    n = sP.inc(r)
                    note("qb", sP, n)
                    V.wait_ge(sP.h, n)
                    guard(V, "qe")
                    n = sV.inc(V.tensor_copy(qe[:, 0:nb_ * 128], pa[:, 0:nb_ * 128]))
                    note("psA", sV, n)
                    G_.wait_ge(sV.h, n)
                    dst = wqT_hbm.rearrange("(fc p) d -> p fc d", p=128)[:, done:done + nb_, rc * 128:(rc + 1) * 128]
                    nst = dma(G_, dG, dst, qe[:, 0:nb_ * 128].rearrange("p (c t) -> p c t", t=128))
                    note("qe", dG, nst)
                    done += nb_
            barrier()

    _ffnctr = [0]

    def emit_ffn(w1qT_hbm, w2qT_hbm, b1_hbm, b2_hbm, xT_list, l1, ntt, out_fn, fsup):
        nfs = DFF // fsup
        nfg = fsup // 512
        xis = (0, 1) if l1 else (0,)
        _ffnctr[0] += 1
        tg = str(_ffnctr[0])
        with nc.sbuf_tensor("fh2" + tg, [128, ntt * D], F32) as h2acc, \
             nc.sbuf_tensor("fw1" + tg, [128, DCH * fsup], BF16) as w1b, \
             nc.sbuf_tensor("fw2" + tg, [128, (fsup // 128) * D], BF16) as w2b, \
             nc.sbuf_tensor("fg0" + tg, [128, 512], BF16) as fg0, \
             nc.sbuf_tensor("fg1" + tg, [128, 512], BF16) as fg1, \
             nc.sbuf_tensor("fgf" + tg, [128, 512], F32) as fgf, \
             nc.sbuf_tensor("fgh" + tg, [128, 512], BF16) as fgh, \
             nc.sbuf_tensor("fgl" + tg, [128, 512], BF16) as fgl, \
             nc.sbuf_tensor("fq2" + tg, [128, 512], BF16) as fq2, \
             nc.sbuf_tensor("fqT" + tg, [128, 8 * 128], BF16) as fqT, \
             nc.sbuf_tensor("fam" + tg, [128, 16], F32) as fam, \
             nc.sbuf_tensor("fex" + tg, [128, 16], I32) as fex, \
             nc.sbuf_tensor("fsc" + tg, [128, 16], F32) as fsc, \
             nc.sbuf_tensor("fiv" + tg, [128, 16], F32) as fiv, \
             nc.sbuf_tensor("fvt" + tg, [128, 512], F32) as fvt:
            nbb2 = dma(SY, dX, bb2[0:1, :], b2_hbm[0:1, :])
            for fs in range(nfs):
                SY.wait_ge(sP.h, sP.n)
                w1src = w1qT_hbm.rearrange("(dc p) f -> p dc f", p=128)[:, :, fs * fsup:(fs + 1) * fsup]
                dma(SY, dwF, w1b[:].rearrange("p (dc f) -> p dc f", f=fsup), w1src)
                w2src = w2qT_hbm.rearrange("(fc p) d -> p fc d", p=128)[
                    :, fs * (fsup // 128):(fs + 1) * (fsup // 128), :]
                dma(SY, dwF, w2b[:].rearrange("p (fc d) -> p fc d", d=D), w2src)
                dma(SY, dwF, bb1[0:1, 0:fsup], b1_hbm[0:1, fs * fsup:(fs + 1) * fsup])
                wsem = dwF.n
                for tt in range(ntt):
                    P.wait_ge(dwF.h, wsem)
                    P.wait_ge(dX.h, nbb2)
                    if fs == 0:
                        guard(P, "psBIG")
                        for dmc in range(NDM):
                            P.matmul(psBIG[:, dmc * 512:(dmc + 1) * 512], ones1[:],
                                     bb2[0:1, dmc * 512:(dmc + 1) * 512], start=True, stop=False)
                    def emit_h(fg):
                        psu, psn = ((psH0, "psH0"), (psH1, "psH1"))[fg % 2]
                        guard(P, psn)
                        P.matmul(psu[:], ones1[:], bb1[0:1, fg * 512:(fg + 1) * 512],
                                 start=True, stop=False)
                        for dc in range(DCH):
                            for xi in xis:
                                r = P.matmul(
                                    psu[:],
                                    xT_list[xi][:, (tt * DCH + dc) * 128:(tt * DCH + dc + 1) * 128],
                                    w1b[:, dc * fsup + fg * 512: dc * fsup + fg * 512 + 512],
                                    start=False, stop=(dc == DCH - 1 and xi == xis[-1]))
                        return sP.inc(r)

                    def emit_gchain(fg, npe):
                        psu, psn = ((psH0, "psH0"), (psH1, "psH1"))[fg % 2]
                        V.wait_ge(sP.h, npe)
                        if not l1:
                            guard(V, "fg0")
                            n = sV.inc(V.tensor_copy(fg0[:], psu[:]))
                            note(psn, sV, n)
                            A.wait_ge(sV.h, n)
                            na = sA.inc(A.activation(fg1[:], fg0[:], AF.Gelu_apprx_tanh))
                            note("fg0", sA, na)
                            V.wait_ge(sA.h, na)
                            V.tensor_copy(fgf[:], fg1[:])
                            V.drain()
                            guard(V, "fq2")
                            nq2 = emit_mx_quant(fgf[:], fq2[:], 512, fam, fex, fsc, fiv, fvt[:])
                            guard(V, "fqT")
                            emit_transposes(fq2[:], 0, fqT, 0, 4, True, nsrc=(sV, nq2))
                            note("fq2", sP, sP.n)
                        else:
                            guard(V, "fgf")
                            n = sV.inc(V.tensor_copy(fgf[:], psu[:]))
                            note(psn, sV, n)
                            A.wait_ge(sV.h, n)
                            na = sA.inc(A.activation(fgf[:], fgf[:], AF.Gelu))
                            V.wait_ge(sA.h, na)
                            sV.inc(V.tensor_copy(fgh[:], fgf[:]))
                            V.drain()
                            V.tensor_tensor(fvt[:], fgf[:], fgh[:], op=ALU.subtract)
                            V.drain()
                            n = sV.inc(V.tensor_copy(fgl[:], fvt[:]))
                            note("fgf", sV, n)
                            guard(V, "fqT")
                            emit_transposes(fgh[:], 0, fqT, 0, 4, True, nsrc=(sV, n))
                            emit_transposes(fgl[:], 0, fqT, 4, 4, True)
                        return sV.n

                    def emit_w2(fg, nq):
                        P.wait_ge(sV.h, nq)
                        for dmc in range(NDM):
                            for sub in range(4):
                                fc = fg * 4 + sub
                                for xi in xis:
                                    first = (fs > 0 and fg == 0 and sub == 0 and xi == 0)
                                    last = (fg == nfg - 1 and sub == 3 and xi == xis[-1])
                                    r = P.matmul(
                                        psBIG[:, dmc * 512:(dmc + 1) * 512],
                                        fqT[:, (xi * 4 + sub) * 128:(xi * 4 + sub + 1) * 128],
                                        w2b[:, fc * D + dmc * 512: fc * D + dmc * 512 + 512],
                                        start=first, stop=last)
                        n = sP.inc(r)
                        note("fqT", sP, n)
                        return n

                    npe = emit_h(0)
                    nq = emit_gchain(0, npe)
                    for fg in range(1, nfg):
                        npe = emit_h(fg)
                        emit_w2(fg - 1, nq)
                        nq = emit_gchain(fg, npe)
                    emit_w2(nfg - 1, nq)
                    V.wait_ge(sP.h, sP.n)
                    vsync()
                    if fs == 0:
                        n = sV.inc(V.tensor_copy(h2acc[:, tt * D:(tt + 1) * D], psBIG[:, 0:D]))
                    else:
                        n = sV.inc(V.tensor_tensor(h2acc[:, tt * D:(tt + 1) * D],
                                                   h2acc[:, tt * D:(tt + 1) * D],
                                                   psBIG[:, 0:D], op=ALU.add))
                    note("psBIG", sV, n)
            for tt in range(ntt):
                out_fn(tt, h2acc)
            barrier()

    emit_quant_spill(w1n0, w1qT0_h, DFF, D, False, None, "a")
    emit_quant_spill(w2n0, w2qT0_h, D, DFF, False, None, "b")

    with nc.sbuf_tensor("q1T", [128, SL * D], BF16) as q1T:
        with nc.sbuf_tensor("f2a", [128, D], F32) as f2a, \
             nc.sbuf_tensor("f2b", [128, D], F32) as f2b, \
             nc.sbuf_tensor("f2c", [128, D], F32) as f2c, \
             nc.sbuf_tensor("xbt", [128, D], BF16) as xbt, \
             nc.sbuf_tensor("q1b", [128, D], BF16) as q1b, \
             nc.sbuf_tensor("lnw", [128, D], F32) as lnw, \
             nc.sbuf_tensor("lnb", [128, D], F32) as lnb, \
             nc.sbuf_tensor("am2", [128, D // 32], F32) as am2, \
             nc.sbuf_tensor("ex2", [128, D // 32], I32) as ex2, \
             nc.sbuf_tensor("sc2", [128, D // 32], F32) as sc2, \
             nc.sbuf_tensor("iv2", [128, D // 32], F32) as iv2:
            dma(SY, dX, lnw[:], lnwh)
            nl = dma(SY, dX, lnb[:], lnbh)
            V.wait_ge(dX.h, nl)
            vack()
            for j in range(SL):
                SY.wait_ge(sV.h, sV.n)
                nl = dma(SY, dX, f2a[:], stream_h[j * 128:(j + 1) * 128, :])
                V.wait_ge(dX.h, nl)
                emit_ln(f2a[:], D, f2b[:], f2c[:], stat[:, 4:5], stat[:, 5:6])
                V.wait_ge(dG.h, dG.n)
                V.tensor_copy(xbt[:], f2c[:])
                V.drain()
                n = sV.inc(V.tensor_copy(f2c[:], xbt[:]))
                V.wait_ge(sV.h, n)
                G_.wait_ge(sV.h, n)
                dma(G_, dG, xb_h[j * 128:(j + 1) * 128, :], xbt[:])
                emit_ln(f2c[:], D, f2a[:], f2b[:], stat[:, 6:7], stat[:, 7:8])
                V.tensor_tensor(f2b[:], f2b[:], lnw[:], op=ALU.mult)
                V.drain()
                V.tensor_tensor(f2b[:], f2b[:], lnb[:], op=ALU.add)
                V.drain()
                V.tensor_copy(q1b[:], f2b[:])
                V.drain()
                V.tensor_copy(f2b[:], q1b[:])
                V.drain()
                nq1 = emit_mx_quant(f2b[:], q1b[:], D, am2, ex2, sc2, iv2, f2a[:])
                emit_transposes(q1b[:], 0, q1T, j * DCH, DCH, True, nsrc=(sV, nq1))
            barrier()

        with nc.sbuf_tensor("o2a", [128, D], BF16) as o2a, \
             nc.sbuf_tensor("o2b", [128, D], BF16) as o2b, \
             nc.sbuf_tensor("o2c", [128, D], F32) as o2c:
            def out_l0(tt, h2acc):
                vsync()
                SY.wait_ge(sG.h, gack())
                nl = dma(SY, dX, o2a[:], xb_h[tt * 128:(tt + 1) * 128, :])
                nl2 = dma(SY, dX, o2c[:], stream_h[tt * 128:(tt + 1) * 128, :])
                V.wait_ge(dX.h, nl2)
                V.tensor_copy(o2b[:], h2acc[:, tt * D:(tt + 1) * D])
                V.drain()
                V.tensor_tensor(o2b[:], o2a[:], o2b[:], op=ALU.add)
                V.drain()
                n = sV.inc(V.tensor_tensor(o2c[:], o2c[:], o2b[:], op=ALU.add))
                G_.wait_ge(sV.h, n)
                dma(G_, dG, stream_h[tt * 128:(tt + 1) * 128, :], o2c[:])

            emit_ffn(w1qT0_h, w2qT0_h, b1h0, b2h0, [q1T], False, SL, out_l0, 512)

    if cfg.debug:
        G_.wait_ge(dG.h, dG.n)
        dma(G_, dG, dbg["dbg_s2"], stream_h)
        barrier()

    with nc.sbuf_tensor("xnT3", [128, SL * D], F32) as xnT3, \
         nc.sbuf_tensor("t3a", [128, D], F32) as t3a, \
         nc.sbuf_tensor("t3b", [128, D], F32) as t3b, \
         nc.sbuf_tensor("t3c", [128, D], F32) as t3c, \
         nc.sbuf_tensor("gam", [128, D], F32) as gam, \
         nc.sbuf_tensor("bet", [128, D], F32) as bet, \
         nc.sbuf_tensor("wb30", [128, DCH * 512], F32) as wb30, \
         nc.sbuf_tensor("wb31", [128, DCH * 512], F32) as wb31, \
         nc.sbuf_tensor("ev30", [128, 512], F32) as ev30, \
         nc.sbuf_tensor("ev31", [128, 512], F32) as ev31:
        dma(SY, dX, gam[:], gamh)
        nl = dma(SY, dX, bet[:], beth)
        V.wait_ge(dX.h, nl)
        vack()
        for j in range(SL):
            SY.wait_ge(sV.h, sV.n)
            nl = dma(SY, dX, t3a[:], stream_h[j * 128:(j + 1) * 128, :])
            V.wait_ge(dX.h, nl)
            emit_ln(t3a[:], D, t3b[:], t3c[:], mu3[:, j:j + 1], rstd3[:, j:j + 1])
            emit_ln(t3c[:], D, t3a[:], t3b[:], stat[:, 4:5], stat[:, 5:6])
            V.tensor_tensor(t3b[:], t3b[:], gam[:], op=ALU.mult)
            V.drain()
            n = sV.inc(V.tensor_tensor(t3b[:], t3b[:], bet[:], op=ALU.add))
            V.wait_ge(sV.h, n)
            emit_transposes(t3b[:], 0, xnT3, j * DCH, DCH, False, nsrc=(sV, n))
        wl = 0
        for sec, base in (('q', 0), ('k', KEY), ('v', 2 * KEY)):
            for oc in range(KEY // 512):
                wb, wbn = ((wb30, "wb30"), (wb31, "wb31"))[wl % 2]
                dwp = (dwA, dwB)[wl % 2]; wl += 1
                guard(SY, wbn)
                src = qkvw.rearrange("(dc p) n -> p dc n", p=128)[:, :, base + oc * 512: base + (oc + 1) * 512]
                wsem = dma(SY, dwp, wb[:].rearrange("p (dc n) -> p dc n", n=512), src)
                for j in range(SL):
                    psu, psn = ((psH0, "psH0"), (psH1, "psH1"))[j % 2]
                    P.wait_ge(dwp.h, wsem)
                    guard(P, psn)
                    for dc in range(DCH):
                        r = P.matmul(psu[:], xnT3[:, (j * DCH + dc) * 128:(j * DCH + dc + 1) * 128],
                                     wb[:, dc * 512:(dc + 1) * 512],
                                     start=(dc == 0), stop=(dc == DCH - 1))
                    n = sP.inc(r)
                    note(wbn, sP, n)
                    V.wait_ge(sP.h, n)
                    guard(V, "ev30")
                    if sec == 'q':
                        n = sV.inc(V.tensor_single_scalar(ev30[:, 0:512], psu[:], SC1, op=ALU.mult))
                    else:
                        n = sV.inc(V.tensor_copy(ev30[:, 0:512], psu[:]))
                    note(psn, sV, n)
                    if sec == 'v':
                        G_.wait_ge(sV.h, n)
                        nst = dma(G_, dG, v1o[j * 128:(j + 1) * 128, oc * 512:(oc + 1) * 512], ev30[:, 0:512])
                        note("ev30", dG, nst)
                    else:
                        V.wait_ge(sV.h, n)
                        guard(V, "ev31")
                        emit_transposes(ev30[:, 0:512], 0, ev31, 0, 4, False, nsrc=(sV, n))
                        note("ev30", sP, sP.n)
                        G_.wait_ge(sV.h, sV.n)
                        nst = spill_T(ev31[:, 0:512], 4, qT_h if sec == 'q' else kT1o, oc * 4, j * 128)
                        note("ev31", dG, nst)
        barrier()
        sCC.inc(G_.collective_compute(
            "AllGather", ALU.bypass, replica_groups=[[0, 1], [2, 3], [4, 5], [6, 7]],
            ins=[kT1o.tensor.ap().opt()], outs=[kT1g.tensor.ap().opt()]))
        sCC.inc(G_.collective_compute(
            "AllGather", ALU.bypass, replica_groups=[[0, 1], [2, 3], [4, 5], [6, 7]],
            ins=[v1o.tensor.ap().opt()], outs=[v1g.tensor.ap().opt()]))
        G_.wait_ge(sCC.h, sCC.n)
        barrier()

    with nc.sbuf_tensor("kb3", [128, DCH * 512], F32) as kb3, \
         nc.sbuf_tensor("vb3", [128, 4 * KEY], F32) as vb3, \
         nc.sbuf_tensor("qT3", [128, D], F32) as qT3, \
         nc.sbuf_tensor("Pb3", [128, 2048], F32) as Pb3, \
         nc.sbuf_tensor("PT3", [128, 2048], F32) as PT3, \
         nc.sbuf_tensor("Ot3", [128, D], F32) as Ot3, \
         nc.sbuf_tensor("oTe3", [128, 512], F32) as oTe3:
        for j in range(SL):
            nkt = 2 * j + 2
            nk = nkt * 128
            SY.wait_ge(sP.h, sP.n)
            qsrc = qT_h.rearrange("(dc p) t -> p dc t", p=128)[:, 0:DCH, j * 128:(j + 1) * 128]
            qsem = dma(SY, dX, qT3[:].rearrange("p (dc t) -> p dc t", t=128), qsrc)
            for c in range((nk + 511) // 512):
                tiles = range(c * 4, min(c * 4 + 4, nkt))
                SY.wait_ge(sP.h, sP.n)
                for kt in tiles:
                    r_ = owner_rank(kt); sl = owner_slot(kt)
                    ksrc = kT1g.rearrange("(dc p) t -> p dc t", p=128)[
                        :, r_ * DCH:(r_ + 1) * DCH, sl * 128:(sl + 1) * 128]
                    kdst = kb3[:].rearrange("p (dc n) -> p dc n", n=512)[:, :, (kt - c * 4) * 128:(kt - c * 4 + 1) * 128]
                    dma(SY, dK, kdst, ksrc)
                w = min(512, nk - c * 512)
                P.wait_ge(dK.h, dK.n)
                P.wait_ge(dX.h, qsem)
                P.wait_ge(sA.h, sA.n)
                P.wait_ge(sV.h, sV.n)
                for dc in range(DCH):
                    r = P.matmul(psBIG[:, c * 512:c * 512 + w],
                                 qT3[:, dc * 128:(dc + 1) * 128],
                                 kb3[:, dc * 512:dc * 512 + w],
                                 start=(dc == 0), stop=(dc == DCH - 1))
                sP.inc(r)
            V.wait_ge(sP.h, sP.n)
            V.tensor_tensor(psBIG[:, nk - 256:nk], psBIG[:, nk - 256:nk],
                            masks[:, j * 256:(j + 1) * 256], op=ALU.add)
            V.drain()
            n = sV.inc(V.reduce_max(negm[:], psBIG[:, 0:nk], axis=AX.X, negate=True))
            A.wait_ge(sV.h, n)
            na = sA.inc(A.activation(Pb3[:, 0:nk], psBIG[:, 0:nk], AF.Exp,
                                     bias=negm[:], accum_out=ssum[:]))
            V.wait_ge(sA.h, na)
            n = sV.inc(V.reciprocal(rsum[:], ssum[:]))
            emit_transposes(Pb3[:], 0, PT3, 0, nkt, False, nsrc=(sA, na))
            for c in range((nk + 511) // 512):
                tiles = list(range(c * 4, min(c * 4 + 4, nkt)))
                SY.wait_ge(sP.h, sP.n)
                for i, kt in enumerate(tiles):
                    r_ = owner_rank(kt); sl = owner_slot(kt)
                    vsrc = v1g[(r_ * TOK + sl * 128):(r_ * TOK + sl * 128 + 128), :]
                    dma(SY, dK, vb3[:, i * KEY:(i + 1) * KEY], vsrc)
                P.wait_ge(dK.h, dK.n)
                if c == 0:
                    P.wait_ge(sA.h, sA.n)
                    P.wait_ge(sV.h, sV.n)
                for dmc in range(NDM):
                    for i, kt in enumerate(tiles):
                        r = P.matmul(psBIG[:, dmc * 512:(dmc + 1) * 512],
                                     PT3[:, kt * 128:(kt + 1) * 128],
                                     vb3[:, i * KEY + dmc * 512: i * KEY + dmc * 512 + 512],
                                     start=(kt == 0), stop=(kt == nkt - 1))
                sP.inc(r)
            n = sP.n
            V.wait_ge(sP.h, n)
            for dmc in range(NDM):
                sV.inc(V.tensor_scalar(Ot3[:, dmc * 512:(dmc + 1) * 512],
                                       psBIG[:, dmc * 512:(dmc + 1) * 512],
                                       rsum[:], None, op0=ALU.mult))
            done = 0
            while done < DCH:
                nb = min(4, DCH - done)
                P.wait_ge(sV.h, sV.n)
                for c2 in range(nb):
                    r = P.transpose(psA_[:, c2 * 128:(c2 + 1) * 128],
                                    Ot3[:, (done + c2) * 128:(done + c2 + 1) * 128], ident[:])
                n = sP.inc(r)
                V.wait_ge(sP.h, n)
                V.wait_ge(dG.h, dG.n)
                n = sV.inc(V.tensor_copy(oTe3[:, 0:nb * 128], psA_[:, 0:nb * 128]))
                G_.wait_ge(sV.h, n)
                spill_T(oTe3[:, 0:nb * 128], nb, oT_h, done, j * 128)
                done += nb
        barrier()

    def resid_l1(j, oc, psu, npe, rt0, rt1, rt2):
        G_.wait_ge(sV.h, sV.n)
        nl = dma(G_, dG, rt0[:], stream_h[j * 128:(j + 1) * 128, oc * 512:(oc + 1) * 512])
        V.wait_ge(dG.h, nl)
        V.wait_ge(sP.h, npe)
        V.tensor_scalar(rt2[:], rt0[:], mu3[:, j:j + 1], None, op0=ALU.subtract)
        V.drain()
        V.tensor_scalar(rt2[:], rt2[:], rstd3[:, j:j + 1], None, op0=ALU.mult)
        V.drain()
        nps = sV.inc(V.tensor_tensor(rt1[:], rt2[:], psu[:], op=ALU.add))
        note("psH0" if psu is psH0 else "psH1", sV, nps)
        V.wait_ge(sV.h, nps)
        n = sV.inc(V.tensor_tensor(rt1[:], rt1[:], rt0[:], op=ALU.add))
        G_.wait_ge(sV.h, n)
        dma(G_, dG, stream_h[j * 128:(j + 1) * 128, oc * 512:(oc + 1) * 512], rt1[:])

    emit_out_proj(oprw, DCH, resid_l1)

    if cfg.debug:
        G_.wait_ge(dG.h, dG.n)
        dma(G_, dG, dbg["dbg_s3"], stream_h)
        barrier()

    def emit_absmean(w_hbm, nrows, ncols, inv_out, tag):
        with nc.sbuf_tensor("aw0" + tag, [128, ncols], F32) as aw0, \
             nc.sbuf_tensor("aw1" + tag, [128, ncols], F32) as aw1:
            for rc in range(nrows // 128):
                wb = (aw0, aw1)[rc % 2]
                dwp = (dwA, dwB)[rc % 2]
                SY.wait_ge(sV.h, sV.n)
                nl = dma(SY, dwp, wb[:], w_hbm[rc * 128:(rc + 1) * 128, :])
                V.wait_ge(dwp.h, nl)
                V.reduce_sum(stat[:, 9:10], wb[:], axis=AX.X, apply_absolute_value=True)
                V.drain()
                if rc == 0:
                    n = sV.inc(V.tensor_single_scalar(stat[:, 8:9], stat[:, 9:10], 1.0, op=ALU.mult))
                else:
                    n = sV.inc(V.tensor_tensor(stat[:, 8:9], stat[:, 8:9], stat[:, 9:10], op=ALU.add))
            P.wait_ge(sV.h, sV.n)
            n = sP.inc(P.matmul(psO[:, 0:1], onesq[:], stat[:, 8:9], start=True, stop=True))
            V.wait_ge(sP.h, n)
            V.tensor_single_scalar(stat[:, 10:11], psO[:, 0:1], 1.0 / (nrows * ncols), op=ALU.mult)
            V.drain()
            V.tensor_single_scalar(stat[:, 10:11], stat[:, 10:11], 1e-8, op=ALU.max)
            V.drain()
            n = sV.inc(V.reciprocal(inv_out, stat[:, 10:11]))
            barrier()

    emit_absmean(w1n1, DFF, D, inv0[:], "m1")
    emit_absmean(w2n1, D, DFF, inv1[:], "m2")
    emit_quant_spill(w1n1, w1qT1_h, DFF, D, True, inv0[:], "c")
    emit_quant_spill(w2n1, w2qT1_h, D, DFF, True, inv1[:], "d")

    HT = SL // 2 if SL >= 2 else 1
    for half in range((SL + HT - 1) // HT):
        tts = list(range(half * HT, min((half + 1) * HT, SL)))
        with nc.sbuf_tensor("xhiT%d" % half, [128, len(tts) * D], BF16) as xhiT, \
             nc.sbuf_tensor("xloT%d" % half, [128, len(tts) * D], BF16) as xloT:
            with nc.sbuf_tensor("f4a%d" % half, [128, D], F32) as f4a, \
                 nc.sbuf_tensor("f4b%d" % half, [128, D], F32) as f4b, \
                 nc.sbuf_tensor("f4c%d" % half, [128, D], F32) as f4c, \
                 nc.sbuf_tensor("xhib%d" % half, [128, D], BF16) as xhib, \
                 nc.sbuf_tensor("xlob%d" % half, [128, D], BF16) as xlob:
                for i, tt in enumerate(tts):
                    SY.wait_ge(sV.h, sV.n)
                    nl = dma(SY, dX, f4a[:], stream_h[tt * 128:(tt + 1) * 128, :])
                    V.wait_ge(dX.h, nl)
                    emit_ln(f4a[:], D, f4b[:], f4c[:], stat[:, 4:5], stat[:, 5:6])
                    V.tensor_copy(xhib[:], f4c[:])
                    V.drain()
                    V.tensor_tensor(f4b[:], f4c[:], xhib[:], op=ALU.subtract)
                    V.drain()
                    n = sV.inc(V.tensor_copy(xlob[:], f4b[:]))
                    V.wait_ge(sV.h, n)
                    emit_transposes(xhib[:], 0, xhiT, i * DCH, DCH, True, nsrc=(sV, n))
                    emit_transposes(xlob[:], 0, xloT, i * DCH, DCH, True)
                barrier()

            with nc.sbuf_tensor("o4a%d" % half, [128, D], F32) as o4a:
                def out_l1(ttl, h2acc, _tts=tts):
                    tt = _tts[ttl]
                    vsync()
                    SY.wait_ge(sG.h, gack())
                    nl = dma(SY, dX, o4a[:], stream_h[tt * 128:(tt + 1) * 128, :])
                    V.wait_ge(dX.h, nl)
                    n = sV.inc(V.tensor_tensor(o4a[:], o4a[:], h2acc[:, ttl * D:(ttl + 1) * D], op=ALU.add))
                    G_.wait_ge(sV.h, n)
                    dma(G_, dG, y_own[tt * 128:(tt + 1) * 128, :], o4a[:])

                emit_ffn(w1qT1_h, w2qT1_h, b1h1, b2h1, [xhiT, xloT], True, len(tts), out_l1, min(1024, DFF))

    barrier()
    es.close()
    return nc



def _rope_tables(cfg, positions, scale):
    half = 64
    j64 = (np.arange(half, dtype=np.float32) / np.float32(half)).astype(np.float32)
    inv = (np.float32(1.0) / (np.float32(10000.0) ** j64)).astype(np.float32)
    ang = (positions[:, None].astype(np.float32) * inv[None, :]).astype(np.float32)
    cos = np.cos(ang.astype(np.float64)).astype(np.float32) * np.float32(scale)
    sin = np.sin(ang.astype(np.float64)).astype(np.float32) * np.float32(scale)
    return cos.astype(np.float32), sin.astype(np.float32)


def _host_inputs(cfg, I, rank, seq):
    T, D, DFF, DK0, SL = cfg.T, cfg.D, cfg.DFF, cfg.DK0, cfg.SL
    rows = np.concatenate([np.arange(g_of(j, rank) * 128, g_of(j, rank) * 128 + 128)
                           for j in range(SL)])
    x2 = np.asarray(I["L0_attn_x"]).reshape(cfg.B * T, D)
    m = {}
    m["x_own"] = np.ascontiguousarray(x2[seq * T + rows])
    m["wqT"] = np.ascontiguousarray(np.asarray(I["L0_attn_Wq"]).T)
    m["wkT"] = np.ascontiguousarray(np.asarray(I["L0_attn_Wk"]).T)
    m["wvT"] = np.ascontiguousarray(np.asarray(I["L0_attn_Wv"]).T)
    m["wpT"] = np.ascontiguousarray(np.asarray(I["L0_attn_Wp"]).T)
    m["w1n0"] = np.ascontiguousarray(np.asarray(I["L0_ffn_W1"]))
    m["w2n0"] = np.ascontiguousarray(np.asarray(I["L0_ffn_W2"]))
    m["b1h0"] = np.ascontiguousarray(np.asarray(I["L0_ffn_b1"]).reshape(1, DFF))
    m["b2h0"] = np.ascontiguousarray(np.asarray(I["L0_ffn_b2"]).reshape(1, D))
    m["lnwh"] = np.ascontiguousarray(np.broadcast_to(np.asarray(I["L0_ffn_ln_w"]), (128, D)))
    m["lnbh"] = np.ascontiguousarray(np.broadcast_to(np.asarray(I["L0_ffn_ln_b"]), (128, D)))
    m["gamh"] = np.ascontiguousarray(np.broadcast_to(np.asarray(I["L1_attn_gamma"]), (128, D)))
    m["beth"] = np.ascontiguousarray(np.broadcast_to(np.asarray(I["L1_attn_beta"]), (128, D)))
    m["qkvw"] = np.ascontiguousarray(np.asarray(I["L1_attn_qkv"]))
    m["oprw"] = np.ascontiguousarray(np.asarray(I["L1_attn_o_proj"]))
    m["w1n1"] = np.ascontiguousarray(np.asarray(I["L1_ffn_w1"]))
    m["w2n1"] = np.ascontiguousarray(np.asarray(I["L1_ffn_w2"]))
    m["b1h1"] = np.ascontiguousarray(np.asarray(I["L1_ffn_b1"]).reshape(1, DFF))
    m["b2h1"] = np.ascontiguousarray(np.asarray(I["L1_ffn_b2"]).reshape(1, D))
    qs = 1.0 / np.sqrt(np.float32(cfg.DH))
    cq = np.empty((128, SL * 64), np.float32); sq = np.empty((128, SL * 64), np.float32)
    ck = np.empty((128, SL * 64), np.float32); sk = np.empty((128, SL * 64), np.float32)
    for j in range(SL):
        pos = np.arange(g_of(j, rank) * 128, g_of(j, rank) * 128 + 128)
        c1, s1 = _rope_tables(cfg, pos, qs)
        c2, s2 = _rope_tables(cfg, pos, 1.0)
        cq[:, j * 64:(j + 1) * 64] = c1; sq[:, j * 64:(j + 1) * 64] = s1
        ck[:, j * 64:(j + 1) * 64] = c2; sk[:, j * 64:(j + 1) * 64] = s2
    m["cosqh"] = cq; m["sinqh"] = sq; m["coskh"] = ck; m["sinkh"] = sk
    tri = np.triu(np.full((128, 128), NEG_BIG, np.float32), 1)
    full = np.full((128, 128), NEG_BIG, np.float32)
    zero = np.zeros((128, 128), np.float32)
    mk = np.empty((128, SL * 256), np.float32)
    for j in range(SL):
        g = g_of(j, rank)
        if g == 2 * j:
            mk[:, j * 256: j * 256 + 128] = tri
            mk[:, j * 256 + 128: j * 256 + 256] = full
        else:
            mk[:, j * 256: j * 256 + 128] = zero
            mk[:, j * 256 + 128: j * 256 + 256] = tri
    m["masksh"] = mk
    m["identh"] = np.eye(128, dtype=np.float32)
    return m, rows


def kernel(**inputs):
    cfg = Cfg()
    nc = build(cfg)
    in_maps = []
    row_list = []
    for core in range(8):
        seq, rank = core // 2, core % 2
        m, rows = _host_inputs(cfg, inputs, rank, seq)
        in_maps.append(m)
        row_list.append((seq, rows))
    res = run_bass_kernel_spmd(nc, in_maps, core_ids=list(range(8)))
    out = np.empty((cfg.B, cfg.T, cfg.D), np.float32)
    for core in range(8):
        seq, rows = row_list[core]
        out.reshape(cfg.B * cfg.T, cfg.D)[seq * cfg.T + rows] = res.results[core]["y_own"]
    return out


# version 03
# speedup vs baseline: 1.1981x; 1.0815x over previous
import numpy as np
import ml_dtypes
from contextlib import ExitStack

import concourse.bass as bass
import concourse.mybir as mybir
from concourse.bass_utils import run_bass_kernel_spmd

F32 = mybir.dt.float32
I32 = mybir.dt.int32
BF16 = mybir.dt.bfloat16
AF = mybir.ActivationFunctionType
ALU = mybir.AluOpType
AX = mybir.AxisListType

MAGIC = float(np.float32(12582912.0))
NEG_BIG = float(np.finfo(np.float32).min)
AMAX_FLOOR = float(np.float32(2.0 ** -100))
EPS_LN = 1e-5


class Cfg:
    def __init__(self, T=2048, D=2048, DFF=8192, DK0=512, debug=False):
        self.T, self.D, self.DFF, self.DK0 = T, D, DFF, DK0
        self.B = 4
        self.DH = 128
        self.H0 = D // self.DH
        self.HK0 = DK0 // self.DH
        self.KEY = D
        self.G = T // 128
        self.SL = self.G // 2
        self.TOK = self.SL * 128
        self.DCH = D // 128
        self.debug = debug


def g_of(j, r):
    return 2 * j + ((j + r) % 2)


def owner_rank(kt):
    return (kt // 2 + kt % 2) % 2


def owner_slot(kt):
    return kt // 2


def build(cfg):
    nc = bass.Bass("TRN2", target_bir_lowering=False)
    T, D, DFF, DK0 = cfg.T, cfg.D, cfg.DFF, cfg.DK0
    SL, TOK, DCH = cfg.SL, cfg.TOK, cfg.DCH
    H0, HK0, KEY = cfg.H0, cfg.HK0, cfg.KEY
    GRP = H0 // HK0
    FSUP = min(1024, DFF)
    NDM = D // 512
    SC1 = float(np.float32(1.0 / np.sqrt(np.float32(KEY))))

    def din(name, shape, dt=F32):
        return nc.dram_tensor(name, shape, dt, kind="ExternalInput").ap()

    x_own = din("x_own", [TOK, D])
    wqT = din("wqT", [D, D]); wkT = din("wkT", [D, DK0]); wvT = din("wvT", [D, DK0]); wpT = din("wpT", [D, D])
    w1n0 = din("w1n0", [DFF, D]); w2n0 = din("w2n0", [D, DFF])
    b1h0 = din("b1h0", [1, DFF]); b2h0 = din("b2h0", [1, D])
    lnwh = din("lnwh", [128, D]); lnbh = din("lnbh", [128, D])
    gamh = din("gamh", [128, D]); beth = din("beth", [128, D])
    qkvw = din("qkvw", [D, 3 * KEY]); oprw = din("oprw", [KEY, D])
    w1n1 = din("w1n1", [DFF, D]); w2n1 = din("w2n1", [D, DFF])
    b1h1 = din("b1h1", [1, DFF]); b2h1 = din("b2h1", [1, D])
    cosqh = din("cosqh", [128, SL * 64]); sinqh = din("sinqh", [128, SL * 64])
    coskh = din("coskh", [128, SL * 64]); sinkh = din("sinkh", [128, SL * 64])
    masksh = din("masksh", [128, SL * 256])
    identh = din("identh", [128, 128])
    y_own = nc.dram_tensor("y_own", [TOK, D], F32, kind="ExternalOutput").ap()
    dbg = {}
    if cfg.debug:
        for nm in ("dbg_s1", "dbg_s2", "dbg_s3"):
            dbg[nm] = nc.dram_tensor(nm, [TOK, D], F32, kind="ExternalOutput").ap()

    def dint(name, shape, dt=F32):
        return nc.dram_tensor(name, shape, dt).ap()

    stream_h = dint("stream_h", [TOK, D])
    xb_h = dint("xb_h", [TOK, D], BF16)
    qT_h = dint("qT_h", [max(D, KEY), TOK])
    oT_h = dint("oT_h", [D, TOK])
    kT0o = dint("kT0o", [DK0, TOK]); kT0g = dint("kT0g", [2 * DK0, TOK])
    v0o = dint("v0o", [TOK, DK0]); v0g = dint("v0g", [2 * TOK, DK0])
    kT1o = dint("kT1o", [KEY, TOK]); kT1g = dint("kT1g", [2 * KEY, TOK])
    v1o = dint("v1o", [TOK, KEY]); v1g = dint("v1g", [2 * TOK, KEY])
    NF = DFF // 128
    w1qT0_h = dint("w1qT0_h", [DCH * NF, 128 * 128], BF16)
    w2qT0_h = dint("w2qT0_h", [NF * DCH, 128 * 128], BF16)
    w1qT1_h = dint("w1qT1_h", [DCH * NF, 128 * 128], BF16)
    w2qT1_h = dint("w2qT1_h", [NF * DCH, 128 * 128], BF16)

    es = ExitStack()
    sbuf = lambda name, shape, dt=F32: es.enter_context(nc.sbuf_tensor(name, shape, dt))

    class Sem:
        __slots__ = ("h", "n")
        def __init__(self, name):
            self.h = es.enter_context(nc.semaphore(name))
            self.n = 0
        def inc(self, instr, k=1):
            instr.then_inc(self.h, k)
            self.n += k
            return self.n

    dW = Sem("dW"); dX = Sem("dX"); dK = Sem("dK"); dG = Sem("dG")
    dwA = Sem("dwA"); dwB = Sem("dwB"); dwF = Sem("dwF"); dGL = Sem("dGL")
    dSv = Sem("dSv"); dQe0 = Sem("dQe0"); dQe1 = Sem("dQe1")
    sV = Sem("sV"); sA = Sem("sA"); sP = Sem("sP"); sCC = Sem("sCC")
    sB = Sem("sB"); sG = Sem("sG")
    CSEMS = (sV, sA, sP, sCC)

    V, A, P, G_, SY = nc.vector, nc.scalar, nc.tensor, nc.gpsimd, nc.sync

    ident = sbuf("ident", [128, 128])
    identb = sbuf("identb", [128, 128], BF16)
    ones1 = sbuf("ones1", [1, 128])
    onesq = sbuf("onesq", [128, 128])
    masks = sbuf("masks", [128, SL * 256])
    stat = sbuf("stat", [128, 16])
    mu3 = sbuf("mu3", [128, SL]); rstd3 = sbuf("rstd3", [128, SL])
    inv0 = sbuf("inv0", [128, 1]); inv1 = sbuf("inv1", [128, 1])
    negm = sbuf("negm", [128, 1]); ssum = sbuf("ssum", [128, 1]); rsum = sbuf("rsum", [128, 1])
    bb1 = sbuf("bb1", [1, FSUP])
    bb2 = sbuf("bb2", [1, D])

    psBIG = es.enter_context(nc.psum_tensor("psBIG", [128, 2048], F32))
    psH0 = es.enter_context(nc.psum_tensor("psH0", [128, 512], F32))
    psH1 = es.enter_context(nc.psum_tensor("psH1", [128, 512], F32))
    psA_ = es.enter_context(nc.psum_tensor("psA_", [128, 512], F32))
    psO = es.enter_context(nc.psum_tensor("psO", [128, 512], F32))

    def barrier():
        SY.wait_ge(dW.h, dW.n); SY.wait_ge(dX.h, dX.n); SY.wait_ge(dK.h, dK.n)
        SY.wait_ge(dwA.h, dwA.n); SY.wait_ge(dwB.h, dwB.n); SY.wait_ge(dwF.h, dwF.n)
        SY.sem_inc(sB.h, 1); sB.n += 1
        G_.wait_ge(dG.h, dG.n)
        G_.wait_ge(dGL.h, dGL.n)
        G_.wait_ge(dSv.h, dSv.n)
        G_.wait_ge(dQe0.h, dQe0.n); G_.wait_ge(dQe1.h, dQe1.n)
        G_.sem_inc(sB.h, 1); sB.n += 1
        for eng in (V, A, P, G_, SY):
            eng.wait_ge(sB.h, sB.n)
            for s in CSEMS:
                eng.wait_ge(s.h, s.n)
        pend.clear()

    pend = {}

    def note(name, sem, val):
        d = pend.setdefault(name, {})
        d[id(sem)] = (sem, max(val, d.get(id(sem), (None, -1))[1]))

    def guard(eng, name):
        for sem, v in pend.pop(name, {}).values():
            eng.wait_ge(sem.h, v)

    def vsync():
        V.wait_ge(sV.h, sV.n)

    def vack():
        return sV.inc(V.tensor_copy(stat[:, 15:16], stat[:, 14:15]))

    def gack():
        G_.wait_ge(dG.h, dG.n)
        G_.sem_inc(sG.h, 1); sG.n += 1
        return sG.n

    def dma(q, qsem, out, in_):
        q.dma_start(out=out, in_=in_).then_inc(qsem.h, 16)
        qsem.n += 16
        return qsem.n

    def emit_ln(xt, ncols, dump, xn_out, mu_ap, rstd_ap):
        V.reduce_sum(stat[:, 0:1], xt, axis=AX.X)
        V.drain()
        V.tensor_single_scalar(mu_ap, stat[:, 0:1], 1.0 / ncols, op=ALU.mult)
        V.drain()
        n = sV.inc(V.tensor_scalar(xn_out, xt, mu_ap, None, op0=ALU.subtract))
        A.wait_ge(sV.h, n)
        n = sA.inc(A.activation(dump, xn_out, AF.Square, accum_out=stat[:, 1:2]))
        V.wait_ge(sA.h, n)
        n = sV.inc(V.tensor_scalar(stat[:, 2:3], stat[:, 1:2], 1.0 / ncols, EPS_LN,
                                   op0=ALU.mult, op1=ALU.add))
        A.wait_ge(sV.h, n)
        n = sA.inc(A.activation(stat[:, 3:4], stat[:, 2:3], AF.Sqrt))
        V.wait_ge(sA.h, n)
        V.reciprocal(rstd_ap, stat[:, 3:4])
        V.drain()
        n = sV.inc(V.tensor_scalar(xn_out, xn_out, rstd_ap, None, op0=ALU.mult))
        V.wait_ge(sV.h, n)
        return n

    def emit_transposes(src, soff, dst, doff, k, bf16, nsrc=None):
        idt = identb if bf16 else ident
        pa = psA_[:].bitcast(BF16) if bf16 else psA_[:]
        if nsrc is not None:
            P.wait_ge(nsrc[0].h, nsrc[1])
        done = 0
        while done < k:
            nb = min(4, k - done)
            guard(P, "psA")
            for c in range(nb):
                n = sP.inc(P.transpose(pa[:, c * 128:(c + 1) * 128],
                                       src[:, (soff + done + c) * 128:(soff + done + c + 1) * 128], idt[:]))
            V.wait_ge(sP.h, n)
            nv = sV.inc(V.tensor_copy(dst[:, (doff + done) * 128:(doff + done + nb) * 128],
                                      pa[:, 0:nb * 128]))
            note("psA", sV, nv)
            done += nb

    def spill_T(src, k, dsth, row0_chunk, col0, qsem=None):
        dst = dsth.rearrange("(c p) t -> p c t", p=128)[:, row0_chunk:row0_chunk + k, col0:col0 + 128]
        return dma(G_, qsem if qsem is not None else dG, dst, src.rearrange("p (c t) -> p c t", t=128))

    def emit_mx_quant(xin, xout_bf, ncols, amx, expb, scl, invs, vtmp):
        nb = ncols // 32
        x3 = xin.rearrange("p (b k) -> p b k", k=32)
        V.tensor_reduce(amx[:, 0:nb], x3, axis=AX.X, op=ALU.max, apply_absolute_value=True)
        V.drain()
        V.tensor_single_scalar(amx[:, 0:nb], amx[:, 0:nb], AMAX_FLOOR, op=ALU.max)
        V.drain()
        V.tensor_single_scalar(expb[:, 0:nb], amx[:, 0:nb].bitcast(I32), 0x7F800000, op=ALU.bitwise_and)
        V.drain()
        V.tensor_single_scalar(scl[:, 0:nb], expb[:, 0:nb].bitcast(F32),
                               float(np.float32(2.0 ** -6)), op=ALU.mult)
        V.drain()
        V.reciprocal(invs[:, 0:nb], scl[:, 0:nb])
        V.drain()
        inv3 = invs[:, 0:nb].unsqueeze(2).broadcast_to([128, nb, 32])
        v3 = vtmp.rearrange("p (b k) -> p b k", k=32)
        V.tensor_tensor(v3, x3, inv3, op=ALU.mult)
        V.drain()
        V.tensor_scalar(vtmp, vtmp, MAGIC, float(np.float32(MAGIC + 127.0)), op0=ALU.add, op1=ALU.min)
        V.drain()
        V.tensor_scalar(vtmp, vtmp, float(np.float32(MAGIC - 128.0)), -MAGIC, op0=ALU.max, op1=ALU.add)
        V.drain()
        sc3 = scl[:, 0:nb].unsqueeze(2).broadcast_to([128, nb, 32])
        o3 = xout_bf.rearrange("p (b k) -> p b k", k=32)
        n = sV.inc(V.tensor_tensor(o3, v3, sc3, op=ALU.mult))
        V.wait_ge(sV.h, n)
        return n

    dma(SY, dX, ident[:], identh)
    dma(SY, dX, masks[:], masksh)
    G_.memset(ones1[:], 1.0)
    V.memset(stat[:], 0.0)
    V.drain()
    n = sV.inc(G_.memset(onesq[:], 1.0))
    dma(G_, dG, stream_h, x_own)
    V.wait_ge(dX.h, dX.n)
    sV.inc(V.tensor_copy(identb[:], ident[:]))
    barrier()

    with nc.sbuf_tensor("xnT", [128, SL * D], F32) as xnT, \
         nc.sbuf_tensor("ta", [128, D], F32) as ta, \
         nc.sbuf_tensor("tb", [128, D], F32) as tb, \
         nc.sbuf_tensor("tc", [128, D], F32) as tc, \
         nc.sbuf_tensor("wb0", [128, DCH * 512], F32) as wb0, \
         nc.sbuf_tensor("wb1", [128, DCH * 512], F32) as wb1, \
         nc.sbuf_tensor("ev0", [128, 512], F32) as ev0, \
         nc.sbuf_tensor("ev1", [128, 512], F32) as ev1, \
         nc.sbuf_tensor("cq", [128, SL * 64], F32) as cq, \
         nc.sbuf_tensor("sq", [128, SL * 64], F32) as sq, \
         nc.sbuf_tensor("ck", [128, SL * 64], F32) as ck, \
         nc.sbuf_tensor("sk", [128, SL * 64], F32) as sk:
        dma(SY, dX, cq[:], cosqh); dma(SY, dX, sq[:], sinqh)
        dma(SY, dX, ck[:], coskh); dma(SY, dX, sk[:], sinkh)
        V.wait_ge(dX.h, dX.n)
        vack()
        for j in range(SL):
            SY.wait_ge(sV.h, sV.n)
            nld = dma(SY, dX, ta[:], stream_h[j * 128:(j + 1) * 128, :])
            V.wait_ge(dX.h, nld)
            nln = emit_ln(ta[:], D, tb[:], tc[:], stat[:, 4:5], stat[:, 5:6])
            emit_transposes(tc[:], 0, xnT, j * DCH, DCH, False, nsrc=(sV, nln))

        def rope_spill(kind, oc, j):
            ct = (cq if kind == 'q' else ck)[:, j * 64:(j + 1) * 64]
            st_ = (sq if kind == 'q' else sk)[:, j * 64:(j + 1) * 64]
            cb = ct.unsqueeze(1).broadcast_to([128, 4, 64])
            sb_ = st_.unsqueeze(1).broadcast_to([128, 4, 64])
            e3 = ev0[:, 0:512].rearrange("p (h c) -> p h c", c=128)
            o3 = ev1[:, 0:512].rearrange("p (h c) -> p h c", c=128)
            x1 = e3[:, :, 0:64]; x2 = e3[:, :, 64:128]
            t1 = ta[:, 0:256].rearrange("p (h c) -> p h c", c=64)
            t2 = ta[:, 256:512].rearrange("p (h c) -> p h c", c=64)
            guard(V, "ev1")
            V.tensor_tensor(t1, x1, cb, op=ALU.mult)
            V.tensor_tensor(t2, x2, sb_, op=ALU.mult)
            V.drain()
            V.tensor_tensor(o3[:, :, 0:64], t1, t2, op=ALU.subtract)
            V.drain()
            V.tensor_tensor(t1, x2, cb, op=ALU.mult)
            V.tensor_tensor(t2, x1, sb_, op=ALU.mult)
            V.drain()
            n = sV.inc(V.tensor_tensor(o3[:, :, 64:128], t1, t2, op=ALU.add))
            P.wait_ge(sV.h, n)
            guard(P, "psA")
            for c in range(4):
                r = sP.inc(P.transpose(psA_[:, c * 128:(c + 1) * 128],
                                       ev1[:, c * 128:(c + 1) * 128], ident[:]))
            n = r
            note("ev1", sP, n)
            V.wait_ge(sP.h, n)
            guard(V, "ev0")
            n = sV.inc(V.tensor_copy(ev0[:, 0:512], psA_[:]))
            note("psA", sV, n)
            G_.wait_ge(sV.h, n)
            nst = spill_T(ev0[:, 0:512], 4, qT_h if kind == 'q' else kT0o, oc * 4, j * 128)
            note("ev0", dG, nst)

        wl = 0
        for wT_hbm, nout, kind in ((wqT, D, 'q'), (wkT, DK0, 'k'), (wvT, DK0, 'v')):
            for oc in range(nout // 512):
                wb, wbn = ((wb0, "wb0"), (wb1, "wb1"))[wl % 2]
                dwp = (dwA, dwB)[wl % 2]; wl += 1
                guard(SY, wbn)
                src = wT_hbm.rearrange("(dc p) n -> p dc n", p=128)[:, :, oc * 512:(oc + 1) * 512]
                wsem = dma(SY, dwp, wb[:].rearrange("p (dc n) -> p dc n", n=512), src)
                for j in range(SL):
                    psu, psn = ((psH0, "psH0"), (psH1, "psH1"))[j % 2]
                    P.wait_ge(dwp.h, wsem)
                    guard(P, psn)
                    for dc in range(DCH):
                        r = P.matmul(psu[:], xnT[:, (j * DCH + dc) * 128:(j * DCH + dc + 1) * 128],
                                     wb[:, dc * 512:(dc + 1) * 512],
                                     start=(dc == 0), stop=(dc == DCH - 1))
                    n = sP.inc(r)
                    note(wbn, sP, n)
                    V.wait_ge(sP.h, n)
                    guard(V, "ev0")
                    n = sV.inc(V.tensor_copy(ev0[:, 0:512], psu[:]))
                    note(psn, sV, n)
                    if kind == 'v':
                        G_.wait_ge(sV.h, n)
                        nst = dma(G_, dSv, v0o[j * 128:(j + 1) * 128, oc * 512:(oc + 1) * 512], ev0[:, 0:512])
                        note("ev0", dSv, nst)
                    else:
                        V.wait_ge(sV.h, n)
                        rope_spill(kind, oc, j)
        barrier()

        sCC.inc(G_.collective_compute(
            "AllGather", ALU.bypass, replica_groups=[[0, 1], [2, 3], [4, 5], [6, 7]],
            ins=[kT0o.tensor.ap().opt()], outs=[kT0g.tensor.ap().opt()]))
        sCC.inc(G_.collective_compute(
            "AllGather", ALU.bypass, replica_groups=[[0, 1], [2, 3], [4, 5], [6, 7]],
            ins=[v0o.tensor.ap().opt()], outs=[v0g.tensor.ap().opt()]))
        G_.wait_ge(sCC.h, sCC.n)
        barrier()

    with nc.sbuf_tensor("kbuf", [128, HK0 * 2048], F32) as kbuf, \
         nc.sbuf_tensor("vbuf", [128, 16 * DK0], F32) as vbuf, \
         nc.sbuf_tensor("qTs", [128, D], F32) as qTs, \
         nc.sbuf_tensor("Pb", [128, 2048], F32) as Pb, \
         nc.sbuf_tensor("PTb", [128, 2048], F32) as PTb, \
         nc.sbuf_tensor("Ot", [128, D], F32) as Ot, \
         nc.sbuf_tensor("oTe", [128, 512], F32) as oTe:
        for j in range(SL):
            nkt = 2 * j + 2
            nk = nkt * 128
            SY.wait_ge(sP.h, sP.n)
            qsrc = qT_h.rearrange("(dc p) t -> p dc t", p=128)[:, 0:DCH, j * 128:(j + 1) * 128]
            qsem = dma(SY, dX, qTs[:].rearrange("p (dc t) -> p dc t", t=128), qsrc)
            for kt in range(nkt):
                r_ = owner_rank(kt); sl = owner_slot(kt)
                ksrc = kT0g.rearrange("(c p) t -> p c t", p=128)[
                    :, r_ * HK0:(r_ + 1) * HK0, sl * 128:(sl + 1) * 128]
                kdst = kbuf[:].rearrange("p (c n) -> p c n", n=2048)[:, :, kt * 128:(kt + 1) * 128]
                dma(SY, dK, kdst, ksrc)
                vsrc = v0g[(r_ * TOK + sl * 128):(r_ * TOK + sl * 128 + 128), :]
                dma(SY, dK, vbuf[:, kt * DK0:(kt + 1) * DK0], vsrc)
            ldsem = dK.n
            for h in range(H0):
                kvh = h // GRP
                P.wait_ge(dK.h, ldsem)
                P.wait_ge(dX.h, qsem)
                P.wait_ge(sA.h, sA.n)
                P.wait_ge(sV.h, sV.n)
                for c in range((nk + 511) // 512):
                    w = min(512, nk - c * 512)
                    r = P.matmul(psBIG[:, c * 512:c * 512 + w],
                                 qTs[:, h * 128:(h + 1) * 128],
                                 kbuf[:, kvh * 2048 + c * 512: kvh * 2048 + c * 512 + w],
                                 start=True, stop=True)
                npe = sP.inc(r)
                V.wait_ge(sP.h, npe)
                V.tensor_tensor(psBIG[:, nk - 256:nk], psBIG[:, nk - 256:nk],
                                masks[:, j * 256:(j + 1) * 256], op=ALU.add)
                V.drain()
                n = sV.inc(V.reduce_max(negm[:], psBIG[:, 0:nk], axis=AX.X, negate=True))
                A.wait_ge(sV.h, n)
                na = sA.inc(A.activation(Pb[:, 0:nk], psBIG[:, 0:nk], AF.Exp,
                                         bias=negm[:], accum_out=ssum[:]))
                V.wait_ge(sA.h, na)
                n = sV.inc(V.reciprocal(rsum[:], ssum[:]))
                emit_transposes(Pb[:], 0, PTb, 0, nkt, False, nsrc=(sA, na))
                P.wait_ge(sV.h, sV.n)
                for kt in range(nkt):
                    r = P.matmul(psO[:, 0:128], PTb[:, kt * 128:(kt + 1) * 128],
                                 vbuf[:, kt * DK0 + kvh * 128: kt * DK0 + (kvh + 1) * 128],
                                 start=(kt == 0), stop=(kt == nkt - 1))
                n = sP.inc(r)
                V.wait_ge(sP.h, n)
                sV.inc(V.tensor_scalar(Ot[:, h * 128:(h + 1) * 128], psO[:, 0:128],
                                       rsum[:], None, op0=ALU.mult))
            done = 0
            while done < DCH:
                nb = min(4, DCH - done)
                P.wait_ge(sV.h, sV.n)
                for c in range(nb):
                    r = P.transpose(psA_[:, c * 128:(c + 1) * 128],
                                    Ot[:, (done + c) * 128:(done + c + 1) * 128], ident[:])
                n = sP.inc(r)
                V.wait_ge(sP.h, n)
                guard(V, "oTe")
                n = sV.inc(V.tensor_copy(oTe[:, 0:nb * 128], psA_[:, 0:nb * 128]))
                G_.wait_ge(sV.h, n)
                nst = spill_T(oTe[:, 0:nb * 128], nb, oT_h, done, j * 128)
                note("oTe", dG, nst)
                done += nb
        barrier()

    _opctr = [0]

    def emit_out_proj(wT_hbm, nin_ch, resid_fn):
        _opctr[0] += 1
        tg = str(_opctr[0])
        with nc.sbuf_tensor("pwb0" + tg, [128, nin_ch * 512], F32) as pwb0, \
             nc.sbuf_tensor("pwb1" + tg, [128, nin_ch * 512], F32) as pwb1, \
             nc.sbuf_tensor("poT" + tg, [128, nin_ch * 128], F32) as poT, \
             nc.sbuf_tensor("prt0" + tg, [128, 512], F32) as prt0, \
             nc.sbuf_tensor("prt1" + tg, [128, 512], F32) as prt1, \
             nc.sbuf_tensor("prt2" + tg, [128, 512], F32) as prt2:
            for oc in range(D // 512):
                wb, wbn = ((pwb0, "pwb0"), (pwb1, "pwb1"))[oc % 2]
                dwp = (dwA, dwB)[oc % 2]
                guard(SY, wbn)
                src = wT_hbm.rearrange("(dc p) n -> p dc n", p=128)[:, :, oc * 512:(oc + 1) * 512]
                wsem = dma(SY, dwp, wb[:].rearrange("p (dc n) -> p dc n", n=512), src)
                for j in range(SL):
                    guard(SY, "poT")
                    xsrc = oT_h.rearrange("(dc p) t -> p dc t", p=128)[:, 0:nin_ch, j * 128:(j + 1) * 128]
                    nx = dma(SY, dX, poT[:].rearrange("p (dc t) -> p dc t", t=128), xsrc)
                    psu, psn = ((psH0, "psH0"), (psH1, "psH1"))[j % 2]
                    P.wait_ge(dwp.h, wsem)
                    P.wait_ge(dX.h, nx)
                    guard(P, psn)
                    for dc in range(nin_ch):
                        r = P.matmul(psu[:], poT[:, dc * 128:(dc + 1) * 128],
                                     wb[:, dc * 512:(dc + 1) * 512],
                                     start=(dc == 0), stop=(dc == nin_ch - 1))
                    npe = sP.inc(r)
                    note(wbn, sP, npe)
                    note("poT", sP, npe)
                    resid_fn(j, oc, psu, npe, prt0, prt1, prt2)
            barrier()

    def resid_l0(j, oc, psu, npe, rt0, rt1, rt2):
        G_.wait_ge(sV.h, sV.n)
        nl = dma(G_, dGL, rt0[:], stream_h[j * 128:(j + 1) * 128, oc * 512:(oc + 1) * 512])
        V.wait_ge(dGL.h, nl)
        V.wait_ge(sP.h, npe)
        guard(V, "rt1")
        n = sV.inc(V.tensor_tensor(rt1[:], rt0[:], psu[:], op=ALU.add))
        note("psH0" if psu is psH0 else "psH1", sV, n)
        G_.wait_ge(sV.h, n)
        nst = dma(G_, dG, stream_h[j * 128:(j + 1) * 128, oc * 512:(oc + 1) * 512], rt1[:])
        note("rt1", dG, nst)

    emit_out_proj(wpT, DCH, resid_l0)

    if cfg.debug:
        G_.wait_ge(dG.h, dG.n)
        dma(G_, dG, dbg["dbg_s1"], stream_h)
        barrier()

    def emit_quant_spill(w_hbm, wqT_hbm, nrows, ncols, ternary, inv_ap, tag):
        nrc = nrows // 128
        with nc.sbuf_tensor("qw0" + tag, [128, ncols], F32) as qw0, \
             nc.sbuf_tensor("qw1" + tag, [128, ncols], F32) as qw1, \
             nc.sbuf_tensor("qv" + tag, [128, ncols], F32) as qv, \
             nc.sbuf_tensor("qb" + tag, [128, ncols], BF16) as qb, \
             nc.sbuf_tensor("qe0" + tag, [128, 512], BF16) as qe0, \
             nc.sbuf_tensor("qe1" + tag, [128, 512], BF16) as qe1, \
             nc.sbuf_tensor("qam" + tag, [128, ncols // 32], F32) as qam, \
             nc.sbuf_tensor("qex" + tag, [128, ncols // 32], I32) as qex, \
             nc.sbuf_tensor("qsc" + tag, [128, ncols // 32], F32) as qsc, \
             nc.sbuf_tensor("qiv" + tag, [128, ncols // 32], F32) as qiv:
            wq4 = wqT_hbm.rearrange("(cc rr) (p t) -> p cc rr t", rr=nrc, t=128)
            qei = 0
            for rc in range(nrc):
                wbq, wbqn = ((qw0, "qw0"), (qw1, "qw1"))[rc % 2]
                dwp = (dwA, dwB)[rc % 2]
                guard(SY, wbqn)
                nl = dma(SY, dwp, wbq[:], w_hbm[rc * 128:(rc + 1) * 128, :])
                V.wait_ge(dwp.h, nl)
                guard(V, "qb")
                if ternary:
                    V.tensor_scalar(qv[:], wbq[:], inv_ap, None, op0=ALU.mult)
                    V.drain()
                    V.tensor_scalar(qv[:], qv[:], MAGIC, float(np.float32(MAGIC + 1.0)),
                                    op0=ALU.add, op1=ALU.min)
                    V.drain()
                    nq = sV.inc(V.tensor_scalar(qb[:], qv[:], float(np.float32(MAGIC - 1.0)), -MAGIC,
                                                 op0=ALU.max, op1=ALU.add))
                    V.wait_ge(sV.h, nq)
                else:
                    emit_mx_quant(wbq[:], qb[:], ncols, qam, qex, qsc, qiv, qv[:])
                note(wbqn, sV, sV.n)
                P.wait_ge(sV.h, sV.n)
                done = 0
                while done < ncols // 128:
                    nb_ = min(4, ncols // 128 - done)
                    guard(P, "psA")
                    pa = psA_[:].bitcast(BF16)
                    for c in range(nb_):
                        n = sP.inc(P.transpose(pa[:, c * 128:(c + 1) * 128],
                                               qb[:, (done + c) * 128:(done + c + 1) * 128], identb[:]))
                    note("qb", sP, n)
                    V.wait_ge(sP.h, n)
                    qe, qen, dqe = ((qe0, "qe0", dQe0), (qe1, "qe1", dQe1))[qei % 2]
                    qei += 1
                    guard(V, qen)
                    n = sV.inc(V.tensor_copy(qe[:, 0:nb_ * 128], pa[:, 0:nb_ * 128]))
                    note("psA", sV, n)
                    G_.wait_ge(sV.h, n)
                    dst = wq4[:, done:done + nb_, rc:rc + 1, :]
                    srcq = qe[:, 0:nb_ * 128].rearrange("p (c t) -> p c t", t=128).unsqueeze(2)
                    nst = dma(G_, dqe, dst, srcq)
                    note(qen, dqe, nst)
                    done += nb_
            barrier()

    _ffnctr = [0]

    def emit_ffn(w1qT_hbm, w2qT_hbm, b1_hbm, b2_hbm, xT_list, l1, ntt, out_fn, fsup):
        nfs = DFF // fsup
        nfg = fsup // 512
        xis = (0, 1) if l1 else (0,)
        _ffnctr[0] += 1
        tg = str(_ffnctr[0])
        fes = ExitStack()
        fsb = lambda name, shape, dt=F32: fes.enter_context(nc.sbuf_tensor(name + tg, shape, dt))
        h2acc = fsb("fh2", [128, ntt * D])
        w1b = fsb("fw1", [128, DCH * fsup], BF16)
        w2b = fsb("fw2", [128, (fsup // 128) * D], BF16)
        fqT = fsb("fqT", [128, 8 * 128], BF16)
        fvt = fsb("fvt", [128, 512])
        if l1:
            fgf = fsb("fgf", [128, 512])
            fgh = fsb("fgh", [128, 512], BF16)
            fgl = fsb("fgl", [128, 512], BF16)
            fg0 = fg1 = fq2 = fam = fex = fsc = fiv = None
        else:
            fg0 = fsb("fg0", [128, 512], BF16)
            fg1 = fsb("fg1", [128, 512], BF16)
            fq2 = fsb("fq2", [128, 512], BF16)
            fam = fsb("fam", [128, 16])
            fex = fsb("fex", [128, 16], I32)
            fsc = fsb("fsc", [128, 16])
            fiv = fsb("fiv", [128, 16])
            fgf = fgh = fgl = None
        with fes:
            nbb2 = dma(SY, dX, bb2[0:1, :], b2_hbm[0:1, :])
            for fs in range(nfs):
                SY.wait_ge(sP.h, sP.n)
                nfc = fsup // 128
                w1t = w1qT_hbm.rearrange("(dc fc) (p t) -> p dc fc t", fc=NF, t=128)
                for dc in range(DCH):
                    dma(SY, dwF, w1b[:, dc * fsup:(dc + 1) * fsup].rearrange("p (fc t) -> p fc t", t=128),
                        w1t[:, dc, fs * nfc:(fs + 1) * nfc, :])
                w2t = w2qT_hbm.rearrange("(fc dm) (p t) -> p fc dm t", dm=DCH, t=128)
                for fc in range(nfc):
                    dma(SY, dwF, w2b[:, fc * D:(fc + 1) * D].rearrange("p (dm t) -> p dm t", t=128),
                        w2t[:, fs * nfc + fc, :, :])
                dma(SY, dwF, bb1[0:1, 0:fsup], b1_hbm[0:1, fs * fsup:(fs + 1) * fsup])
                wsem = dwF.n
                for tt in range(ntt):
                    P.wait_ge(dwF.h, wsem)
                    P.wait_ge(dX.h, nbb2)
                    if fs == 0:
                        guard(P, "psBIG")
                        for dmc in range(NDM):
                            P.matmul(psBIG[:, dmc * 512:(dmc + 1) * 512], ones1[:],
                                     bb2[0:1, dmc * 512:(dmc + 1) * 512], start=True, stop=False)
                    def emit_h(fg):
                        psu, psn = ((psH0, "psH0"), (psH1, "psH1"))[fg % 2]
                        guard(P, psn)
                        P.matmul(psu[:], ones1[:], bb1[0:1, fg * 512:(fg + 1) * 512],
                                 start=True, stop=False)
                        for dc in range(DCH):
                            for xi in xis:
                                r = P.matmul(
                                    psu[:],
                                    xT_list[xi][:, (tt * DCH + dc) * 128:(tt * DCH + dc + 1) * 128],
                                    w1b[:, dc * fsup + fg * 512: dc * fsup + fg * 512 + 512],
                                    start=False, stop=(dc == DCH - 1 and xi == xis[-1]))
                        return sP.inc(r)

                    def emit_gchain(fg, npe):
                        psu, psn = ((psH0, "psH0"), (psH1, "psH1"))[fg % 2]
                        V.wait_ge(sP.h, npe)
                        if not l1:
                            guard(V, "fg0")
                            n = sV.inc(V.tensor_copy(fg0[:], psu[:]))
                            note(psn, sV, n)
                            A.wait_ge(sV.h, n)
                            na = sA.inc(A.activation(fg1[:], fg0[:], AF.Gelu_apprx_tanh))
                            note("fg0", sA, na)
                            V.wait_ge(sA.h, na)
                            guard(V, "fq2")
                            nq2 = emit_mx_quant(fg1[:], fq2[:], 512, fam, fex, fsc, fiv, fvt[:])
                            guard(V, "fqT")
                            emit_transposes(fq2[:], 0, fqT, 0, 4, True, nsrc=(sV, nq2))
                            note("fq2", sP, sP.n)
                        else:
                            guard(V, "fgf")
                            n = sV.inc(V.tensor_copy(fgf[:], psu[:]))
                            note(psn, sV, n)
                            A.wait_ge(sV.h, n)
                            na = sA.inc(A.activation(fgf[:], fgf[:], AF.Gelu))
                            V.wait_ge(sA.h, na)
                            sV.inc(V.tensor_copy(fgh[:], fgf[:]))
                            V.drain()
                            V.tensor_tensor(fvt[:], fgf[:], fgh[:], op=ALU.subtract)
                            V.drain()
                            n = sV.inc(V.tensor_copy(fgl[:], fvt[:]))
                            note("fgf", sV, n)
                            guard(V, "fqT")
                            emit_transposes(fgh[:], 0, fqT, 0, 4, True, nsrc=(sV, n))
                            emit_transposes(fgl[:], 0, fqT, 4, 4, True)
                        return sV.n

                    def emit_w2(fg, nq):
                        P.wait_ge(sV.h, nq)
                        for dmc in range(NDM):
                            for sub in range(4):
                                fc = fg * 4 + sub
                                for xi in xis:
                                    first = (fs > 0 and fg == 0 and sub == 0 and xi == 0)
                                    last = (fg == nfg - 1 and sub == 3 and xi == xis[-1])
                                    r = P.matmul(
                                        psBIG[:, dmc * 512:(dmc + 1) * 512],
                                        fqT[:, (xi * 4 + sub) * 128:(xi * 4 + sub + 1) * 128],
                                        w2b[:, fc * D + dmc * 512: fc * D + dmc * 512 + 512],
                                        start=first, stop=last)
                        n = sP.inc(r)
                        note("fqT", sP, n)
                        return n

                    npe = emit_h(0)
                    nq = emit_gchain(0, npe)
                    for fg in range(1, nfg):
                        npe = emit_h(fg)
                        emit_w2(fg - 1, nq)
                        nq = emit_gchain(fg, npe)
                    emit_w2(nfg - 1, nq)
                    V.wait_ge(sP.h, sP.n)
                    vsync()
                    if fs == 0:
                        n = sV.inc(V.tensor_copy(h2acc[:, tt * D:(tt + 1) * D], psBIG[:, 0:D]))
                    else:
                        n = sV.inc(V.tensor_tensor(h2acc[:, tt * D:(tt + 1) * D],
                                                   h2acc[:, tt * D:(tt + 1) * D],
                                                   psBIG[:, 0:D], op=ALU.add))
                    note("psBIG", sV, n)
            for tt in range(ntt):
                out_fn(tt, h2acc)
            barrier()

    emit_quant_spill(w1n0, w1qT0_h, DFF, D, False, None, "a")
    emit_quant_spill(w2n0, w2qT0_h, D, DFF, False, None, "b")

    with nc.sbuf_tensor("q1T", [128, SL * D], BF16) as q1T:
        with nc.sbuf_tensor("f2a", [128, D], F32) as f2a, \
             nc.sbuf_tensor("f2b", [128, D], F32) as f2b, \
             nc.sbuf_tensor("f2c", [128, D], F32) as f2c, \
             nc.sbuf_tensor("xbt", [128, D], BF16) as xbt, \
             nc.sbuf_tensor("q1b", [128, D], BF16) as q1b, \
             nc.sbuf_tensor("lnw", [128, D], F32) as lnw, \
             nc.sbuf_tensor("lnb", [128, D], F32) as lnb, \
             nc.sbuf_tensor("am2", [128, D // 32], F32) as am2, \
             nc.sbuf_tensor("ex2", [128, D // 32], I32) as ex2, \
             nc.sbuf_tensor("sc2", [128, D // 32], F32) as sc2, \
             nc.sbuf_tensor("iv2", [128, D // 32], F32) as iv2:
            dma(SY, dX, lnw[:], lnwh)
            nl = dma(SY, dX, lnb[:], lnbh)
            V.wait_ge(dX.h, nl)
            vack()
            for j in range(SL):
                SY.wait_ge(sV.h, sV.n)
                nl = dma(SY, dX, f2a[:], stream_h[j * 128:(j + 1) * 128, :])
                V.wait_ge(dX.h, nl)
                emit_ln(f2a[:], D, f2b[:], f2c[:], stat[:, 4:5], stat[:, 5:6])
                guard(V, "xbt")
                V.tensor_copy(xbt[:], f2c[:])
                V.drain()
                n = sV.inc(V.tensor_copy(f2c[:], xbt[:]))
                V.wait_ge(sV.h, n)
                G_.wait_ge(sV.h, n)
                nst = dma(G_, dG, xb_h[j * 128:(j + 1) * 128, :], xbt[:])
                note("xbt", dG, nst)
                emit_ln(f2c[:], D, f2a[:], f2b[:], stat[:, 6:7], stat[:, 7:8])
                V.tensor_tensor(f2b[:], f2b[:], lnw[:], op=ALU.mult)
                V.drain()
                V.tensor_tensor(f2b[:], f2b[:], lnb[:], op=ALU.add)
                V.drain()
                V.tensor_copy(q1b[:], f2b[:])
                V.drain()
                V.tensor_copy(f2b[:], q1b[:])
                V.drain()
                nq1 = emit_mx_quant(f2b[:], q1b[:], D, am2, ex2, sc2, iv2, f2a[:])
                emit_transposes(q1b[:], 0, q1T, j * DCH, DCH, True, nsrc=(sV, nq1))
            barrier()

        with nc.sbuf_tensor("o2a", [128, D], BF16) as o2a, \
             nc.sbuf_tensor("o2b", [128, D], BF16) as o2b, \
             nc.sbuf_tensor("o2c", [128, D], F32) as o2c:
            def out_l0(tt, h2acc):
                vsync()
                SY.wait_ge(sG.h, gack())
                nl = dma(SY, dX, o2a[:], xb_h[tt * 128:(tt + 1) * 128, :])
                nl2 = dma(SY, dX, o2c[:], stream_h[tt * 128:(tt + 1) * 128, :])
                V.wait_ge(dX.h, nl2)
                V.tensor_copy(o2b[:], h2acc[:, tt * D:(tt + 1) * D])
                V.drain()
                V.tensor_tensor(o2b[:], o2a[:], o2b[:], op=ALU.add)
                V.drain()
                n = sV.inc(V.tensor_tensor(o2c[:], o2c[:], o2b[:], op=ALU.add))
                G_.wait_ge(sV.h, n)
                dma(G_, dG, stream_h[tt * 128:(tt + 1) * 128, :], o2c[:])

            emit_ffn(w1qT0_h, w2qT0_h, b1h0, b2h0, [q1T], False, SL, out_l0, min(1024, DFF))

    if cfg.debug:
        G_.wait_ge(dG.h, dG.n)
        dma(G_, dG, dbg["dbg_s2"], stream_h)
        barrier()

    with nc.sbuf_tensor("xnT3", [128, SL * D], F32) as xnT3, \
         nc.sbuf_tensor("t3a", [128, D], F32) as t3a, \
         nc.sbuf_tensor("t3b", [128, D], F32) as t3b, \
         nc.sbuf_tensor("t3c", [128, D], F32) as t3c, \
         nc.sbuf_tensor("gam", [128, D], F32) as gam, \
         nc.sbuf_tensor("bet", [128, D], F32) as bet, \
         nc.sbuf_tensor("wb30", [128, DCH * 512], F32) as wb30, \
         nc.sbuf_tensor("wb31", [128, DCH * 512], F32) as wb31, \
         nc.sbuf_tensor("ev30", [128, 512], F32) as ev30, \
         nc.sbuf_tensor("ev31", [128, 512], F32) as ev31:
        dma(SY, dX, gam[:], gamh)
        nl = dma(SY, dX, bet[:], beth)
        V.wait_ge(dX.h, nl)
        vack()
        for j in range(SL):
            SY.wait_ge(sV.h, sV.n)
            nl = dma(SY, dX, t3a[:], stream_h[j * 128:(j + 1) * 128, :])
            V.wait_ge(dX.h, nl)
            emit_ln(t3a[:], D, t3b[:], t3c[:], mu3[:, j:j + 1], rstd3[:, j:j + 1])
            emit_ln(t3c[:], D, t3a[:], t3b[:], stat[:, 4:5], stat[:, 5:6])
            V.tensor_tensor(t3b[:], t3b[:], gam[:], op=ALU.mult)
            V.drain()
            n = sV.inc(V.tensor_tensor(t3b[:], t3b[:], bet[:], op=ALU.add))
            V.wait_ge(sV.h, n)
            emit_transposes(t3b[:], 0, xnT3, j * DCH, DCH, False, nsrc=(sV, n))
        wl = 0
        for sec, base in (('q', 0), ('k', KEY), ('v', 2 * KEY)):
            for oc in range(KEY // 512):
                wb, wbn = ((wb30, "wb30"), (wb31, "wb31"))[wl % 2]
                dwp = (dwA, dwB)[wl % 2]; wl += 1
                guard(SY, wbn)
                src = qkvw.rearrange("(dc p) n -> p dc n", p=128)[:, :, base + oc * 512: base + (oc + 1) * 512]
                wsem = dma(SY, dwp, wb[:].rearrange("p (dc n) -> p dc n", n=512), src)
                for j in range(SL):
                    psu, psn = ((psH0, "psH0"), (psH1, "psH1"))[j % 2]
                    P.wait_ge(dwp.h, wsem)
                    guard(P, psn)
                    for dc in range(DCH):
                        r = P.matmul(psu[:], xnT3[:, (j * DCH + dc) * 128:(j * DCH + dc + 1) * 128],
                                     wb[:, dc * 512:(dc + 1) * 512],
                                     start=(dc == 0), stop=(dc == DCH - 1))
                    n = sP.inc(r)
                    note(wbn, sP, n)
                    V.wait_ge(sP.h, n)
                    guard(V, "ev30")
                    if sec == 'q':
                        n = sV.inc(V.tensor_single_scalar(ev30[:, 0:512], psu[:], SC1, op=ALU.mult))
                    else:
                        n = sV.inc(V.tensor_copy(ev30[:, 0:512], psu[:]))
                    note(psn, sV, n)
                    if sec == 'v':
                        G_.wait_ge(sV.h, n)
                        nst = dma(G_, dSv, v1o[j * 128:(j + 1) * 128, oc * 512:(oc + 1) * 512], ev30[:, 0:512])
                        note("ev30", dSv, nst)
                    else:
                        V.wait_ge(sV.h, n)
                        guard(V, "ev31")
                        emit_transposes(ev30[:, 0:512], 0, ev31, 0, 4, False, nsrc=(sV, n))
                        note("ev30", sP, sP.n)
                        G_.wait_ge(sV.h, sV.n)
                        nst = spill_T(ev31[:, 0:512], 4, qT_h if sec == 'q' else kT1o, oc * 4, j * 128)
                        note("ev31", dG, nst)
        barrier()
        sCC.inc(G_.collective_compute(
            "AllGather", ALU.bypass, replica_groups=[[0, 1], [2, 3], [4, 5], [6, 7]],
            ins=[kT1o.tensor.ap().opt()], outs=[kT1g.tensor.ap().opt()]))
        sCC.inc(G_.collective_compute(
            "AllGather", ALU.bypass, replica_groups=[[0, 1], [2, 3], [4, 5], [6, 7]],
            ins=[v1o.tensor.ap().opt()], outs=[v1g.tensor.ap().opt()]))
        G_.wait_ge(sCC.h, sCC.n)
        barrier()

    with nc.sbuf_tensor("kb3", [128, DCH * 512], F32) as kb3, \
         nc.sbuf_tensor("vb3", [128, 4 * KEY], F32) as vb3, \
         nc.sbuf_tensor("qT3", [128, D], F32) as qT3, \
         nc.sbuf_tensor("Pb3", [128, 2048], F32) as Pb3, \
         nc.sbuf_tensor("PT3", [128, 2048], F32) as PT3, \
         nc.sbuf_tensor("Ot3", [128, D], F32) as Ot3, \
         nc.sbuf_tensor("oTe3", [128, 512], F32) as oTe3:
        for j in range(SL):
            nkt = 2 * j + 2
            nk = nkt * 128
            SY.wait_ge(sP.h, sP.n)
            qsrc = qT_h.rearrange("(dc p) t -> p dc t", p=128)[:, 0:DCH, j * 128:(j + 1) * 128]
            qsem = dma(SY, dX, qT3[:].rearrange("p (dc t) -> p dc t", t=128), qsrc)
            for c in range((nk + 511) // 512):
                tiles = range(c * 4, min(c * 4 + 4, nkt))
                SY.wait_ge(sP.h, sP.n)
                for kt in tiles:
                    r_ = owner_rank(kt); sl = owner_slot(kt)
                    ksrc = kT1g.rearrange("(dc p) t -> p dc t", p=128)[
                        :, r_ * DCH:(r_ + 1) * DCH, sl * 128:(sl + 1) * 128]
                    kdst = kb3[:].rearrange("p (dc n) -> p dc n", n=512)[:, :, (kt - c * 4) * 128:(kt - c * 4 + 1) * 128]
                    dma(SY, dK, kdst, ksrc)
                w = min(512, nk - c * 512)
                P.wait_ge(dK.h, dK.n)
                P.wait_ge(dX.h, qsem)
                P.wait_ge(sA.h, sA.n)
                P.wait_ge(sV.h, sV.n)
                for dc in range(DCH):
                    r = P.matmul(psBIG[:, c * 512:c * 512 + w],
                                 qT3[:, dc * 128:(dc + 1) * 128],
                                 kb3[:, dc * 512:dc * 512 + w],
                                 start=(dc == 0), stop=(dc == DCH - 1))
                sP.inc(r)
            V.wait_ge(sP.h, sP.n)
            V.tensor_tensor(psBIG[:, nk - 256:nk], psBIG[:, nk - 256:nk],
                            masks[:, j * 256:(j + 1) * 256], op=ALU.add)
            V.drain()
            n = sV.inc(V.reduce_max(negm[:], psBIG[:, 0:nk], axis=AX.X, negate=True))
            A.wait_ge(sV.h, n)
            na = sA.inc(A.activation(Pb3[:, 0:nk], psBIG[:, 0:nk], AF.Exp,
                                     bias=negm[:], accum_out=ssum[:]))
            V.wait_ge(sA.h, na)
            n = sV.inc(V.reciprocal(rsum[:], ssum[:]))
            emit_transposes(Pb3[:], 0, PT3, 0, nkt, False, nsrc=(sA, na))
            for c in range((nk + 511) // 512):
                tiles = list(range(c * 4, min(c * 4 + 4, nkt)))
                SY.wait_ge(sP.h, sP.n)
                for i, kt in enumerate(tiles):
                    r_ = owner_rank(kt); sl = owner_slot(kt)
                    vsrc = v1g[(r_ * TOK + sl * 128):(r_ * TOK + sl * 128 + 128), :]
                    dma(SY, dK, vb3[:, i * KEY:(i + 1) * KEY], vsrc)
                P.wait_ge(dK.h, dK.n)
                if c == 0:
                    P.wait_ge(sA.h, sA.n)
                    P.wait_ge(sV.h, sV.n)
                for dmc in range(NDM):
                    for i, kt in enumerate(tiles):
                        r = P.matmul(psBIG[:, dmc * 512:(dmc + 1) * 512],
                                     PT3[:, kt * 128:(kt + 1) * 128],
                                     vb3[:, i * KEY + dmc * 512: i * KEY + dmc * 512 + 512],
                                     start=(kt == 0), stop=(kt == nkt - 1))
                sP.inc(r)
            n = sP.n
            V.wait_ge(sP.h, n)
            for dmc in range(NDM):
                sV.inc(V.tensor_scalar(Ot3[:, dmc * 512:(dmc + 1) * 512],
                                       psBIG[:, dmc * 512:(dmc + 1) * 512],
                                       rsum[:], None, op0=ALU.mult))
            done = 0
            while done < DCH:
                nb = min(4, DCH - done)
                P.wait_ge(sV.h, sV.n)
                for c2 in range(nb):
                    r = P.transpose(psA_[:, c2 * 128:(c2 + 1) * 128],
                                    Ot3[:, (done + c2) * 128:(done + c2 + 1) * 128], ident[:])
                n = sP.inc(r)
                V.wait_ge(sP.h, n)
                guard(V, "oTe3")
                n = sV.inc(V.tensor_copy(oTe3[:, 0:nb * 128], psA_[:, 0:nb * 128]))
                G_.wait_ge(sV.h, n)
                nst = spill_T(oTe3[:, 0:nb * 128], nb, oT_h, done, j * 128)
                note("oTe3", dG, nst)
                done += nb
        barrier()

    def resid_l1(j, oc, psu, npe, rt0, rt1, rt2):
        G_.wait_ge(sV.h, sV.n)
        nl = dma(G_, dGL, rt0[:], stream_h[j * 128:(j + 1) * 128, oc * 512:(oc + 1) * 512])
        V.wait_ge(dGL.h, nl)
        V.wait_ge(sP.h, npe)
        V.tensor_scalar(rt2[:], rt0[:], mu3[:, j:j + 1], None, op0=ALU.subtract)
        V.drain()
        V.tensor_scalar(rt2[:], rt2[:], rstd3[:, j:j + 1], None, op0=ALU.mult)
        V.drain()
        guard(V, "rt1")
        nps = sV.inc(V.tensor_tensor(rt1[:], rt2[:], psu[:], op=ALU.add))
        note("psH0" if psu is psH0 else "psH1", sV, nps)
        V.wait_ge(sV.h, nps)
        n = sV.inc(V.tensor_tensor(rt1[:], rt1[:], rt0[:], op=ALU.add))
        G_.wait_ge(sV.h, n)
        nst = dma(G_, dG, stream_h[j * 128:(j + 1) * 128, oc * 512:(oc + 1) * 512], rt1[:])
        note("rt1", dG, nst)

    emit_out_proj(oprw, DCH, resid_l1)

    if cfg.debug:
        G_.wait_ge(dG.h, dG.n)
        dma(G_, dG, dbg["dbg_s3"], stream_h)
        barrier()

    def emit_absmean(w_hbm, nrows, ncols, inv_out, tag):
        with nc.sbuf_tensor("aw0" + tag, [128, ncols], F32) as aw0, \
             nc.sbuf_tensor("aw1" + tag, [128, ncols], F32) as aw1:
            for rc in range(nrows // 128):
                wb, awn = ((aw0, "aw0"), (aw1, "aw1"))[rc % 2]
                dwp = (dwA, dwB)[rc % 2]
                guard(SY, awn)
                nl = dma(SY, dwp, wb[:], w_hbm[rc * 128:(rc + 1) * 128, :])
                V.wait_ge(dwp.h, nl)
                V.reduce_sum(stat[:, 9:10], wb[:], axis=AX.X, apply_absolute_value=True)
                V.drain()
                if rc == 0:
                    n = sV.inc(V.tensor_single_scalar(stat[:, 8:9], stat[:, 9:10], 1.0, op=ALU.mult))
                else:
                    n = sV.inc(V.tensor_tensor(stat[:, 8:9], stat[:, 8:9], stat[:, 9:10], op=ALU.add))
                note(awn, sV, n)
                V.wait_ge(sV.h, n)
            P.wait_ge(sV.h, sV.n)
            n = sP.inc(P.matmul(psO[:, 0:1], onesq[:], stat[:, 8:9], start=True, stop=True))
            V.wait_ge(sP.h, n)
            V.tensor_single_scalar(stat[:, 10:11], psO[:, 0:1], 1.0 / (nrows * ncols), op=ALU.mult)
            V.drain()
            V.tensor_single_scalar(stat[:, 10:11], stat[:, 10:11], 1e-8, op=ALU.max)
            V.drain()
            n = sV.inc(V.reciprocal(inv_out, stat[:, 10:11]))
            barrier()

    emit_absmean(w1n1, DFF, D, inv0[:], "m1")
    emit_absmean(w2n1, D, DFF, inv1[:], "m2")
    emit_quant_spill(w1n1, w1qT1_h, DFF, D, True, inv0[:], "c")
    emit_quant_spill(w2n1, w2qT1_h, D, DFF, True, inv1[:], "d")

    HT = SL // 2 if SL >= 2 else 1
    for half in range((SL + HT - 1) // HT):
        tts = list(range(half * HT, min((half + 1) * HT, SL)))
        with nc.sbuf_tensor("xhiT%d" % half, [128, len(tts) * D], BF16) as xhiT, \
             nc.sbuf_tensor("xloT%d" % half, [128, len(tts) * D], BF16) as xloT:
            with nc.sbuf_tensor("f4a%d" % half, [128, D], F32) as f4a, \
                 nc.sbuf_tensor("f4b%d" % half, [128, D], F32) as f4b, \
                 nc.sbuf_tensor("f4c%d" % half, [128, D], F32) as f4c, \
                 nc.sbuf_tensor("xhib%d" % half, [128, D], BF16) as xhib, \
                 nc.sbuf_tensor("xlob%d" % half, [128, D], BF16) as xlob:
                for i, tt in enumerate(tts):
                    SY.wait_ge(sV.h, sV.n)
                    nl = dma(SY, dX, f4a[:], stream_h[tt * 128:(tt + 1) * 128, :])
                    V.wait_ge(dX.h, nl)
                    emit_ln(f4a[:], D, f4b[:], f4c[:], stat[:, 4:5], stat[:, 5:6])
                    V.tensor_copy(xhib[:], f4c[:])
                    V.drain()
                    V.tensor_tensor(f4b[:], f4c[:], xhib[:], op=ALU.subtract)
                    V.drain()
                    n = sV.inc(V.tensor_copy(xlob[:], f4b[:]))
                    V.wait_ge(sV.h, n)
                    emit_transposes(xhib[:], 0, xhiT, i * DCH, DCH, True, nsrc=(sV, n))
                    emit_transposes(xlob[:], 0, xloT, i * DCH, DCH, True)
                barrier()

            with nc.sbuf_tensor("o4a%d" % half, [128, D], F32) as o4a:
                def out_l1(ttl, h2acc, _tts=tts):
                    tt = _tts[ttl]
                    vsync()
                    SY.wait_ge(sG.h, gack())
                    nl = dma(SY, dX, o4a[:], stream_h[tt * 128:(tt + 1) * 128, :])
                    V.wait_ge(dX.h, nl)
                    n = sV.inc(V.tensor_tensor(o4a[:], o4a[:], h2acc[:, ttl * D:(ttl + 1) * D], op=ALU.add))
                    G_.wait_ge(sV.h, n)
                    dma(G_, dG, y_own[tt * 128:(tt + 1) * 128, :], o4a[:])

                emit_ffn(w1qT1_h, w2qT1_h, b1h1, b2h1, [xhiT, xloT], True, len(tts), out_l1, min(1024, DFF))

    barrier()
    es.close()
    return nc



def _rope_tables(cfg, positions, scale):
    half = 64
    j64 = (np.arange(half, dtype=np.float32) / np.float32(half)).astype(np.float32)
    inv = (np.float32(1.0) / (np.float32(10000.0) ** j64)).astype(np.float32)
    ang = (positions[:, None].astype(np.float32) * inv[None, :]).astype(np.float32)
    cos = np.cos(ang.astype(np.float64)).astype(np.float32) * np.float32(scale)
    sin = np.sin(ang.astype(np.float64)).astype(np.float32) * np.float32(scale)
    return cos.astype(np.float32), sin.astype(np.float32)


def _host_inputs(cfg, I, rank, seq):
    T, D, DFF, DK0, SL = cfg.T, cfg.D, cfg.DFF, cfg.DK0, cfg.SL
    rows = np.concatenate([np.arange(g_of(j, rank) * 128, g_of(j, rank) * 128 + 128)
                           for j in range(SL)])
    x2 = np.asarray(I["L0_attn_x"]).reshape(cfg.B * T, D)
    m = {}
    m["x_own"] = np.ascontiguousarray(x2[seq * T + rows])
    m["wqT"] = np.ascontiguousarray(np.asarray(I["L0_attn_Wq"]).T)
    m["wkT"] = np.ascontiguousarray(np.asarray(I["L0_attn_Wk"]).T)
    m["wvT"] = np.ascontiguousarray(np.asarray(I["L0_attn_Wv"]).T)
    m["wpT"] = np.ascontiguousarray(np.asarray(I["L0_attn_Wp"]).T)
    m["w1n0"] = np.ascontiguousarray(np.asarray(I["L0_ffn_W1"]))
    m["w2n0"] = np.ascontiguousarray(np.asarray(I["L0_ffn_W2"]))
    m["b1h0"] = np.ascontiguousarray(np.asarray(I["L0_ffn_b1"]).reshape(1, DFF))
    m["b2h0"] = np.ascontiguousarray(np.asarray(I["L0_ffn_b2"]).reshape(1, D))
    m["lnwh"] = np.ascontiguousarray(np.broadcast_to(np.asarray(I["L0_ffn_ln_w"]), (128, D)))
    m["lnbh"] = np.ascontiguousarray(np.broadcast_to(np.asarray(I["L0_ffn_ln_b"]), (128, D)))
    m["gamh"] = np.ascontiguousarray(np.broadcast_to(np.asarray(I["L1_attn_gamma"]), (128, D)))
    m["beth"] = np.ascontiguousarray(np.broadcast_to(np.asarray(I["L1_attn_beta"]), (128, D)))
    m["qkvw"] = np.ascontiguousarray(np.asarray(I["L1_attn_qkv"]))
    m["oprw"] = np.ascontiguousarray(np.asarray(I["L1_attn_o_proj"]))
    m["w1n1"] = np.ascontiguousarray(np.asarray(I["L1_ffn_w1"]))
    m["w2n1"] = np.ascontiguousarray(np.asarray(I["L1_ffn_w2"]))
    m["b1h1"] = np.ascontiguousarray(np.asarray(I["L1_ffn_b1"]).reshape(1, DFF))
    m["b2h1"] = np.ascontiguousarray(np.asarray(I["L1_ffn_b2"]).reshape(1, D))
    qs = 1.0 / np.sqrt(np.float32(cfg.DH))
    cq = np.empty((128, SL * 64), np.float32); sq = np.empty((128, SL * 64), np.float32)
    ck = np.empty((128, SL * 64), np.float32); sk = np.empty((128, SL * 64), np.float32)
    for j in range(SL):
        pos = np.arange(g_of(j, rank) * 128, g_of(j, rank) * 128 + 128)
        c1, s1 = _rope_tables(cfg, pos, qs)
        c2, s2 = _rope_tables(cfg, pos, 1.0)
        cq[:, j * 64:(j + 1) * 64] = c1; sq[:, j * 64:(j + 1) * 64] = s1
        ck[:, j * 64:(j + 1) * 64] = c2; sk[:, j * 64:(j + 1) * 64] = s2
    m["cosqh"] = cq; m["sinqh"] = sq; m["coskh"] = ck; m["sinkh"] = sk
    tri = np.triu(np.full((128, 128), NEG_BIG, np.float32), 1)
    full = np.full((128, 128), NEG_BIG, np.float32)
    zero = np.zeros((128, 128), np.float32)
    mk = np.empty((128, SL * 256), np.float32)
    for j in range(SL):
        g = g_of(j, rank)
        if g == 2 * j:
            mk[:, j * 256: j * 256 + 128] = tri
            mk[:, j * 256 + 128: j * 256 + 256] = full
        else:
            mk[:, j * 256: j * 256 + 128] = zero
            mk[:, j * 256 + 128: j * 256 + 256] = tri
    m["masksh"] = mk
    m["identh"] = np.eye(128, dtype=np.float32)
    return m, rows


def kernel(**inputs):
    cfg = Cfg()
    nc = build(cfg)
    in_maps = []
    row_list = []
    for core in range(8):
        seq, rank = core // 2, core % 2
        m, rows = _host_inputs(cfg, inputs, rank, seq)
        in_maps.append(m)
        row_list.append((seq, rows))
    res = run_bass_kernel_spmd(nc, in_maps, core_ids=list(range(8)))
    out = np.empty((cfg.B, cfg.T, cfg.D), np.float32)
    for core in range(8):
        seq, rows = row_list[core]
        out.reshape(cfg.B * cfg.T, cfg.D)[seq * cfg.T + rows] = res.results[core]["y_own"]
    return out


# version 04
# speedup vs baseline: 1.4331x; 1.1961x over previous
import numpy as np
import ml_dtypes
from contextlib import ExitStack

import concourse.bass as bass
import concourse.mybir as mybir
from concourse.bass_utils import run_bass_kernel_spmd

F32 = mybir.dt.float32
I32 = mybir.dt.int32
BF16 = mybir.dt.bfloat16
AF = mybir.ActivationFunctionType
ALU = mybir.AluOpType
AX = mybir.AxisListType

MAGIC = float(np.float32(12582912.0))
NEG_BIG = float(np.finfo(np.float32).min)
AMAX_FLOOR = float(np.float32(2.0 ** -100))
EPS_LN = 1e-5


class Cfg:
    def __init__(self, T=2048, D=2048, DFF=8192, DK0=512, debug=False):
        self.T, self.D, self.DFF, self.DK0 = T, D, DFF, DK0
        self.B = 4
        self.DH = 128
        self.H0 = D // self.DH
        self.HK0 = DK0 // self.DH
        self.KEY = D
        self.G = T // 128
        self.SL = self.G // 2
        self.TOK = self.SL * 128
        self.DCH = D // 128
        self.debug = debug


def g_of(j, r):
    return 2 * j + ((j + r) % 2)


def owner_rank(kt):
    return (kt // 2 + kt % 2) % 2


def owner_slot(kt):
    return kt // 2


def build(cfg):
    nc = bass.Bass("TRN2", target_bir_lowering=False)
    T, D, DFF, DK0 = cfg.T, cfg.D, cfg.DFF, cfg.DK0
    SL, TOK, DCH = cfg.SL, cfg.TOK, cfg.DCH
    H0, HK0, KEY = cfg.H0, cfg.HK0, cfg.KEY
    GRP = H0 // HK0
    FSUP = min(1024, DFF)
    NDM = D // 512
    SC1 = float(np.float32(1.0 / np.sqrt(np.float32(KEY))))

    def din(name, shape, dt=F32):
        return nc.dram_tensor(name, shape, dt, kind="ExternalInput").ap()

    x_own = din("x_own", [TOK, D])
    wqT = din("wqT", [D, D]); wkT = din("wkT", [D, DK0]); wvT = din("wvT", [D, DK0]); wpT = din("wpT", [D, D])
    w1n0 = din("w1n0", [DFF, D]); w2n0 = din("w2n0", [D, DFF])
    b1h0 = din("b1h0", [1, DFF]); b2h0 = din("b2h0", [1, D])
    lnwh = din("lnwh", [128, D]); lnbh = din("lnbh", [128, D])
    gamh = din("gamh", [128, D]); beth = din("beth", [128, D])
    qkvw = din("qkvw", [D, 3 * KEY]); oprw = din("oprw", [KEY, D])
    w1n1 = din("w1n1", [DFF, D]); w2n1 = din("w2n1", [D, DFF])
    b1h1 = din("b1h1", [1, DFF]); b2h1 = din("b2h1", [1, D])
    cosqh = din("cosqh", [128, SL * 64]); sinqh = din("sinqh", [128, SL * 64])
    coskh = din("coskh", [128, SL * 64]); sinkh = din("sinkh", [128, SL * 64])
    masksh = din("masksh", [128, SL * 256])
    w1s0 = din("w1s0", [DFF // 8, D]); w2s0 = din("w2s0", [D, DFF // 8])
    w1s1 = din("w1s1", [DFF // 8, D]); w2s1 = din("w2s1", [D, DFF // 8])
    identh = din("identh", [128, 128])
    y_own = nc.dram_tensor("y_own", [TOK, D], F32, kind="ExternalOutput").ap()
    dbg = {}
    if cfg.debug:
        for nm in ("dbg_s1", "dbg_s2", "dbg_s3"):
            dbg[nm] = nc.dram_tensor(nm, [TOK, D], F32, kind="ExternalOutput").ap()

    def dint(name, shape, dt=F32):
        return nc.dram_tensor(name, shape, dt).ap()

    stream_h = dint("stream_h", [TOK, D])
    xb_h = dint("xb_h", [TOK, D], BF16)
    qT_h = dint("qT_h", [max(D, KEY), TOK])
    oT_h = dint("oT_h", [D, TOK])
    kT0o = dint("kT0o", [DK0, TOK]); kT0g = dint("kT0g", [2 * DK0, TOK])
    v0o = dint("v0o", [TOK, DK0]); v0g = dint("v0g", [2 * TOK, DK0])
    kT1o = dint("kT1o", [KEY, TOK]); kT1g = dint("kT1g", [2 * KEY, TOK])
    v1o = dint("v1o", [TOK, KEY]); v1g = dint("v1g", [2 * TOK, KEY])
    NF = DFF // 128
    NF8 = NF // 8
    w1qT0_o = dint("w1qT0_o", [DCH * NF8, 128 * 128], BF16)
    w2qT0_o = dint("w2qT0_o", [NF8 * DCH, 128 * 128], BF16)
    w1qT1_o = dint("w1qT1_o", [DCH * NF8, 128 * 128], BF16)
    w2qT1_o = dint("w2qT1_o", [NF8 * DCH, 128 * 128], BF16)
    w1qT0_h = dint("w1qT0_h", [8 * DCH * NF8, 128 * 128], BF16)
    w2qT0_h = dint("w2qT0_h", [8 * NF8 * DCH, 128 * 128], BF16)
    w1qT1_h = dint("w1qT1_h", [8 * DCH * NF8, 128 * 128], BF16)
    w2qT1_h = dint("w2qT1_h", [8 * NF8 * DCH, 128 * 128], BF16)
    am_own = dint("am_own", [128, 2])
    am_red = dint("am_red", [128, 2])

    es = ExitStack()
    sbuf = lambda name, shape, dt=F32: es.enter_context(nc.sbuf_tensor(name, shape, dt))

    class Sem:
        __slots__ = ("h", "n")
        def __init__(self, name):
            self.h = es.enter_context(nc.semaphore(name))
            self.n = 0
        def inc(self, instr, k=1):
            instr.then_inc(self.h, k)
            self.n += k
            return self.n

    dW = Sem("dW"); dX = Sem("dX"); dK = Sem("dK"); dG = Sem("dG")
    dwA = Sem("dwA"); dwB = Sem("dwB"); dwF = Sem("dwF"); dGL = Sem("dGL")
    dSv = Sem("dSv"); dQe0 = Sem("dQe0"); dQe1 = Sem("dQe1")
    sV = Sem("sV"); sA = Sem("sA"); sP = Sem("sP"); sCC = Sem("sCC")
    sB = Sem("sB"); sG = Sem("sG")
    CSEMS = (sV, sA, sP)

    V, A, P, G_, SY = nc.vector, nc.scalar, nc.tensor, nc.gpsimd, nc.sync

    ident = sbuf("ident", [128, 128])
    identb = sbuf("identb", [128, 128], BF16)
    ones1 = sbuf("ones1", [1, 128])
    onesq = sbuf("onesq", [128, 128])
    masks = sbuf("masks", [128, SL * 256])
    stat = sbuf("stat", [128, 16])
    mu3 = sbuf("mu3", [128, SL]); rstd3 = sbuf("rstd3", [128, SL])
    inv0 = sbuf("inv0", [128, 1]); inv1 = sbuf("inv1", [128, 1])
    negm = sbuf("negm", [128, 1]); ssum = sbuf("ssum", [128, 1]); rsum = sbuf("rsum", [128, 1])
    bb1 = sbuf("bb1", [1, FSUP])
    bb2 = sbuf("bb2", [1, D])

    psBIG = es.enter_context(nc.psum_tensor("psBIG", [128, 2048], F32))
    psH0 = es.enter_context(nc.psum_tensor("psH0", [128, 512], F32))
    psH1 = es.enter_context(nc.psum_tensor("psH1", [128, 512], F32))
    psA_ = es.enter_context(nc.psum_tensor("psA_", [128, 512], F32))
    psO = es.enter_context(nc.psum_tensor("psO", [128, 512], F32))

    def barrier():
        SY.wait_ge(dW.h, dW.n); SY.wait_ge(dX.h, dX.n); SY.wait_ge(dK.h, dK.n)
        SY.wait_ge(dwA.h, dwA.n); SY.wait_ge(dwB.h, dwB.n); SY.wait_ge(dwF.h, dwF.n)
        SY.sem_inc(sB.h, 1); sB.n += 1
        G_.wait_ge(dG.h, dG.n)
        G_.wait_ge(dGL.h, dGL.n)
        G_.wait_ge(dSv.h, dSv.n)
        G_.wait_ge(dQe0.h, dQe0.n); G_.wait_ge(dQe1.h, dQe1.n)
        G_.sem_inc(sB.h, 1); sB.n += 1
        for eng in (V, A, P, G_, SY):
            eng.wait_ge(sB.h, sB.n)
            for s in CSEMS:
                eng.wait_ge(s.h, s.n)
        pend.clear()

    pend = {}

    def note(name, sem, val):
        d = pend.setdefault(name, {})
        d[id(sem)] = (sem, max(val, d.get(id(sem), (None, -1))[1]))

    def guard(eng, name):
        for sem, v in pend.pop(name, {}).values():
            eng.wait_ge(sem.h, v)

    def vsync():
        V.wait_ge(sV.h, sV.n)

    def vack():
        return sV.inc(V.tensor_copy(stat[:, 15:16], stat[:, 14:15]))

    def gack():
        G_.wait_ge(dG.h, dG.n)
        G_.sem_inc(sG.h, 1); sG.n += 1
        return sG.n

    def dma(q, qsem, out, in_):
        q.dma_start(out=out, in_=in_).then_inc(qsem.h, 16)
        qsem.n += 16
        return qsem.n

    def emit_ln(xt, ncols, dump, xn_out, mu_ap, rstd_ap):
        V.reduce_sum(stat[:, 0:1], xt, axis=AX.X)
        V.drain()
        V.tensor_single_scalar(mu_ap, stat[:, 0:1], 1.0 / ncols, op=ALU.mult)
        V.drain()
        n = sV.inc(V.tensor_scalar(xn_out, xt, mu_ap, None, op0=ALU.subtract))
        A.wait_ge(sV.h, n)
        n = sA.inc(A.activation(dump, xn_out, AF.Square, accum_out=stat[:, 1:2]))
        V.wait_ge(sA.h, n)
        n = sV.inc(V.tensor_scalar(stat[:, 2:3], stat[:, 1:2], 1.0 / ncols, EPS_LN,
                                   op0=ALU.mult, op1=ALU.add))
        A.wait_ge(sV.h, n)
        n = sA.inc(A.activation(stat[:, 3:4], stat[:, 2:3], AF.Sqrt))
        V.wait_ge(sA.h, n)
        V.reciprocal(rstd_ap, stat[:, 3:4])
        V.drain()
        n = sV.inc(V.tensor_scalar(xn_out, xn_out, rstd_ap, None, op0=ALU.mult))
        V.wait_ge(sV.h, n)
        return n

    def emit_transposes(src, soff, dst, doff, k, bf16, nsrc=None):
        idt = identb if bf16 else ident
        pa = psA_[:].bitcast(BF16) if bf16 else psA_[:]
        if nsrc is not None:
            P.wait_ge(nsrc[0].h, nsrc[1])
        done = 0
        while done < k:
            nb = min(4, k - done)
            guard(P, "psA")
            for c in range(nb):
                n = sP.inc(P.transpose(pa[:, c * 128:(c + 1) * 128],
                                       src[:, (soff + done + c) * 128:(soff + done + c + 1) * 128], idt[:]))
            V.wait_ge(sP.h, n)
            nv = sV.inc(V.tensor_copy(dst[:, (doff + done) * 128:(doff + done + nb) * 128],
                                      pa[:, 0:nb * 128]))
            note("psA", sV, nv)
            done += nb

    def spill_T(src, k, dsth, row0_chunk, col0, qsem=None):
        dst = dsth.rearrange("(c p) t -> p c t", p=128)[:, row0_chunk:row0_chunk + k, col0:col0 + 128]
        return dma(G_, qsem if qsem is not None else dG, dst, src.rearrange("p (c t) -> p c t", t=128))

    def emit_mx_quant(xin, xout_bf, ncols, amx, expb, scl, invs, vtmp):
        nb = ncols // 32
        x3 = xin.rearrange("p (b k) -> p b k", k=32)
        V.tensor_reduce(amx[:, 0:nb], x3, axis=AX.X, op=ALU.max, apply_absolute_value=True)
        V.drain()
        V.tensor_single_scalar(amx[:, 0:nb], amx[:, 0:nb], AMAX_FLOOR, op=ALU.max)
        V.drain()
        V.tensor_single_scalar(expb[:, 0:nb], amx[:, 0:nb].bitcast(I32), 0x7F800000, op=ALU.bitwise_and)
        V.drain()
        V.tensor_single_scalar(scl[:, 0:nb], expb[:, 0:nb].bitcast(F32),
                               float(np.float32(2.0 ** -6)), op=ALU.mult)
        V.drain()
        V.reciprocal(invs[:, 0:nb], scl[:, 0:nb])
        V.drain()
        inv3 = invs[:, 0:nb].unsqueeze(2).broadcast_to([128, nb, 32])
        v3 = vtmp.rearrange("p (b k) -> p b k", k=32)
        V.tensor_tensor(v3, x3, inv3, op=ALU.mult)
        V.drain()
        V.tensor_scalar(vtmp, vtmp, MAGIC, float(np.float32(MAGIC + 127.0)), op0=ALU.add, op1=ALU.min)
        V.drain()
        V.tensor_scalar(vtmp, vtmp, float(np.float32(MAGIC - 128.0)), -MAGIC, op0=ALU.max, op1=ALU.add)
        V.drain()
        sc3 = scl[:, 0:nb].unsqueeze(2).broadcast_to([128, nb, 32])
        o3 = xout_bf.rearrange("p (b k) -> p b k", k=32)
        n = sV.inc(V.tensor_tensor(o3, v3, sc3, op=ALU.mult))
        V.wait_ge(sV.h, n)
        return n

    def emit_quant_spill(w_hbm, wqT_hbm, nrows, ncols, ternary, inv_ap, tag):
        nrc = nrows // 128
        with nc.sbuf_tensor("qw0" + tag, [128, ncols], F32) as qw0, \
             nc.sbuf_tensor("qw1" + tag, [128, ncols], F32) as qw1, \
             nc.sbuf_tensor("qv" + tag, [128, ncols], F32) as qv, \
             nc.sbuf_tensor("qb" + tag, [128, ncols], BF16) as qb, \
             nc.sbuf_tensor("qe0" + tag, [128, 512], BF16) as qe0, \
             nc.sbuf_tensor("qe1" + tag, [128, 512], BF16) as qe1, \
             nc.sbuf_tensor("qam" + tag, [128, ncols // 32], F32) as qam, \
             nc.sbuf_tensor("qex" + tag, [128, ncols // 32], I32) as qex, \
             nc.sbuf_tensor("qsc" + tag, [128, ncols // 32], F32) as qsc, \
             nc.sbuf_tensor("qiv" + tag, [128, ncols // 32], F32) as qiv:
            wq4 = wqT_hbm.rearrange("(cc rr) (p t) -> p cc rr t", rr=nrc, t=128)
            qei = 0
            for rc in range(nrc):
                wbq, wbqn = ((qw0, "qw0"), (qw1, "qw1"))[rc % 2]
                dwp = (dwA, dwB)[rc % 2]
                guard(SY, wbqn)
                nl = dma(SY, dwp, wbq[:], w_hbm[rc * 128:(rc + 1) * 128, :])
                V.wait_ge(dwp.h, nl)
                guard(V, "qb")
                if ternary:
                    V.tensor_scalar(qv[:], wbq[:], inv_ap, None, op0=ALU.mult)
                    V.drain()
                    V.tensor_scalar(qv[:], qv[:], MAGIC, float(np.float32(MAGIC + 1.0)),
                                    op0=ALU.add, op1=ALU.min)
                    V.drain()
                    nq = sV.inc(V.tensor_scalar(qb[:], qv[:], float(np.float32(MAGIC - 1.0)), -MAGIC,
                                                 op0=ALU.max, op1=ALU.add))
                    V.wait_ge(sV.h, nq)
                else:
                    emit_mx_quant(wbq[:], qb[:], ncols, qam, qex, qsc, qiv, qv[:])
                note(wbqn, sV, sV.n)
                P.wait_ge(sV.h, sV.n)
                done = 0
                while done < ncols // 128:
                    nb_ = min(4, ncols // 128 - done)
                    guard(P, "psA")
                    pa = psA_[:].bitcast(BF16)
                    for c in range(nb_):
                        n = sP.inc(P.transpose(pa[:, c * 128:(c + 1) * 128],
                                               qb[:, (done + c) * 128:(done + c + 1) * 128], identb[:]))
                    note("qb", sP, n)
                    V.wait_ge(sP.h, n)
                    qe, qen, dqe = ((qe0, "qe0", dQe0), (qe1, "qe1", dQe1))[qei % 2]
                    qei += 1
                    guard(V, qen)
                    n = sV.inc(V.tensor_copy(qe[:, 0:nb_ * 128], pa[:, 0:nb_ * 128]))
                    note("psA", sV, n)
                    G_.wait_ge(sV.h, n)
                    dst = wq4[:, done:done + nb_, rc:rc + 1, :]
                    srcq = qe[:, 0:nb_ * 128].rearrange("p (c t) -> p c t", t=128).unsqueeze(2)
                    nst = dma(G_, dqe, dst, srcq)
                    note(qen, dqe, nst)
                    done += nb_
            barrier()


    def emit_absmean_shard(w_hbm, nrows, ncols, amb, col, tag):
        with nc.sbuf_tensor("aw0" + tag, [128, ncols], F32) as aw0, \
             nc.sbuf_tensor("aw1" + tag, [128, ncols], F32) as aw1:
            for rc in range(nrows // 128):
                wb, awn = ((aw0, "aw0"), (aw1, "aw1"))[rc % 2]
                dwp = (dwA, dwB)[rc % 2]
                guard(SY, awn)
                nl = dma(SY, dwp, wb[:], w_hbm[rc * 128:(rc + 1) * 128, :])
                V.wait_ge(dwp.h, nl)
                V.reduce_sum(stat[:, 9:10], wb[:], axis=AX.X, apply_absolute_value=True)
                V.drain()
                if rc == 0:
                    n = sV.inc(V.tensor_single_scalar(stat[:, 8:9], stat[:, 9:10], 1.0, op=ALU.mult))
                else:
                    n = sV.inc(V.tensor_tensor(stat[:, 8:9], stat[:, 8:9], stat[:, 9:10], op=ALU.add))
                note(awn, sV, n)
                V.wait_ge(sV.h, n)
            P.wait_ge(sV.h, sV.n)
            n = sP.inc(P.matmul(psO[:, 0:1], onesq[:], stat[:, 8:9], start=True, stop=True))
            V.wait_ge(sP.h, n)
            n = sV.inc(V.tensor_copy(amb[:, col:col + 1], psO[:, 0:1]))
            V.wait_ge(sV.h, n)
        return n


    dma(SY, dX, ident[:], identh)
    dma(SY, dX, masks[:], masksh)
    G_.memset(ones1[:], 1.0)
    V.memset(stat[:], 0.0)
    V.drain()
    n = sV.inc(G_.memset(onesq[:], 1.0))
    dma(G_, dG, stream_h, x_own)
    V.wait_ge(dX.h, dX.n)
    sV.inc(V.tensor_copy(identb[:], ident[:]))
    barrier()

    AGALL = [[0, 1, 2, 3, 4, 5, 6, 7]]
    with nc.sbuf_tensor("amb", [128, 2], F32) as amb:
        emit_absmean_shard(w1s1, DFF // 8, D, amb, 0, "m1")
        emit_absmean_shard(w2s1.tensor.ap().rearrange("d (r c) -> d r c", r=1)[:, 0, :]
                           if False else w2s1, D, DFF // 8, amb, 1, "m2")
        G_.wait_ge(sV.h, sV.n)
        dma(G_, dG, am_own.tensor.ap(), amb[:])
        G_.wait_ge(dG.h, dG.n)
        n_ar = sCC.inc(G_.collective_compute(
            "AllReduce", ALU.add, replica_groups=AGALL,
            ins=[am_own.tensor.ap().opt()], outs=[am_red.tensor.ap().opt()]))
    barrier()
    emit_quant_spill(w1s0, w1qT0_o, DFF // 8, D, False, None, "qa")
    emit_quant_spill(w2s0, w2qT0_o, D, DFF // 8, False, None, "qb")
    G_.wait_ge(dQe0.h, dQe0.n); G_.wait_ge(dQe1.h, dQe1.n)
    sCC.inc(G_.collective_compute(
        "AllGather", ALU.bypass, replica_groups=AGALL,
        ins=[w1qT0_o.tensor.ap().opt()], outs=[w1qT0_h.tensor.ap().opt()]))
    cc_l0 = sCC.inc(G_.collective_compute(
        "AllGather", ALU.bypass, replica_groups=AGALL,
        ins=[w2qT0_o.tensor.ap().opt()], outs=[w2qT0_h.tensor.ap().opt()]))
    with nc.sbuf_tensor("amr", [128, 2], F32) as amr:
        G_.wait_ge(sCC.h, n_ar)
        nl = dma(G_, dGL, amr[:], am_red.tensor.ap())
        V.wait_ge(dGL.h, nl)
        V.tensor_single_scalar(amr[:], amr[:], 1.0 / (DFF * D), op=ALU.mult)
        V.drain()
        V.tensor_single_scalar(amr[:], amr[:], 1e-8, op=ALU.max)
        V.drain()
        V.reciprocal(inv0[:], amr[:, 0:1])
        V.drain()
        n = sV.inc(V.reciprocal(inv1[:], amr[:, 1:2]))
        V.wait_ge(sV.h, n)
    barrier()
    emit_quant_spill(w1s1, w1qT1_o, DFF // 8, D, True, inv0[:], "qc")
    emit_quant_spill(w2s1, w2qT1_o, D, DFF // 8, True, inv1[:], "qd")
    G_.wait_ge(dQe0.h, dQe0.n); G_.wait_ge(dQe1.h, dQe1.n)
    sCC.inc(G_.collective_compute(
        "AllGather", ALU.bypass, replica_groups=AGALL,
        ins=[w1qT1_o.tensor.ap().opt()], outs=[w1qT1_h.tensor.ap().opt()]))
    cc_l1 = sCC.inc(G_.collective_compute(
        "AllGather", ALU.bypass, replica_groups=AGALL,
        ins=[w2qT1_o.tensor.ap().opt()], outs=[w2qT1_h.tensor.ap().opt()]))
    barrier()

    with nc.sbuf_tensor("xnT", [128, SL * D], F32) as xnT, \
         nc.sbuf_tensor("ta", [128, D], F32) as ta, \
         nc.sbuf_tensor("tb", [128, D], F32) as tb, \
         nc.sbuf_tensor("tc", [128, D], F32) as tc, \
         nc.sbuf_tensor("wb0", [128, DCH * 512], F32) as wb0, \
         nc.sbuf_tensor("wb1", [128, DCH * 512], F32) as wb1, \
         nc.sbuf_tensor("ev0", [128, 512], F32) as ev0, \
         nc.sbuf_tensor("ev1", [128, 512], F32) as ev1, \
         nc.sbuf_tensor("cq", [128, SL * 64], F32) as cq, \
         nc.sbuf_tensor("sq", [128, SL * 64], F32) as sq, \
         nc.sbuf_tensor("ck", [128, SL * 64], F32) as ck, \
         nc.sbuf_tensor("sk", [128, SL * 64], F32) as sk:
        dma(SY, dX, cq[:], cosqh); dma(SY, dX, sq[:], sinqh)
        dma(SY, dX, ck[:], coskh); dma(SY, dX, sk[:], sinkh)
        V.wait_ge(dX.h, dX.n)
        vack()
        for j in range(SL):
            SY.wait_ge(sV.h, sV.n)
            nld = dma(SY, dX, ta[:], stream_h[j * 128:(j + 1) * 128, :])
            V.wait_ge(dX.h, nld)
            nln = emit_ln(ta[:], D, tb[:], tc[:], stat[:, 4:5], stat[:, 5:6])
            emit_transposes(tc[:], 0, xnT, j * DCH, DCH, False, nsrc=(sV, nln))

        def rope_spill(kind, oc, j):
            ct = (cq if kind == 'q' else ck)[:, j * 64:(j + 1) * 64]
            st_ = (sq if kind == 'q' else sk)[:, j * 64:(j + 1) * 64]
            cb = ct.unsqueeze(1).broadcast_to([128, 4, 64])
            sb_ = st_.unsqueeze(1).broadcast_to([128, 4, 64])
            e3 = ev0[:, 0:512].rearrange("p (h c) -> p h c", c=128)
            o3 = ev1[:, 0:512].rearrange("p (h c) -> p h c", c=128)
            x1 = e3[:, :, 0:64]; x2 = e3[:, :, 64:128]
            t1 = ta[:, 0:256].rearrange("p (h c) -> p h c", c=64)
            t2 = ta[:, 256:512].rearrange("p (h c) -> p h c", c=64)
            guard(V, "ev1")
            V.tensor_tensor(t1, x1, cb, op=ALU.mult)
            V.tensor_tensor(t2, x2, sb_, op=ALU.mult)
            V.drain()
            V.tensor_tensor(o3[:, :, 0:64], t1, t2, op=ALU.subtract)
            V.drain()
            V.tensor_tensor(t1, x2, cb, op=ALU.mult)
            V.tensor_tensor(t2, x1, sb_, op=ALU.mult)
            V.drain()
            n = sV.inc(V.tensor_tensor(o3[:, :, 64:128], t1, t2, op=ALU.add))
            P.wait_ge(sV.h, n)
            guard(P, "psA")
            for c in range(4):
                r = sP.inc(P.transpose(psA_[:, c * 128:(c + 1) * 128],
                                       ev1[:, c * 128:(c + 1) * 128], ident[:]))
            n = r
            note("ev1", sP, n)
            V.wait_ge(sP.h, n)
            guard(V, "ev0")
            n = sV.inc(V.tensor_copy(ev0[:, 0:512], psA_[:]))
            note("psA", sV, n)
            G_.wait_ge(sV.h, n)
            nst = spill_T(ev0[:, 0:512], 4, qT_h if kind == 'q' else kT0o, oc * 4, j * 128)
            note("ev0", dG, nst)

        wl = 0
        for wT_hbm, nout, kind in ((wqT, D, 'q'), (wkT, DK0, 'k'), (wvT, DK0, 'v')):
            for oc in range(nout // 512):
                wb, wbn = ((wb0, "wb0"), (wb1, "wb1"))[wl % 2]
                dwp = (dwA, dwB)[wl % 2]; wl += 1
                guard(SY, wbn)
                src = wT_hbm.rearrange("(dc p) n -> p dc n", p=128)[:, :, oc * 512:(oc + 1) * 512]
                wsem = dma(SY, dwp, wb[:].rearrange("p (dc n) -> p dc n", n=512), src)
                for j in range(SL):
                    psu, psn = ((psH0, "psH0"), (psH1, "psH1"))[j % 2]
                    P.wait_ge(dwp.h, wsem)
                    guard(P, psn)
                    for dc in range(DCH):
                        r = P.matmul(psu[:], xnT[:, (j * DCH + dc) * 128:(j * DCH + dc + 1) * 128],
                                     wb[:, dc * 512:(dc + 1) * 512],
                                     start=(dc == 0), stop=(dc == DCH - 1))
                    n = sP.inc(r)
                    note(wbn, sP, n)
                    V.wait_ge(sP.h, n)
                    guard(V, "ev0")
                    n = sV.inc(V.tensor_copy(ev0[:, 0:512], psu[:]))
                    note(psn, sV, n)
                    if kind == 'v':
                        G_.wait_ge(sV.h, n)
                        nst = dma(G_, dSv, v0o[j * 128:(j + 1) * 128, oc * 512:(oc + 1) * 512], ev0[:, 0:512])
                        note("ev0", dSv, nst)
                    else:
                        V.wait_ge(sV.h, n)
                        rope_spill(kind, oc, j)
        barrier()

        sCC.inc(G_.collective_compute(
            "AllGather", ALU.bypass, replica_groups=[[0, 1], [2, 3], [4, 5], [6, 7]],
            ins=[kT0o.tensor.ap().opt()], outs=[kT0g.tensor.ap().opt()]))
        sCC.inc(G_.collective_compute(
            "AllGather", ALU.bypass, replica_groups=[[0, 1], [2, 3], [4, 5], [6, 7]],
            ins=[v0o.tensor.ap().opt()], outs=[v0g.tensor.ap().opt()]))
        G_.wait_ge(sCC.h, sCC.n)
        barrier()

    with nc.sbuf_tensor("kbuf", [128, HK0 * 2048], F32) as kbuf, \
         nc.sbuf_tensor("vbuf", [128, 16 * DK0], F32) as vbuf, \
         nc.sbuf_tensor("qTs", [128, D], F32) as qTs, \
         nc.sbuf_tensor("Pb", [128, 2048], F32) as Pb, \
         nc.sbuf_tensor("PTb", [128, 2048], F32) as PTb, \
         nc.sbuf_tensor("Ot", [128, D], F32) as Ot, \
         nc.sbuf_tensor("oTe", [128, 512], F32) as oTe:
        for j in range(SL):
            nkt = 2 * j + 2
            nk = nkt * 128
            SY.wait_ge(sP.h, sP.n)
            qsrc = qT_h.rearrange("(dc p) t -> p dc t", p=128)[:, 0:DCH, j * 128:(j + 1) * 128]
            qsem = dma(SY, dX, qTs[:].rearrange("p (dc t) -> p dc t", t=128), qsrc)
            for kt in range(nkt):
                r_ = owner_rank(kt); sl = owner_slot(kt)
                ksrc = kT0g.rearrange("(c p) t -> p c t", p=128)[
                    :, r_ * HK0:(r_ + 1) * HK0, sl * 128:(sl + 1) * 128]
                kdst = kbuf[:].rearrange("p (c n) -> p c n", n=2048)[:, :, kt * 128:(kt + 1) * 128]
                dma(SY, dK, kdst, ksrc)
                vsrc = v0g[(r_ * TOK + sl * 128):(r_ * TOK + sl * 128 + 128), :]
                dma(SY, dK, vbuf[:, kt * DK0:(kt + 1) * DK0], vsrc)
            ldsem = dK.n
            for h in range(H0):
                kvh = h // GRP
                P.wait_ge(dK.h, ldsem)
                P.wait_ge(dX.h, qsem)
                P.wait_ge(sA.h, sA.n)
                P.wait_ge(sV.h, sV.n)
                for c in range((nk + 511) // 512):
                    w = min(512, nk - c * 512)
                    r = P.matmul(psBIG[:, c * 512:c * 512 + w],
                                 qTs[:, h * 128:(h + 1) * 128],
                                 kbuf[:, kvh * 2048 + c * 512: kvh * 2048 + c * 512 + w],
                                 start=True, stop=True)
                npe = sP.inc(r)
                V.wait_ge(sP.h, npe)
                V.tensor_tensor(psBIG[:, nk - 256:nk], psBIG[:, nk - 256:nk],
                                masks[:, j * 256:(j + 1) * 256], op=ALU.add)
                V.drain()
                n = sV.inc(V.reduce_max(negm[:], psBIG[:, 0:nk], axis=AX.X, negate=True))
                A.wait_ge(sV.h, n)
                na = sA.inc(A.activation(Pb[:, 0:nk], psBIG[:, 0:nk], AF.Exp,
                                         bias=negm[:], accum_out=ssum[:]))
                V.wait_ge(sA.h, na)
                n = sV.inc(V.reciprocal(rsum[:], ssum[:]))
                emit_transposes(Pb[:], 0, PTb, 0, nkt, False, nsrc=(sA, na))
                P.wait_ge(sV.h, sV.n)
                for kt in range(nkt):
                    r = P.matmul(psO[:, 0:128], PTb[:, kt * 128:(kt + 1) * 128],
                                 vbuf[:, kt * DK0 + kvh * 128: kt * DK0 + (kvh + 1) * 128],
                                 start=(kt == 0), stop=(kt == nkt - 1))
                n = sP.inc(r)
                V.wait_ge(sP.h, n)
                sV.inc(V.tensor_scalar(Ot[:, h * 128:(h + 1) * 128], psO[:, 0:128],
                                       rsum[:], None, op0=ALU.mult))
            done = 0
            while done < DCH:
                nb = min(4, DCH - done)
                P.wait_ge(sV.h, sV.n)
                for c in range(nb):
                    r = P.transpose(psA_[:, c * 128:(c + 1) * 128],
                                    Ot[:, (done + c) * 128:(done + c + 1) * 128], ident[:])
                n = sP.inc(r)
                V.wait_ge(sP.h, n)
                guard(V, "oTe")
                n = sV.inc(V.tensor_copy(oTe[:, 0:nb * 128], psA_[:, 0:nb * 128]))
                G_.wait_ge(sV.h, n)
                nst = spill_T(oTe[:, 0:nb * 128], nb, oT_h, done, j * 128)
                note("oTe", dG, nst)
                done += nb
        barrier()

    _opctr = [0]

    def emit_out_proj(wT_hbm, nin_ch, resid_fn):
        _opctr[0] += 1
        tg = str(_opctr[0])
        with nc.sbuf_tensor("pwb0" + tg, [128, nin_ch * 512], F32) as pwb0, \
             nc.sbuf_tensor("pwb1" + tg, [128, nin_ch * 512], F32) as pwb1, \
             nc.sbuf_tensor("poT" + tg, [128, nin_ch * 128], F32) as poT, \
             nc.sbuf_tensor("prt0" + tg, [128, 512], F32) as prt0, \
             nc.sbuf_tensor("prt1" + tg, [128, 512], F32) as prt1, \
             nc.sbuf_tensor("prt2" + tg, [128, 512], F32) as prt2:
            for oc in range(D // 512):
                wb, wbn = ((pwb0, "pwb0"), (pwb1, "pwb1"))[oc % 2]
                dwp = (dwA, dwB)[oc % 2]
                guard(SY, wbn)
                src = wT_hbm.rearrange("(dc p) n -> p dc n", p=128)[:, :, oc * 512:(oc + 1) * 512]
                wsem = dma(SY, dwp, wb[:].rearrange("p (dc n) -> p dc n", n=512), src)
                for j in range(SL):
                    guard(SY, "poT")
                    xsrc = oT_h.rearrange("(dc p) t -> p dc t", p=128)[:, 0:nin_ch, j * 128:(j + 1) * 128]
                    nx = dma(SY, dX, poT[:].rearrange("p (dc t) -> p dc t", t=128), xsrc)
                    psu, psn = ((psH0, "psH0"), (psH1, "psH1"))[j % 2]
                    P.wait_ge(dwp.h, wsem)
                    P.wait_ge(dX.h, nx)
                    guard(P, psn)
                    for dc in range(nin_ch):
                        r = P.matmul(psu[:], poT[:, dc * 128:(dc + 1) * 128],
                                     wb[:, dc * 512:(dc + 1) * 512],
                                     start=(dc == 0), stop=(dc == nin_ch - 1))
                    npe = sP.inc(r)
                    note(wbn, sP, npe)
                    note("poT", sP, npe)
                    resid_fn(j, oc, psu, npe, prt0, prt1, prt2)
            barrier()

    def resid_l0(j, oc, psu, npe, rt0, rt1, rt2):
        G_.wait_ge(sV.h, sV.n)
        nl = dma(G_, dGL, rt0[:], stream_h[j * 128:(j + 1) * 128, oc * 512:(oc + 1) * 512])
        V.wait_ge(dGL.h, nl)
        V.wait_ge(sP.h, npe)
        guard(V, "rt1")
        n = sV.inc(V.tensor_tensor(rt1[:], rt0[:], psu[:], op=ALU.add))
        note("psH0" if psu is psH0 else "psH1", sV, n)
        G_.wait_ge(sV.h, n)
        nst = dma(G_, dG, stream_h[j * 128:(j + 1) * 128, oc * 512:(oc + 1) * 512], rt1[:])
        note("rt1", dG, nst)

    emit_out_proj(wpT, DCH, resid_l0)

    if cfg.debug:
        G_.wait_ge(dG.h, dG.n)
        dma(G_, dG, dbg["dbg_s1"], stream_h)
        barrier()

    _ffnctr = [0]

    def emit_ffn(w1qT_hbm, w2qT_hbm, b1_hbm, b2_hbm, xT_list, l1, ntt, out_fn, fsup, wait_cc):
        nfs = DFF // fsup
        nfg = fsup // 512
        assert fsup == NF8 * 128
        xis = (0, 1) if l1 else (0,)
        _ffnctr[0] += 1
        tg = str(_ffnctr[0])
        fes = ExitStack()
        fsb = lambda name, shape, dt=F32: fes.enter_context(nc.sbuf_tensor(name + tg, shape, dt))
        h2acc = fsb("fh2", [128, ntt * D])
        w1b = fsb("fw1", [128, DCH * fsup], BF16)
        w2b = fsb("fw2", [128, (fsup // 128) * D], BF16)
        fqT = fsb("fqT", [128, 8 * 128], BF16)
        fvt = fsb("fvt", [128, 512])
        if l1:
            fgf = fsb("fgf", [128, 512])
            fgh = fsb("fgh", [128, 512], BF16)
            fgl = fsb("fgl", [128, 512], BF16)
            fg0 = fg1 = fq2 = fam = fex = fsc = fiv = None
        else:
            fg0 = fsb("fg0", [128, 512], BF16)
            fg1 = fsb("fg1", [128, 512], BF16)
            fq2 = fsb("fq2", [128, 512], BF16)
            fam = fsb("fam", [128, 16])
            fex = fsb("fex", [128, 16], I32)
            fsc = fsb("fsc", [128, 16])
            fiv = fsb("fiv", [128, 16])
            fgf = fgh = fgl = None
        with fes:
            nbb2 = dma(SY, dX, bb2[0:1, :], b2_hbm[0:1, :])
            SY.wait_ge(sCC.h, wait_cc)
            for fs in range(nfs):
                SY.wait_ge(sP.h, sP.n)
                nfc = fsup // 128
                w1t = w1qT_hbm.rearrange("(r dc fc) (p t) -> p r dc fc t", dc=DCH, fc=NF8, t=128)
                for dc in range(DCH):
                    dma(SY, dwF, w1b[:, dc * fsup:(dc + 1) * fsup].rearrange("p (fc t) -> p fc t", t=128),
                        w1t[:, fs, dc, :, :])
                w2t = w2qT_hbm.rearrange("(r fc dm) (p t) -> p r fc dm t", fc=NF8, dm=DCH, t=128)
                for fc in range(nfc):
                    dma(SY, dwF, w2b[:, fc * D:(fc + 1) * D].rearrange("p (dm t) -> p dm t", t=128),
                        w2t[:, fs, fc, :, :])
                dma(SY, dwF, bb1[0:1, 0:fsup], b1_hbm[0:1, fs * fsup:(fs + 1) * fsup])
                wsem = dwF.n
                for tt in range(ntt):
                    P.wait_ge(dwF.h, wsem)
                    P.wait_ge(dX.h, nbb2)
                    if fs == 0:
                        guard(P, "psBIG")
                        for dmc in range(NDM):
                            P.matmul(psBIG[:, dmc * 512:(dmc + 1) * 512], ones1[:],
                                     bb2[0:1, dmc * 512:(dmc + 1) * 512], start=True, stop=False)
                    def emit_h(fg):
                        psu, psn = ((psH0, "psH0"), (psH1, "psH1"))[fg % 2]
                        guard(P, psn)
                        P.matmul(psu[:], ones1[:], bb1[0:1, fg * 512:(fg + 1) * 512],
                                 start=True, stop=False)
                        for dc in range(DCH):
                            for xi in xis:
                                r = P.matmul(
                                    psu[:],
                                    xT_list[xi][:, (tt * DCH + dc) * 128:(tt * DCH + dc + 1) * 128],
                                    w1b[:, dc * fsup + fg * 512: dc * fsup + fg * 512 + 512],
                                    start=False, stop=(dc == DCH - 1 and xi == xis[-1]))
                        return sP.inc(r)

                    def emit_gchain(fg, npe):
                        psu, psn = ((psH0, "psH0"), (psH1, "psH1"))[fg % 2]
                        V.wait_ge(sP.h, npe)
                        if not l1:
                            guard(V, "fg0")
                            n = sV.inc(V.tensor_copy(fg0[:], psu[:]))
                            note(psn, sV, n)
                            A.wait_ge(sV.h, n)
                            na = sA.inc(A.activation(fg1[:], fg0[:], AF.Gelu_apprx_tanh))
                            note("fg0", sA, na)
                            V.wait_ge(sA.h, na)
                            guard(V, "fq2")
                            nq2 = emit_mx_quant(fg1[:], fq2[:], 512, fam, fex, fsc, fiv, fvt[:])
                            guard(V, "fqT")
                            emit_transposes(fq2[:], 0, fqT, 0, 4, True, nsrc=(sV, nq2))
                            note("fq2", sP, sP.n)
                        else:
                            guard(V, "fgf")
                            n = sV.inc(V.tensor_copy(fgf[:], psu[:]))
                            note(psn, sV, n)
                            A.wait_ge(sV.h, n)
                            na = sA.inc(A.activation(fgf[:], fgf[:], AF.Gelu))
                            V.wait_ge(sA.h, na)
                            sV.inc(V.tensor_copy(fgh[:], fgf[:]))
                            V.drain()
                            V.tensor_tensor(fvt[:], fgf[:], fgh[:], op=ALU.subtract)
                            V.drain()
                            n = sV.inc(V.tensor_copy(fgl[:], fvt[:]))
                            note("fgf", sV, n)
                            guard(V, "fqT")
                            emit_transposes(fgh[:], 0, fqT, 0, 4, True, nsrc=(sV, n))
                            emit_transposes(fgl[:], 0, fqT, 4, 4, True)
                        return sV.n

                    def emit_w2(fg, nq):
                        P.wait_ge(sV.h, nq)
                        for dmc in range(NDM):
                            for sub in range(4):
                                fc = fg * 4 + sub
                                for xi in xis:
                                    first = (fs > 0 and fg == 0 and sub == 0 and xi == 0)
                                    last = (fg == nfg - 1 and sub == 3 and xi == xis[-1])
                                    r = P.matmul(
                                        psBIG[:, dmc * 512:(dmc + 1) * 512],
                                        fqT[:, (xi * 4 + sub) * 128:(xi * 4 + sub + 1) * 128],
                                        w2b[:, fc * D + dmc * 512: fc * D + dmc * 512 + 512],
                                        start=first, stop=last)
                        n = sP.inc(r)
                        note("fqT", sP, n)
                        return n

                    npe = emit_h(0)
                    nq = emit_gchain(0, npe)
                    for fg in range(1, nfg):
                        npe = emit_h(fg)
                        emit_w2(fg - 1, nq)
                        nq = emit_gchain(fg, npe)
                    emit_w2(nfg - 1, nq)
                    V.wait_ge(sP.h, sP.n)
                    vsync()
                    if fs == 0:
                        n = sV.inc(V.tensor_copy(h2acc[:, tt * D:(tt + 1) * D], psBIG[:, 0:D]))
                    else:
                        n = sV.inc(V.tensor_tensor(h2acc[:, tt * D:(tt + 1) * D],
                                                   h2acc[:, tt * D:(tt + 1) * D],
                                                   psBIG[:, 0:D], op=ALU.add))
                    note("psBIG", sV, n)
            for tt in range(ntt):
                out_fn(tt, h2acc)
            barrier()

    with nc.sbuf_tensor("q1T", [128, SL * D], BF16) as q1T:
        with nc.sbuf_tensor("f2a", [128, D], F32) as f2a, \
             nc.sbuf_tensor("f2b", [128, D], F32) as f2b, \
             nc.sbuf_tensor("f2c", [128, D], F32) as f2c, \
             nc.sbuf_tensor("xbt", [128, D], BF16) as xbt, \
             nc.sbuf_tensor("q1b", [128, D], BF16) as q1b, \
             nc.sbuf_tensor("lnw", [128, D], F32) as lnw, \
             nc.sbuf_tensor("lnb", [128, D], F32) as lnb, \
             nc.sbuf_tensor("am2", [128, D // 32], F32) as am2, \
             nc.sbuf_tensor("ex2", [128, D // 32], I32) as ex2, \
             nc.sbuf_tensor("sc2", [128, D // 32], F32) as sc2, \
             nc.sbuf_tensor("iv2", [128, D // 32], F32) as iv2:
            dma(SY, dX, lnw[:], lnwh)
            nl = dma(SY, dX, lnb[:], lnbh)
            V.wait_ge(dX.h, nl)
            vack()
            for j in range(SL):
                SY.wait_ge(sV.h, sV.n)
                nl = dma(SY, dX, f2a[:], stream_h[j * 128:(j + 1) * 128, :])
                V.wait_ge(dX.h, nl)
                emit_ln(f2a[:], D, f2b[:], f2c[:], stat[:, 4:5], stat[:, 5:6])
                guard(V, "xbt")
                V.tensor_copy(xbt[:], f2c[:])
                V.drain()
                n = sV.inc(V.tensor_copy(f2c[:], xbt[:]))
                V.wait_ge(sV.h, n)
                G_.wait_ge(sV.h, n)
                nst = dma(G_, dG, xb_h[j * 128:(j + 1) * 128, :], xbt[:])
                note("xbt", dG, nst)
                emit_ln(f2c[:], D, f2a[:], f2b[:], stat[:, 6:7], stat[:, 7:8])
                V.tensor_tensor(f2b[:], f2b[:], lnw[:], op=ALU.mult)
                V.drain()
                V.tensor_tensor(f2b[:], f2b[:], lnb[:], op=ALU.add)
                V.drain()
                V.tensor_copy(q1b[:], f2b[:])
                V.drain()
                V.tensor_copy(f2b[:], q1b[:])
                V.drain()
                nq1 = emit_mx_quant(f2b[:], q1b[:], D, am2, ex2, sc2, iv2, f2a[:])
                emit_transposes(q1b[:], 0, q1T, j * DCH, DCH, True, nsrc=(sV, nq1))
            barrier()

        with nc.sbuf_tensor("o2a", [128, D], BF16) as o2a, \
             nc.sbuf_tensor("o2b", [128, D], BF16) as o2b, \
             nc.sbuf_tensor("o2c", [128, D], F32) as o2c:
            def out_l0(tt, h2acc):
                vsync()
                SY.wait_ge(sG.h, gack())
                nl = dma(SY, dX, o2a[:], xb_h[tt * 128:(tt + 1) * 128, :])
                nl2 = dma(SY, dX, o2c[:], stream_h[tt * 128:(tt + 1) * 128, :])
                V.wait_ge(dX.h, nl2)
                V.tensor_copy(o2b[:], h2acc[:, tt * D:(tt + 1) * D])
                V.drain()
                V.tensor_tensor(o2b[:], o2a[:], o2b[:], op=ALU.add)
                V.drain()
                n = sV.inc(V.tensor_tensor(o2c[:], o2c[:], o2b[:], op=ALU.add))
                G_.wait_ge(sV.h, n)
                dma(G_, dG, stream_h[tt * 128:(tt + 1) * 128, :], o2c[:])

            emit_ffn(w1qT0_h, w2qT0_h, b1h0, b2h0, [q1T], False, SL, out_l0, DFF // 8, cc_l0)

    if cfg.debug:
        G_.wait_ge(dG.h, dG.n)
        dma(G_, dG, dbg["dbg_s2"], stream_h)
        barrier()

    with nc.sbuf_tensor("xnT3", [128, SL * D], F32) as xnT3, \
         nc.sbuf_tensor("t3a", [128, D], F32) as t3a, \
         nc.sbuf_tensor("t3b", [128, D], F32) as t3b, \
         nc.sbuf_tensor("t3c", [128, D], F32) as t3c, \
         nc.sbuf_tensor("gam", [128, D], F32) as gam, \
         nc.sbuf_tensor("bet", [128, D], F32) as bet, \
         nc.sbuf_tensor("wb30", [128, DCH * 512], F32) as wb30, \
         nc.sbuf_tensor("wb31", [128, DCH * 512], F32) as wb31, \
         nc.sbuf_tensor("ev30", [128, 512], F32) as ev30, \
         nc.sbuf_tensor("ev31", [128, 512], F32) as ev31:
        dma(SY, dX, gam[:], gamh)
        nl = dma(SY, dX, bet[:], beth)
        V.wait_ge(dX.h, nl)
        vack()
        for j in range(SL):
            SY.wait_ge(sV.h, sV.n)
            nl = dma(SY, dX, t3a[:], stream_h[j * 128:(j + 1) * 128, :])
            V.wait_ge(dX.h, nl)
            emit_ln(t3a[:], D, t3b[:], t3c[:], mu3[:, j:j + 1], rstd3[:, j:j + 1])
            emit_ln(t3c[:], D, t3a[:], t3b[:], stat[:, 4:5], stat[:, 5:6])
            V.tensor_tensor(t3b[:], t3b[:], gam[:], op=ALU.mult)
            V.drain()
            n = sV.inc(V.tensor_tensor(t3b[:], t3b[:], bet[:], op=ALU.add))
            V.wait_ge(sV.h, n)
            emit_transposes(t3b[:], 0, xnT3, j * DCH, DCH, False, nsrc=(sV, n))
        wl = 0
        for sec, base in (('q', 0), ('k', KEY), ('v', 2 * KEY)):
            for oc in range(KEY // 512):
                wb, wbn = ((wb30, "wb30"), (wb31, "wb31"))[wl % 2]
                dwp = (dwA, dwB)[wl % 2]; wl += 1
                guard(SY, wbn)
                src = qkvw.rearrange("(dc p) n -> p dc n", p=128)[:, :, base + oc * 512: base + (oc + 1) * 512]
                wsem = dma(SY, dwp, wb[:].rearrange("p (dc n) -> p dc n", n=512), src)
                for j in range(SL):
                    psu, psn = ((psH0, "psH0"), (psH1, "psH1"))[j % 2]
                    P.wait_ge(dwp.h, wsem)
                    guard(P, psn)
                    for dc in range(DCH):
                        r = P.matmul(psu[:], xnT3[:, (j * DCH + dc) * 128:(j * DCH + dc + 1) * 128],
                                     wb[:, dc * 512:(dc + 1) * 512],
                                     start=(dc == 0), stop=(dc == DCH - 1))
                    n = sP.inc(r)
                    note(wbn, sP, n)
                    V.wait_ge(sP.h, n)
                    guard(V, "ev30")
                    if sec == 'q':
                        n = sV.inc(V.tensor_single_scalar(ev30[:, 0:512], psu[:], SC1, op=ALU.mult))
                    else:
                        n = sV.inc(V.tensor_copy(ev30[:, 0:512], psu[:]))
                    note(psn, sV, n)
                    if sec == 'v':
                        G_.wait_ge(sV.h, n)
                        nst = dma(G_, dSv, v1o[j * 128:(j + 1) * 128, oc * 512:(oc + 1) * 512], ev30[:, 0:512])
                        note("ev30", dSv, nst)
                    else:
                        V.wait_ge(sV.h, n)
                        guard(V, "ev31")
                        emit_transposes(ev30[:, 0:512], 0, ev31, 0, 4, False, nsrc=(sV, n))
                        note("ev30", sP, sP.n)
                        G_.wait_ge(sV.h, sV.n)
                        nst = spill_T(ev31[:, 0:512], 4, qT_h if sec == 'q' else kT1o, oc * 4, j * 128)
                        note("ev31", dG, nst)
        barrier()
        sCC.inc(G_.collective_compute(
            "AllGather", ALU.bypass, replica_groups=[[0, 1], [2, 3], [4, 5], [6, 7]],
            ins=[kT1o.tensor.ap().opt()], outs=[kT1g.tensor.ap().opt()]))
        sCC.inc(G_.collective_compute(
            "AllGather", ALU.bypass, replica_groups=[[0, 1], [2, 3], [4, 5], [6, 7]],
            ins=[v1o.tensor.ap().opt()], outs=[v1g.tensor.ap().opt()]))
        G_.wait_ge(sCC.h, sCC.n)
        barrier()

    with nc.sbuf_tensor("kb3", [128, DCH * 512], F32) as kb3, \
         nc.sbuf_tensor("vb3", [128, 4 * KEY], F32) as vb3, \
         nc.sbuf_tensor("qT3", [128, D], F32) as qT3, \
         nc.sbuf_tensor("Pb3", [128, 2048], F32) as Pb3, \
         nc.sbuf_tensor("PT3", [128, 2048], F32) as PT3, \
         nc.sbuf_tensor("Ot3", [128, D], F32) as Ot3, \
         nc.sbuf_tensor("oTe3", [128, 512], F32) as oTe3:
        for j in range(SL):
            nkt = 2 * j + 2
            nk = nkt * 128
            SY.wait_ge(sP.h, sP.n)
            qsrc = qT_h.rearrange("(dc p) t -> p dc t", p=128)[:, 0:DCH, j * 128:(j + 1) * 128]
            qsem = dma(SY, dX, qT3[:].rearrange("p (dc t) -> p dc t", t=128), qsrc)
            for c in range((nk + 511) // 512):
                tiles = range(c * 4, min(c * 4 + 4, nkt))
                SY.wait_ge(sP.h, sP.n)
                for kt in tiles:
                    r_ = owner_rank(kt); sl = owner_slot(kt)
                    ksrc = kT1g.rearrange("(dc p) t -> p dc t", p=128)[
                        :, r_ * DCH:(r_ + 1) * DCH, sl * 128:(sl + 1) * 128]
                    kdst = kb3[:].rearrange("p (dc n) -> p dc n", n=512)[:, :, (kt - c * 4) * 128:(kt - c * 4 + 1) * 128]
                    dma(SY, dK, kdst, ksrc)
                w = min(512, nk - c * 512)
                P.wait_ge(dK.h, dK.n)
                P.wait_ge(dX.h, qsem)
                P.wait_ge(sA.h, sA.n)
                P.wait_ge(sV.h, sV.n)
                for dc in range(DCH):
                    r = P.matmul(psBIG[:, c * 512:c * 512 + w],
                                 qT3[:, dc * 128:(dc + 1) * 128],
                                 kb3[:, dc * 512:dc * 512 + w],
                                 start=(dc == 0), stop=(dc == DCH - 1))
                sP.inc(r)
            V.wait_ge(sP.h, sP.n)
            V.tensor_tensor(psBIG[:, nk - 256:nk], psBIG[:, nk - 256:nk],
                            masks[:, j * 256:(j + 1) * 256], op=ALU.add)
            V.drain()
            n = sV.inc(V.reduce_max(negm[:], psBIG[:, 0:nk], axis=AX.X, negate=True))
            A.wait_ge(sV.h, n)
            na = sA.inc(A.activation(Pb3[:, 0:nk], psBIG[:, 0:nk], AF.Exp,
                                     bias=negm[:], accum_out=ssum[:]))
            V.wait_ge(sA.h, na)
            n = sV.inc(V.reciprocal(rsum[:], ssum[:]))
            emit_transposes(Pb3[:], 0, PT3, 0, nkt, False, nsrc=(sA, na))
            for c in range((nk + 511) // 512):
                tiles = list(range(c * 4, min(c * 4 + 4, nkt)))
                SY.wait_ge(sP.h, sP.n)
                for i, kt in enumerate(tiles):
                    r_ = owner_rank(kt); sl = owner_slot(kt)
                    vsrc = v1g[(r_ * TOK + sl * 128):(r_ * TOK + sl * 128 + 128), :]
                    dma(SY, dK, vb3[:, i * KEY:(i + 1) * KEY], vsrc)
                P.wait_ge(dK.h, dK.n)
                if c == 0:
                    P.wait_ge(sA.h, sA.n)
                    P.wait_ge(sV.h, sV.n)
                for dmc in range(NDM):
                    for i, kt in enumerate(tiles):
                        r = P.matmul(psBIG[:, dmc * 512:(dmc + 1) * 512],
                                     PT3[:, kt * 128:(kt + 1) * 128],
                                     vb3[:, i * KEY + dmc * 512: i * KEY + dmc * 512 + 512],
                                     start=(kt == 0), stop=(kt == nkt - 1))
                sP.inc(r)
            n = sP.n
            V.wait_ge(sP.h, n)
            for dmc in range(NDM):
                sV.inc(V.tensor_scalar(Ot3[:, dmc * 512:(dmc + 1) * 512],
                                       psBIG[:, dmc * 512:(dmc + 1) * 512],
                                       rsum[:], None, op0=ALU.mult))
            done = 0
            while done < DCH:
                nb = min(4, DCH - done)
                P.wait_ge(sV.h, sV.n)
                for c2 in range(nb):
                    r = P.transpose(psA_[:, c2 * 128:(c2 + 1) * 128],
                                    Ot3[:, (done + c2) * 128:(done + c2 + 1) * 128], ident[:])
                n = sP.inc(r)
                V.wait_ge(sP.h, n)
                guard(V, "oTe3")
                n = sV.inc(V.tensor_copy(oTe3[:, 0:nb * 128], psA_[:, 0:nb * 128]))
                G_.wait_ge(sV.h, n)
                nst = spill_T(oTe3[:, 0:nb * 128], nb, oT_h, done, j * 128)
                note("oTe3", dG, nst)
                done += nb
        barrier()

    def resid_l1(j, oc, psu, npe, rt0, rt1, rt2):
        G_.wait_ge(sV.h, sV.n)
        nl = dma(G_, dGL, rt0[:], stream_h[j * 128:(j + 1) * 128, oc * 512:(oc + 1) * 512])
        V.wait_ge(dGL.h, nl)
        V.wait_ge(sP.h, npe)
        V.tensor_scalar(rt2[:], rt0[:], mu3[:, j:j + 1], None, op0=ALU.subtract)
        V.drain()
        V.tensor_scalar(rt2[:], rt2[:], rstd3[:, j:j + 1], None, op0=ALU.mult)
        V.drain()
        guard(V, "rt1")
        nps = sV.inc(V.tensor_tensor(rt1[:], rt2[:], psu[:], op=ALU.add))
        note("psH0" if psu is psH0 else "psH1", sV, nps)
        V.wait_ge(sV.h, nps)
        n = sV.inc(V.tensor_tensor(rt1[:], rt1[:], rt0[:], op=ALU.add))
        G_.wait_ge(sV.h, n)
        nst = dma(G_, dG, stream_h[j * 128:(j + 1) * 128, oc * 512:(oc + 1) * 512], rt1[:])
        note("rt1", dG, nst)

    emit_out_proj(oprw, DCH, resid_l1)

    if cfg.debug:
        G_.wait_ge(dG.h, dG.n)
        dma(G_, dG, dbg["dbg_s3"], stream_h)
        barrier()

    HT = SL // 2 if SL >= 2 else 1
    for half in range((SL + HT - 1) // HT):
        tts = list(range(half * HT, min((half + 1) * HT, SL)))
        with nc.sbuf_tensor("xhiT%d" % half, [128, len(tts) * D], BF16) as xhiT, \
             nc.sbuf_tensor("xloT%d" % half, [128, len(tts) * D], BF16) as xloT:
            with nc.sbuf_tensor("f4a%d" % half, [128, D], F32) as f4a, \
                 nc.sbuf_tensor("f4b%d" % half, [128, D], F32) as f4b, \
                 nc.sbuf_tensor("f4c%d" % half, [128, D], F32) as f4c, \
                 nc.sbuf_tensor("xhib%d" % half, [128, D], BF16) as xhib, \
                 nc.sbuf_tensor("xlob%d" % half, [128, D], BF16) as xlob:
                for i, tt in enumerate(tts):
                    SY.wait_ge(sV.h, sV.n)
                    nl = dma(SY, dX, f4a[:], stream_h[tt * 128:(tt + 1) * 128, :])
                    V.wait_ge(dX.h, nl)
                    emit_ln(f4a[:], D, f4b[:], f4c[:], stat[:, 4:5], stat[:, 5:6])
                    V.tensor_copy(xhib[:], f4c[:])
                    V.drain()
                    V.tensor_tensor(f4b[:], f4c[:], xhib[:], op=ALU.subtract)
                    V.drain()
                    n = sV.inc(V.tensor_copy(xlob[:], f4b[:]))
                    V.wait_ge(sV.h, n)
                    emit_transposes(xhib[:], 0, xhiT, i * DCH, DCH, True, nsrc=(sV, n))
                    emit_transposes(xlob[:], 0, xloT, i * DCH, DCH, True)
                barrier()

            with nc.sbuf_tensor("o4a%d" % half, [128, D], F32) as o4a:
                def out_l1(ttl, h2acc, _tts=tts):
                    tt = _tts[ttl]
                    vsync()
                    SY.wait_ge(sG.h, gack())
                    nl = dma(SY, dX, o4a[:], stream_h[tt * 128:(tt + 1) * 128, :])
                    V.wait_ge(dX.h, nl)
                    n = sV.inc(V.tensor_tensor(o4a[:], o4a[:], h2acc[:, ttl * D:(ttl + 1) * D], op=ALU.add))
                    G_.wait_ge(sV.h, n)
                    dma(G_, dG, y_own[tt * 128:(tt + 1) * 128, :], o4a[:])

                emit_ffn(w1qT1_h, w2qT1_h, b1h1, b2h1, [xhiT, xloT], True, len(tts), out_l1, DFF // 8, cc_l1)

    barrier()
    es.close()
    return nc



def _rope_tables(cfg, positions, scale):
    half = 64
    j64 = (np.arange(half, dtype=np.float32) / np.float32(half)).astype(np.float32)
    inv = (np.float32(1.0) / (np.float32(10000.0) ** j64)).astype(np.float32)
    ang = (positions[:, None].astype(np.float32) * inv[None, :]).astype(np.float32)
    cos = np.cos(ang.astype(np.float64)).astype(np.float32) * np.float32(scale)
    sin = np.sin(ang.astype(np.float64)).astype(np.float32) * np.float32(scale)
    return cos.astype(np.float32), sin.astype(np.float32)


def _host_inputs(cfg, I, core):
    seq, rank = core // 2, core % 2
    T, D, DFF, DK0, SL = cfg.T, cfg.D, cfg.DFF, cfg.DK0, cfg.SL
    rows = np.concatenate([np.arange(g_of(j, rank) * 128, g_of(j, rank) * 128 + 128)
                           for j in range(SL)])
    x2 = np.asarray(I["L0_attn_x"]).reshape(cfg.B * T, D)
    m = {}
    m["x_own"] = np.ascontiguousarray(x2[seq * T + rows])
    m["wqT"] = np.ascontiguousarray(np.asarray(I["L0_attn_Wq"]).T)
    m["wkT"] = np.ascontiguousarray(np.asarray(I["L0_attn_Wk"]).T)
    m["wvT"] = np.ascontiguousarray(np.asarray(I["L0_attn_Wv"]).T)
    m["wpT"] = np.ascontiguousarray(np.asarray(I["L0_attn_Wp"]).T)
    m["w1n0"] = np.ascontiguousarray(np.asarray(I["L0_ffn_W1"]))
    m["w2n0"] = np.ascontiguousarray(np.asarray(I["L0_ffn_W2"]))
    m["b1h0"] = np.ascontiguousarray(np.asarray(I["L0_ffn_b1"]).reshape(1, DFF))
    m["b2h0"] = np.ascontiguousarray(np.asarray(I["L0_ffn_b2"]).reshape(1, D))
    m["lnwh"] = np.ascontiguousarray(np.broadcast_to(np.asarray(I["L0_ffn_ln_w"]), (128, D)))
    m["lnbh"] = np.ascontiguousarray(np.broadcast_to(np.asarray(I["L0_ffn_ln_b"]), (128, D)))
    m["gamh"] = np.ascontiguousarray(np.broadcast_to(np.asarray(I["L1_attn_gamma"]), (128, D)))
    m["beth"] = np.ascontiguousarray(np.broadcast_to(np.asarray(I["L1_attn_beta"]), (128, D)))
    m["qkvw"] = np.ascontiguousarray(np.asarray(I["L1_attn_qkv"]))
    m["oprw"] = np.ascontiguousarray(np.asarray(I["L1_attn_o_proj"]))
    m["w1n1"] = np.ascontiguousarray(np.asarray(I["L1_ffn_w1"]))
    m["w2n1"] = np.ascontiguousarray(np.asarray(I["L1_ffn_w2"]))
    fsh = DFF // 8
    m["w1s0"] = np.ascontiguousarray(np.asarray(I["L0_ffn_W1"])[core * fsh:(core + 1) * fsh, :])
    m["w2s0"] = np.ascontiguousarray(np.asarray(I["L0_ffn_W2"])[:, core * fsh:(core + 1) * fsh])
    m["w1s1"] = np.ascontiguousarray(np.asarray(I["L1_ffn_w1"])[core * fsh:(core + 1) * fsh, :])
    m["w2s1"] = np.ascontiguousarray(np.asarray(I["L1_ffn_w2"])[:, core * fsh:(core + 1) * fsh])
    m["b1h1"] = np.ascontiguousarray(np.asarray(I["L1_ffn_b1"]).reshape(1, DFF))
    m["b2h1"] = np.ascontiguousarray(np.asarray(I["L1_ffn_b2"]).reshape(1, D))
    qs = 1.0 / np.sqrt(np.float32(cfg.DH))
    cq = np.empty((128, SL * 64), np.float32); sq = np.empty((128, SL * 64), np.float32)
    ck = np.empty((128, SL * 64), np.float32); sk = np.empty((128, SL * 64), np.float32)
    for j in range(SL):
        pos = np.arange(g_of(j, rank) * 128, g_of(j, rank) * 128 + 128)
        c1, s1 = _rope_tables(cfg, pos, qs)
        c2, s2 = _rope_tables(cfg, pos, 1.0)
        cq[:, j * 64:(j + 1) * 64] = c1; sq[:, j * 64:(j + 1) * 64] = s1
        ck[:, j * 64:(j + 1) * 64] = c2; sk[:, j * 64:(j + 1) * 64] = s2
    m["cosqh"] = cq; m["sinqh"] = sq; m["coskh"] = ck; m["sinkh"] = sk
    tri = np.triu(np.full((128, 128), NEG_BIG, np.float32), 1)
    full = np.full((128, 128), NEG_BIG, np.float32)
    zero = np.zeros((128, 128), np.float32)
    mk = np.empty((128, SL * 256), np.float32)
    for j in range(SL):
        g = g_of(j, rank)
        if g == 2 * j:
            mk[:, j * 256: j * 256 + 128] = tri
            mk[:, j * 256 + 128: j * 256 + 256] = full
        else:
            mk[:, j * 256: j * 256 + 128] = zero
            mk[:, j * 256 + 128: j * 256 + 256] = tri
    m["masksh"] = mk
    m["identh"] = np.eye(128, dtype=np.float32)
    return m, rows


def kernel(**inputs):
    cfg = Cfg()
    nc = build(cfg)
    in_maps = []
    row_list = []
    for core in range(8):
        m, rows = _host_inputs(cfg, inputs, core)
        in_maps.append(m)
        row_list.append((core // 2, rows))
    res = run_bass_kernel_spmd(nc, in_maps, core_ids=list(range(8)))
    out = np.empty((cfg.B, cfg.T, cfg.D), np.float32)
    for core in range(8):
        seq, rows = row_list[core]
        out.reshape(cfg.B * cfg.T, cfg.D)[seq * cfg.T + rows] = res.results[core]["y_own"]
    return out


# version 05
# speedup vs baseline: 1.4849x; 1.0362x over previous
import numpy as np
import ml_dtypes
from contextlib import ExitStack

import concourse.bass as bass
import concourse.mybir as mybir
from concourse.bass_utils import run_bass_kernel_spmd

F32 = mybir.dt.float32
I32 = mybir.dt.int32
BF16 = mybir.dt.bfloat16
AF = mybir.ActivationFunctionType
ALU = mybir.AluOpType
AX = mybir.AxisListType

MAGIC = float(np.float32(12582912.0))
NEG_BIG = float(np.finfo(np.float32).min)
AMAX_FLOOR = float(np.float32(2.0 ** -100))
EPS_LN = 1e-5


class Cfg:
    def __init__(self, T=2048, D=2048, DFF=8192, DK0=512, debug=False):
        self.T, self.D, self.DFF, self.DK0 = T, D, DFF, DK0
        self.B = 4
        self.DH = 128
        self.H0 = D // self.DH
        self.HK0 = DK0 // self.DH
        self.KEY = D
        self.G = T // 128
        self.SL = self.G // 2
        self.TOK = self.SL * 128
        self.DCH = D // 128
        self.debug = debug


def g_of(j, r):
    return 2 * j + ((j + r) % 2)


def owner_rank(kt):
    return (kt // 2 + kt % 2) % 2


def owner_slot(kt):
    return kt // 2


def build(cfg):
    nc = bass.Bass("TRN2", target_bir_lowering=False)
    T, D, DFF, DK0 = cfg.T, cfg.D, cfg.DFF, cfg.DK0
    SL, TOK, DCH = cfg.SL, cfg.TOK, cfg.DCH
    H0, HK0, KEY = cfg.H0, cfg.HK0, cfg.KEY
    GRP = H0 // HK0
    FSUP = min(1024, DFF)
    NDM = D // 512
    SC1 = float(np.float32(1.0 / np.sqrt(np.float32(KEY))))

    def din(name, shape, dt=F32):
        return nc.dram_tensor(name, shape, dt, kind="ExternalInput").ap()

    x_own = din("x_own", [TOK, D])
    wqT = din("wqT", [D, D]); wkT = din("wkT", [D, DK0]); wvT = din("wvT", [D, DK0]); wpT = din("wpT", [D, D])
    w1n0 = din("w1n0", [DFF, D]); w2n0 = din("w2n0", [D, DFF])
    b1h0 = din("b1h0", [1, DFF]); b2h0 = din("b2h0", [1, D])
    lnwh = din("lnwh", [128, D]); lnbh = din("lnbh", [128, D])
    gamh = din("gamh", [128, D]); beth = din("beth", [128, D])
    qkvw = din("qkvw", [D, 3 * KEY]); oprw = din("oprw", [KEY, D])
    w1n1 = din("w1n1", [DFF, D]); w2n1 = din("w2n1", [D, DFF])
    b1h1 = din("b1h1", [1, DFF]); b2h1 = din("b2h1", [1, D])
    cosqh = din("cosqh", [128, SL * 64]); sinqh = din("sinqh", [128, SL * 64])
    coskh = din("coskh", [128, SL * 64]); sinkh = din("sinkh", [128, SL * 64])
    masksh = din("masksh", [128, SL * 256])
    w1s0 = din("w1s0", [DFF // 8, D]); w2s0 = din("w2s0", [D, DFF // 8])
    w1s1 = din("w1s1", [DFF // 8, D]); w2s1 = din("w2s1", [D, DFF // 8])
    identh = din("identh", [128, 128])
    y_own = nc.dram_tensor("y_own", [TOK, D], F32, kind="ExternalOutput").ap()
    dbg = {}
    if cfg.debug:
        for nm in ("dbg_s1", "dbg_s2", "dbg_s3"):
            dbg[nm] = nc.dram_tensor(nm, [TOK, D], F32, kind="ExternalOutput").ap()

    def dint(name, shape, dt=F32):
        return nc.dram_tensor(name, shape, dt).ap()

    stream_h = dint("stream_h", [TOK, D])
    xb_h = dint("xb_h", [TOK, D], BF16)
    qT_h = dint("qT_h", [max(D, KEY), TOK])
    oT_h = dint("oT_h", [D, TOK])
    kT0o = dint("kT0o", [DK0, TOK]); kT0g = dint("kT0g", [2 * DK0, TOK])
    v0o = dint("v0o", [TOK, DK0]); v0g = dint("v0g", [2 * TOK, DK0])
    kT1o = dint("kT1o", [KEY, TOK]); kT1g = dint("kT1g", [2 * KEY, TOK])
    v1o = dint("v1o", [TOK, KEY]); v1g = dint("v1g", [2 * TOK, KEY])
    NF = DFF // 128
    NF8 = NF // 8
    w1qT0_o = dint("w1qT0_o", [DCH * NF8, 128 * 128], BF16)
    w2qT0_o = dint("w2qT0_o", [NF8 * DCH, 128 * 128], BF16)
    w1qT1_o = dint("w1qT1_o", [DCH * NF8, 128 * 128], BF16)
    w2qT1_o = dint("w2qT1_o", [NF8 * DCH, 128 * 128], BF16)
    w1qT0_h = dint("w1qT0_h", [8 * DCH * NF8, 128 * 128], BF16)
    w2qT0_h = dint("w2qT0_h", [8 * NF8 * DCH, 128 * 128], BF16)
    w1qT1_h = dint("w1qT1_h", [8 * DCH * NF8, 128 * 128], BF16)
    w2qT1_h = dint("w2qT1_h", [8 * NF8 * DCH, 128 * 128], BF16)
    am_own = dint("am_own", [128, 2])
    am_red = dint("am_red", [128, 2])

    es = ExitStack()
    sbuf = lambda name, shape, dt=F32: es.enter_context(nc.sbuf_tensor(name, shape, dt))

    class Sem:
        __slots__ = ("h", "n")
        def __init__(self, name):
            self.h = es.enter_context(nc.semaphore(name))
            self.n = 0
        def inc(self, instr, k=1):
            instr.then_inc(self.h, k)
            self.n += k
            return self.n

    dW = Sem("dW"); dX = Sem("dX"); dK = Sem("dK"); dG = Sem("dG")
    dwA = Sem("dwA"); dwB = Sem("dwB"); dwF = Sem("dwF"); dGL = Sem("dGL")
    dSv = Sem("dSv"); dQe0 = Sem("dQe0"); dQe1 = Sem("dQe1")
    dXA = Sem("dXA"); dXB = Sem("dXB")
    sV = Sem("sV"); sA = Sem("sA"); sP = Sem("sP"); sCC = Sem("sCC")
    sB = Sem("sB"); sG = Sem("sG")
    CSEMS = (sV, sA, sP)

    V, A, P, G_, SY = nc.vector, nc.scalar, nc.tensor, nc.gpsimd, nc.sync

    ident = sbuf("ident", [128, 128])
    identb = sbuf("identb", [128, 128], BF16)
    ones1 = sbuf("ones1", [1, 128])
    onesq = sbuf("onesq", [128, 128])
    masks = sbuf("masks", [128, SL * 256])
    stat = sbuf("stat", [128, 16])
    mu3 = sbuf("mu3", [128, SL]); rstd3 = sbuf("rstd3", [128, SL])
    inv0 = sbuf("inv0", [128, 1]); inv1 = sbuf("inv1", [128, 1])
    negm = sbuf("negm", [128, 1]); ssum = sbuf("ssum", [128, 1]); rsum = sbuf("rsum", [128, 1])
    bb1 = sbuf("bb1", [1, FSUP])
    bb2 = sbuf("bb2", [1, D])

    psBIG = es.enter_context(nc.psum_tensor("psBIG", [128, 2048], F32))
    psH0 = es.enter_context(nc.psum_tensor("psH0", [128, 512], F32))
    psH1 = es.enter_context(nc.psum_tensor("psH1", [128, 512], F32))
    psA_ = es.enter_context(nc.psum_tensor("psA_", [128, 512], F32))
    psO = es.enter_context(nc.psum_tensor("psO", [128, 512], F32))

    def barrier():
        SY.wait_ge(dW.h, dW.n); SY.wait_ge(dX.h, dX.n); SY.wait_ge(dK.h, dK.n)
        SY.wait_ge(dwA.h, dwA.n); SY.wait_ge(dwB.h, dwB.n); SY.wait_ge(dwF.h, dwF.n)
        SY.wait_ge(dXA.h, dXA.n); SY.wait_ge(dXB.h, dXB.n)
        SY.sem_inc(sB.h, 1); sB.n += 1
        G_.wait_ge(dG.h, dG.n)
        G_.wait_ge(dGL.h, dGL.n)
        G_.wait_ge(dSv.h, dSv.n)
        G_.wait_ge(dQe0.h, dQe0.n); G_.wait_ge(dQe1.h, dQe1.n)
        G_.sem_inc(sB.h, 1); sB.n += 1
        for eng in (V, A, P, G_, SY):
            eng.wait_ge(sB.h, sB.n)
            for s in CSEMS:
                eng.wait_ge(s.h, s.n)
        pend.clear()

    pend = {}

    def note(name, sem, val):
        d = pend.setdefault(name, {})
        d[id(sem)] = (sem, max(val, d.get(id(sem), (None, -1))[1]))

    def guard(eng, name):
        for sem, v in pend.pop(name, {}).values():
            eng.wait_ge(sem.h, v)

    def vsync():
        V.wait_ge(sV.h, sV.n)

    def vack():
        return sV.inc(V.tensor_copy(stat[:, 15:16], stat[:, 14:15]))

    def gack():
        G_.wait_ge(dG.h, dG.n)
        G_.sem_inc(sG.h, 1); sG.n += 1
        return sG.n

    def dma(q, qsem, out, in_):
        q.dma_start(out=out, in_=in_).then_inc(qsem.h, 16)
        qsem.n += 16
        return qsem.n

    def emit_ln(xt, ncols, dump, xn_out, mu_ap, rstd_ap):
        V.reduce_sum(stat[:, 0:1], xt, axis=AX.X)
        V.drain()
        V.tensor_single_scalar(mu_ap, stat[:, 0:1], 1.0 / ncols, op=ALU.mult)
        V.drain()
        n = sV.inc(V.tensor_scalar(xn_out, xt, mu_ap, None, op0=ALU.subtract))
        A.wait_ge(sV.h, n)
        n = sA.inc(A.activation(dump, xn_out, AF.Square, accum_out=stat[:, 1:2]))
        V.wait_ge(sA.h, n)
        n = sV.inc(V.tensor_scalar(stat[:, 2:3], stat[:, 1:2], 1.0 / ncols, EPS_LN,
                                   op0=ALU.mult, op1=ALU.add))
        A.wait_ge(sV.h, n)
        n = sA.inc(A.activation(stat[:, 3:4], stat[:, 2:3], AF.Sqrt))
        V.wait_ge(sA.h, n)
        V.reciprocal(rstd_ap, stat[:, 3:4])
        V.drain()
        n = sV.inc(V.tensor_scalar(xn_out, xn_out, rstd_ap, None, op0=ALU.mult))
        V.wait_ge(sV.h, n)
        return n

    def emit_transposes(src, soff, dst, doff, k, bf16, nsrc=None):
        idt = identb if bf16 else ident
        pa = psA_[:].bitcast(BF16) if bf16 else psA_[:]
        if nsrc is not None:
            P.wait_ge(nsrc[0].h, nsrc[1])
        done = 0
        while done < k:
            nb = min(4, k - done)
            guard(P, "psA")
            for c in range(nb):
                n = sP.inc(P.transpose(pa[:, c * 128:(c + 1) * 128],
                                       src[:, (soff + done + c) * 128:(soff + done + c + 1) * 128], idt[:]))
            V.wait_ge(sP.h, n)
            nv = sV.inc(V.tensor_copy(dst[:, (doff + done) * 128:(doff + done + nb) * 128],
                                      pa[:, 0:nb * 128]))
            note("psA", sV, nv)
            done += nb

    def spill_T(src, k, dsth, row0_chunk, col0, qsem=None):
        dst = dsth.rearrange("(c p) t -> p c t", p=128)[:, row0_chunk:row0_chunk + k, col0:col0 + 128]
        return dma(G_, qsem if qsem is not None else dG, dst, src.rearrange("p (c t) -> p c t", t=128))

    def emit_mx_quant(xin, xout_bf, ncols, amx, expb, scl, invs, vtmp):
        nb = ncols // 32
        x3 = xin.rearrange("p (b k) -> p b k", k=32)
        V.tensor_reduce(amx[:, 0:nb], x3, axis=AX.X, op=ALU.max, apply_absolute_value=True)
        V.drain()
        V.tensor_single_scalar(amx[:, 0:nb], amx[:, 0:nb], AMAX_FLOOR, op=ALU.max)
        V.drain()
        V.tensor_single_scalar(expb[:, 0:nb], amx[:, 0:nb].bitcast(I32), 0x7F800000, op=ALU.bitwise_and)
        V.drain()
        V.tensor_single_scalar(scl[:, 0:nb], expb[:, 0:nb].bitcast(F32),
                               float(np.float32(2.0 ** -6)), op=ALU.mult)
        V.drain()
        V.reciprocal(invs[:, 0:nb], scl[:, 0:nb])
        V.drain()
        inv3 = invs[:, 0:nb].unsqueeze(2).broadcast_to([128, nb, 32])
        v3 = vtmp.rearrange("p (b k) -> p b k", k=32)
        V.tensor_tensor(v3, x3, inv3, op=ALU.mult)
        V.drain()
        V.tensor_scalar(vtmp, vtmp, MAGIC, float(np.float32(MAGIC + 127.0)), op0=ALU.add, op1=ALU.min)
        V.drain()
        V.tensor_scalar(vtmp, vtmp, float(np.float32(MAGIC - 128.0)), -MAGIC, op0=ALU.max, op1=ALU.add)
        V.drain()
        sc3 = scl[:, 0:nb].unsqueeze(2).broadcast_to([128, nb, 32])
        o3 = xout_bf.rearrange("p (b k) -> p b k", k=32)
        n = sV.inc(V.tensor_tensor(o3, v3, sc3, op=ALU.mult))
        V.wait_ge(sV.h, n)
        return n

    def emit_quant_spill(w_hbm, wqT_hbm, nrows, ncols, ternary, inv_ap, tag):
        nrc = nrows // 128
        with nc.sbuf_tensor("qw0" + tag, [128, ncols], F32) as qw0, \
             nc.sbuf_tensor("qw1" + tag, [128, ncols], F32) as qw1, \
             nc.sbuf_tensor("qv" + tag, [128, ncols], F32) as qv, \
             nc.sbuf_tensor("qb" + tag, [128, ncols], BF16) as qb, \
             nc.sbuf_tensor("qe0" + tag, [128, 512], BF16) as qe0, \
             nc.sbuf_tensor("qe1" + tag, [128, 512], BF16) as qe1, \
             nc.sbuf_tensor("qam" + tag, [128, ncols // 32], F32) as qam, \
             nc.sbuf_tensor("qex" + tag, [128, ncols // 32], I32) as qex, \
             nc.sbuf_tensor("qsc" + tag, [128, ncols // 32], F32) as qsc, \
             nc.sbuf_tensor("qiv" + tag, [128, ncols // 32], F32) as qiv:
            wq4 = wqT_hbm.rearrange("(cc rr) (p t) -> p cc rr t", rr=nrc, t=128)
            qei = 0
            for rc in range(nrc):
                wbq, wbqn = ((qw0, "qw0"), (qw1, "qw1"))[rc % 2]
                dwp = (dwA, dwB)[rc % 2]
                guard(SY, wbqn)
                nl = dma(SY, dwp, wbq[:], w_hbm[rc * 128:(rc + 1) * 128, :])
                V.wait_ge(dwp.h, nl)
                guard(V, "qb")
                if ternary:
                    V.tensor_scalar(qv[:], wbq[:], inv_ap, None, op0=ALU.mult)
                    V.drain()
                    V.tensor_scalar(qv[:], qv[:], MAGIC, float(np.float32(MAGIC + 1.0)),
                                    op0=ALU.add, op1=ALU.min)
                    V.drain()
                    nq = sV.inc(V.tensor_scalar(qb[:], qv[:], float(np.float32(MAGIC - 1.0)), -MAGIC,
                                                 op0=ALU.max, op1=ALU.add))
                    V.wait_ge(sV.h, nq)
                else:
                    emit_mx_quant(wbq[:], qb[:], ncols, qam, qex, qsc, qiv, qv[:])
                note(wbqn, sV, sV.n)
                P.wait_ge(sV.h, sV.n)
                done = 0
                while done < ncols // 128:
                    nb_ = min(4, ncols // 128 - done)
                    guard(P, "psA")
                    pa = psA_[:].bitcast(BF16)
                    for c in range(nb_):
                        n = sP.inc(P.transpose(pa[:, c * 128:(c + 1) * 128],
                                               qb[:, (done + c) * 128:(done + c + 1) * 128], identb[:]))
                    note("qb", sP, n)
                    V.wait_ge(sP.h, n)
                    qe, qen, dqe = ((qe0, "qe0", dQe0), (qe1, "qe1", dQe1))[qei % 2]
                    qei += 1
                    guard(V, qen)
                    n = sV.inc(V.tensor_copy(qe[:, 0:nb_ * 128], pa[:, 0:nb_ * 128]))
                    note("psA", sV, n)
                    G_.wait_ge(sV.h, n)
                    dst = wq4[:, done:done + nb_, rc:rc + 1, :]
                    srcq = qe[:, 0:nb_ * 128].rearrange("p (c t) -> p c t", t=128).unsqueeze(2)
                    nst = dma(G_, dqe, dst, srcq)
                    note(qen, dqe, nst)
                    done += nb_
            barrier()


    def emit_absmean_shard(w_hbm, nrows, ncols, amb, col, tag):
        with nc.sbuf_tensor("aw0" + tag, [128, ncols], F32) as aw0, \
             nc.sbuf_tensor("aw1" + tag, [128, ncols], F32) as aw1:
            for rc in range(nrows // 128):
                wb, awn = ((aw0, "aw0"), (aw1, "aw1"))[rc % 2]
                dwp = (dwA, dwB)[rc % 2]
                guard(SY, awn)
                nl = dma(SY, dwp, wb[:], w_hbm[rc * 128:(rc + 1) * 128, :])
                V.wait_ge(dwp.h, nl)
                V.reduce_sum(stat[:, 9:10], wb[:], axis=AX.X, apply_absolute_value=True)
                V.drain()
                if rc == 0:
                    n = sV.inc(V.tensor_single_scalar(stat[:, 8:9], stat[:, 9:10], 1.0, op=ALU.mult))
                else:
                    n = sV.inc(V.tensor_tensor(stat[:, 8:9], stat[:, 8:9], stat[:, 9:10], op=ALU.add))
                note(awn, sV, n)
                V.wait_ge(sV.h, n)
            P.wait_ge(sV.h, sV.n)
            n = sP.inc(P.matmul(psO[:, 0:1], onesq[:], stat[:, 8:9], start=True, stop=True))
            V.wait_ge(sP.h, n)
            n = sV.inc(V.tensor_copy(amb[:, col:col + 1], psO[:, 0:1]))
            V.wait_ge(sV.h, n)
        return n


    dma(SY, dX, ident[:], identh)
    dma(SY, dX, masks[:], masksh)
    G_.memset(ones1[:], 1.0)
    V.memset(stat[:], 0.0)
    V.drain()
    n = sV.inc(G_.memset(onesq[:], 1.0))
    dma(G_, dG, stream_h, x_own)
    V.wait_ge(dX.h, dX.n)
    sV.inc(V.tensor_copy(identb[:], ident[:]))
    barrier()

    AGALL = [[0, 1, 2, 3, 4, 5, 6, 7]]
    with nc.sbuf_tensor("amb", [128, 2], F32) as amb:
        emit_absmean_shard(w1s1, DFF // 8, D, amb, 0, "m1")
        emit_absmean_shard(w2s1.tensor.ap().rearrange("d (r c) -> d r c", r=1)[:, 0, :]
                           if False else w2s1, D, DFF // 8, amb, 1, "m2")
        G_.wait_ge(sV.h, sV.n)
        dma(G_, dG, am_own.tensor.ap(), amb[:])
        G_.wait_ge(dG.h, dG.n)
        n_ar = sCC.inc(G_.collective_compute(
            "AllReduce", ALU.add, replica_groups=AGALL,
            ins=[am_own.tensor.ap().opt()], outs=[am_red.tensor.ap().opt()]))
    barrier()
    emit_quant_spill(w1s0, w1qT0_o, DFF // 8, D, False, None, "qa")
    emit_quant_spill(w2s0, w2qT0_o, D, DFF // 8, False, None, "qb")
    G_.wait_ge(dQe0.h, dQe0.n); G_.wait_ge(dQe1.h, dQe1.n)
    sCC.inc(G_.collective_compute(
        "AllGather", ALU.bypass, replica_groups=AGALL,
        ins=[w1qT0_o.tensor.ap().opt()], outs=[w1qT0_h.tensor.ap().opt()]))
    cc_l0 = sCC.inc(G_.collective_compute(
        "AllGather", ALU.bypass, replica_groups=AGALL,
        ins=[w2qT0_o.tensor.ap().opt()], outs=[w2qT0_h.tensor.ap().opt()]))
    with nc.sbuf_tensor("amr", [128, 2], F32) as amr:
        G_.wait_ge(sCC.h, n_ar)
        nl = dma(G_, dGL, amr[:], am_red.tensor.ap())
        V.wait_ge(dGL.h, nl)
        V.tensor_single_scalar(amr[:], amr[:], 1.0 / (DFF * D), op=ALU.mult)
        V.drain()
        V.tensor_single_scalar(amr[:], amr[:], 1e-8, op=ALU.max)
        V.drain()
        V.reciprocal(inv0[:], amr[:, 0:1])
        V.drain()
        n = sV.inc(V.reciprocal(inv1[:], amr[:, 1:2]))
        V.wait_ge(sV.h, n)
    barrier()
    emit_quant_spill(w1s1, w1qT1_o, DFF // 8, D, True, inv0[:], "qc")
    emit_quant_spill(w2s1, w2qT1_o, D, DFF // 8, True, inv1[:], "qd")
    G_.wait_ge(dQe0.h, dQe0.n); G_.wait_ge(dQe1.h, dQe1.n)
    sCC.inc(G_.collective_compute(
        "AllGather", ALU.bypass, replica_groups=AGALL,
        ins=[w1qT1_o.tensor.ap().opt()], outs=[w1qT1_h.tensor.ap().opt()]))
    cc_l1 = sCC.inc(G_.collective_compute(
        "AllGather", ALU.bypass, replica_groups=AGALL,
        ins=[w2qT1_o.tensor.ap().opt()], outs=[w2qT1_h.tensor.ap().opt()]))
    barrier()

    with nc.sbuf_tensor("xnT", [128, SL * D], F32) as xnT, \
         nc.sbuf_tensor("ta", [128, D], F32) as ta, \
         nc.sbuf_tensor("tb", [128, D], F32) as tb, \
         nc.sbuf_tensor("tc", [128, D], F32) as tc, \
         nc.sbuf_tensor("wb0", [128, DCH * 512], F32) as wb0, \
         nc.sbuf_tensor("wb1", [128, DCH * 512], F32) as wb1, \
         nc.sbuf_tensor("ev0", [128, 512], F32) as ev0, \
         nc.sbuf_tensor("ev1", [128, 512], F32) as ev1, \
         nc.sbuf_tensor("cq", [128, SL * 64], F32) as cq, \
         nc.sbuf_tensor("sq", [128, SL * 64], F32) as sq, \
         nc.sbuf_tensor("ck", [128, SL * 64], F32) as ck, \
         nc.sbuf_tensor("sk", [128, SL * 64], F32) as sk:
        dma(SY, dX, cq[:], cosqh); dma(SY, dX, sq[:], sinqh)
        dma(SY, dX, ck[:], coskh); dma(SY, dX, sk[:], sinkh)
        V.wait_ge(dX.h, dX.n)
        vack()
        for j in range(SL):
            SY.wait_ge(sV.h, sV.n)
            nld = dma(SY, dX, ta[:], stream_h[j * 128:(j + 1) * 128, :])
            V.wait_ge(dX.h, nld)
            nln = emit_ln(ta[:], D, tb[:], tc[:], stat[:, 4:5], stat[:, 5:6])
            emit_transposes(tc[:], 0, xnT, j * DCH, DCH, False, nsrc=(sV, nln))

        def rope_spill(kind, oc, j):
            ct = (cq if kind == 'q' else ck)[:, j * 64:(j + 1) * 64]
            st_ = (sq if kind == 'q' else sk)[:, j * 64:(j + 1) * 64]
            cb = ct.unsqueeze(1).broadcast_to([128, 4, 64])
            sb_ = st_.unsqueeze(1).broadcast_to([128, 4, 64])
            e3 = ev0[:, 0:512].rearrange("p (h c) -> p h c", c=128)
            o3 = ev1[:, 0:512].rearrange("p (h c) -> p h c", c=128)
            x1 = e3[:, :, 0:64]; x2 = e3[:, :, 64:128]
            t1 = ta[:, 0:256].rearrange("p (h c) -> p h c", c=64)
            t2 = ta[:, 256:512].rearrange("p (h c) -> p h c", c=64)
            guard(V, "ev1")
            V.tensor_tensor(t1, x1, cb, op=ALU.mult)
            V.tensor_tensor(t2, x2, sb_, op=ALU.mult)
            V.drain()
            V.tensor_tensor(o3[:, :, 0:64], t1, t2, op=ALU.subtract)
            V.drain()
            V.tensor_tensor(t1, x2, cb, op=ALU.mult)
            V.tensor_tensor(t2, x1, sb_, op=ALU.mult)
            V.drain()
            n = sV.inc(V.tensor_tensor(o3[:, :, 64:128], t1, t2, op=ALU.add))
            P.wait_ge(sV.h, n)
            guard(P, "psA")
            for c in range(4):
                r = sP.inc(P.transpose(psA_[:, c * 128:(c + 1) * 128],
                                       ev1[:, c * 128:(c + 1) * 128], ident[:]))
            n = r
            note("ev1", sP, n)
            V.wait_ge(sP.h, n)
            guard(V, "ev0")
            n = sV.inc(V.tensor_copy(ev0[:, 0:512], psA_[:]))
            note("psA", sV, n)
            G_.wait_ge(sV.h, n)
            nst = spill_T(ev0[:, 0:512], 4, qT_h if kind == 'q' else kT0o, oc * 4, j * 128)
            note("ev0", dG, nst)

        wl = 0
        for wT_hbm, nout, kind in ((wqT, D, 'q'), (wkT, DK0, 'k'), (wvT, DK0, 'v')):
            for oc in range(nout // 512):
                wb, wbn = ((wb0, "wb0"), (wb1, "wb1"))[wl % 2]
                dwp = (dwA, dwB)[wl % 2]; wl += 1
                guard(SY, wbn)
                src = wT_hbm.rearrange("(dc p) n -> p dc n", p=128)[:, :, oc * 512:(oc + 1) * 512]
                wsem = dma(SY, dwp, wb[:].rearrange("p (dc n) -> p dc n", n=512), src)
                for j in range(SL):
                    psu, psn = ((psH0, "psH0"), (psH1, "psH1"))[j % 2]
                    P.wait_ge(dwp.h, wsem)
                    guard(P, psn)
                    for dc in range(DCH):
                        r = P.matmul(psu[:], xnT[:, (j * DCH + dc) * 128:(j * DCH + dc + 1) * 128],
                                     wb[:, dc * 512:(dc + 1) * 512],
                                     start=(dc == 0), stop=(dc == DCH - 1))
                    n = sP.inc(r)
                    note(wbn, sP, n)
                    V.wait_ge(sP.h, n)
                    guard(V, "ev0")
                    n = sV.inc(V.tensor_copy(ev0[:, 0:512], psu[:]))
                    note(psn, sV, n)
                    if kind == 'v':
                        G_.wait_ge(sV.h, n)
                        nst = dma(G_, dSv, v0o[j * 128:(j + 1) * 128, oc * 512:(oc + 1) * 512], ev0[:, 0:512])
                        note("ev0", dSv, nst)
                    else:
                        V.wait_ge(sV.h, n)
                        rope_spill(kind, oc, j)
        barrier()

        sCC.inc(G_.collective_compute(
            "AllGather", ALU.bypass, replica_groups=[[0, 1], [2, 3], [4, 5], [6, 7]],
            ins=[kT0o.tensor.ap().opt()], outs=[kT0g.tensor.ap().opt()]))
        sCC.inc(G_.collective_compute(
            "AllGather", ALU.bypass, replica_groups=[[0, 1], [2, 3], [4, 5], [6, 7]],
            ins=[v0o.tensor.ap().opt()], outs=[v0g.tensor.ap().opt()]))
        G_.wait_ge(sCC.h, sCC.n)
        barrier()

    with nc.sbuf_tensor("kbuf", [128, HK0 * 2048], F32) as kbuf, \
         nc.sbuf_tensor("vbuf", [128, 16 * DK0], F32) as vbuf, \
         nc.sbuf_tensor("qTs", [128, D], F32) as qTs, \
         nc.sbuf_tensor("Pb", [128, 2048], F32) as Pb, \
         nc.sbuf_tensor("PTb", [128, 2048], F32) as PTb, \
         nc.sbuf_tensor("Ot", [128, D], F32) as Ot, \
         nc.sbuf_tensor("oTe", [128, 512], F32) as oTe:
        for j in range(SL):
            nkt = 2 * j + 2
            nk = nkt * 128
            SY.wait_ge(sP.h, sP.n)
            qsrc = qT_h.rearrange("(dc p) t -> p dc t", p=128)[:, 0:DCH, j * 128:(j + 1) * 128]
            qsem = dma(SY, dX, qTs[:].rearrange("p (dc t) -> p dc t", t=128), qsrc)
            for kt in range(nkt):
                r_ = owner_rank(kt); sl = owner_slot(kt)
                ksrc = kT0g.rearrange("(c p) t -> p c t", p=128)[
                    :, r_ * HK0:(r_ + 1) * HK0, sl * 128:(sl + 1) * 128]
                kdst = kbuf[:].rearrange("p (c n) -> p c n", n=2048)[:, :, kt * 128:(kt + 1) * 128]
                dma(SY, dK, kdst, ksrc)
                vsrc = v0g[(r_ * TOK + sl * 128):(r_ * TOK + sl * 128 + 128), :]
                dma(SY, dK, vbuf[:, kt * DK0:(kt + 1) * DK0], vsrc)
            ldsem = dK.n
            for h in range(H0):
                kvh = h // GRP
                P.wait_ge(dK.h, ldsem)
                P.wait_ge(dX.h, qsem)
                P.wait_ge(sA.h, sA.n)
                P.wait_ge(sV.h, sV.n)
                for c in range((nk + 511) // 512):
                    w = min(512, nk - c * 512)
                    r = P.matmul(psBIG[:, c * 512:c * 512 + w],
                                 qTs[:, h * 128:(h + 1) * 128],
                                 kbuf[:, kvh * 2048 + c * 512: kvh * 2048 + c * 512 + w],
                                 start=True, stop=True)
                npe = sP.inc(r)
                V.wait_ge(sP.h, npe)
                V.tensor_tensor(psBIG[:, nk - 256:nk], psBIG[:, nk - 256:nk],
                                masks[:, j * 256:(j + 1) * 256], op=ALU.add)
                V.drain()
                n = sV.inc(V.reduce_max(negm[:], psBIG[:, 0:nk], axis=AX.X, negate=True))
                A.wait_ge(sV.h, n)
                na = sA.inc(A.activation(Pb[:, 0:nk], psBIG[:, 0:nk], AF.Exp,
                                         bias=negm[:], accum_out=ssum[:]))
                V.wait_ge(sA.h, na)
                n = sV.inc(V.reciprocal(rsum[:], ssum[:]))
                emit_transposes(Pb[:], 0, PTb, 0, nkt, False, nsrc=(sA, na))
                P.wait_ge(sV.h, sV.n)
                for kt in range(nkt):
                    r = P.matmul(psO[:, 0:128], PTb[:, kt * 128:(kt + 1) * 128],
                                 vbuf[:, kt * DK0 + kvh * 128: kt * DK0 + (kvh + 1) * 128],
                                 start=(kt == 0), stop=(kt == nkt - 1))
                n = sP.inc(r)
                V.wait_ge(sP.h, n)
                sV.inc(V.tensor_scalar(Ot[:, h * 128:(h + 1) * 128], psO[:, 0:128],
                                       rsum[:], None, op0=ALU.mult))
            done = 0
            while done < DCH:
                nb = min(4, DCH - done)
                P.wait_ge(sV.h, sV.n)
                for c in range(nb):
                    r = P.transpose(psA_[:, c * 128:(c + 1) * 128],
                                    Ot[:, (done + c) * 128:(done + c + 1) * 128], ident[:])
                n = sP.inc(r)
                V.wait_ge(sP.h, n)
                guard(V, "oTe")
                n = sV.inc(V.tensor_copy(oTe[:, 0:nb * 128], psA_[:, 0:nb * 128]))
                G_.wait_ge(sV.h, n)
                nst = spill_T(oTe[:, 0:nb * 128], nb, oT_h, done, j * 128)
                note("oTe", dG, nst)
                done += nb
        barrier()

    _opctr = [0]

    def emit_out_proj(wT_hbm, nin_ch, resid_fn):
        _opctr[0] += 1
        tg = str(_opctr[0])
        with nc.sbuf_tensor("pwb0" + tg, [128, nin_ch * 512], F32) as pwb0, \
             nc.sbuf_tensor("pwb1" + tg, [128, nin_ch * 512], F32) as pwb1, \
             nc.sbuf_tensor("poT0" + tg, [128, nin_ch * 128], F32) as poT0, \
             nc.sbuf_tensor("poT1" + tg, [128, nin_ch * 128], F32) as poT1, \
             nc.sbuf_tensor("prt0" + tg, [128, 512], F32) as prt0, \
             nc.sbuf_tensor("prt1" + tg, [128, 512], F32) as prt1, \
             nc.sbuf_tensor("prt2" + tg, [128, 512], F32) as prt2:
            for oc in range(D // 512):
                wb, wbn = ((pwb0, "pwb0"), (pwb1, "pwb1"))[oc % 2]
                dwp = (dwA, dwB)[oc % 2]
                guard(SY, wbn)
                src = wT_hbm.rearrange("(dc p) n -> p dc n", p=128)[:, :, oc * 512:(oc + 1) * 512]
                wsem = dma(SY, dwp, wb[:].rearrange("p (dc n) -> p dc n", n=512), src)
                for j in range(SL):
                    poT, poTn, dxp = ((poT0, "poT0", dXA), (poT1, "poT1", dXB))[j % 2]
                    guard(SY, poTn)
                    xsrc = oT_h.rearrange("(dc p) t -> p dc t", p=128)[:, 0:nin_ch, j * 128:(j + 1) * 128]
                    nx = dma(SY, dxp, poT[:].rearrange("p (dc t) -> p dc t", t=128), xsrc)
                    psu, psn = ((psH0, "psH0"), (psH1, "psH1"))[j % 2]
                    P.wait_ge(dwp.h, wsem)
                    P.wait_ge(dxp.h, nx)
                    guard(P, psn)
                    for dc in range(nin_ch):
                        r = P.matmul(psu[:], poT[:, dc * 128:(dc + 1) * 128],
                                     wb[:, dc * 512:(dc + 1) * 512],
                                     start=(dc == 0), stop=(dc == nin_ch - 1))
                    npe = sP.inc(r)
                    note(wbn, sP, npe)
                    note(poTn, sP, npe)
                    resid_fn(j, oc, psu, npe, prt0, prt1, prt2)
            barrier()

    def resid_l0(j, oc, psu, npe, rt0, rt1, rt2):
        G_.wait_ge(sV.h, sV.n)
        nl = dma(G_, dGL, rt0[:], stream_h[j * 128:(j + 1) * 128, oc * 512:(oc + 1) * 512])
        V.wait_ge(dGL.h, nl)
        V.wait_ge(sP.h, npe)
        guard(V, "rt1")
        n = sV.inc(V.tensor_tensor(rt1[:], rt0[:], psu[:], op=ALU.add))
        note("psH0" if psu is psH0 else "psH1", sV, n)
        G_.wait_ge(sV.h, n)
        nst = dma(G_, dG, stream_h[j * 128:(j + 1) * 128, oc * 512:(oc + 1) * 512], rt1[:])
        note("rt1", dG, nst)

    emit_out_proj(wpT, DCH, resid_l0)

    if cfg.debug:
        G_.wait_ge(dG.h, dG.n)
        dma(G_, dG, dbg["dbg_s1"], stream_h)
        barrier()

    _ffnctr = [0]

    def emit_ffn(w1qT_hbm, w2qT_hbm, b1_hbm, b2_hbm, xT_list, l1, ntt, out_fn, fsup, wait_cc):
        nfs = DFF // fsup
        nfg = fsup // 512
        assert fsup == NF8 * 128
        xis = (0, 1) if l1 else (0,)
        _ffnctr[0] += 1
        tg = str(_ffnctr[0])
        fes = ExitStack()
        fsb = lambda name, shape, dt=F32: fes.enter_context(nc.sbuf_tensor(name + tg, shape, dt))
        h2acc = fsb("fh2", [128, ntt * D])
        w1b = fsb("fw1", [128, DCH * fsup], BF16)
        w2b = fsb("fw2", [128, (fsup // 128) * D], BF16)
        fqT = fsb("fqT", [128, 8 * 128], BF16)
        fvt = fsb("fvt", [128, 512])
        if l1:
            fgf = fsb("fgf", [128, 512])
            fgh = fsb("fgh", [128, 512], BF16)
            fgl = fsb("fgl", [128, 512], BF16)
            fg0 = fg1 = fq2 = fam = fex = fsc = fiv = None
        else:
            fg0 = fsb("fg0", [128, 512], BF16)
            fg1 = fsb("fg1", [128, 512], BF16)
            fq2 = fsb("fq2", [128, 512], BF16)
            fam = fsb("fam", [128, 16])
            fex = fsb("fex", [128, 16], I32)
            fsc = fsb("fsc", [128, 16])
            fiv = fsb("fiv", [128, 16])
            fgf = fgh = fgl = None
        with fes:
            nbb2 = dma(SY, dX, bb2[0:1, :], b2_hbm[0:1, :])
            SY.wait_ge(sCC.h, wait_cc)
            for fs in range(nfs):
                guard(SY, "fw1")
                guard(SY, "fw2")
                nfc = fsup // 128
                w1t = w1qT_hbm.rearrange("(r dc fc) (p t) -> p r dc fc t", dc=DCH, fc=NF8, t=128)
                for dc in range(DCH):
                    dma(SY, dwF, w1b[:, dc * fsup:(dc + 1) * fsup].rearrange("p (fc t) -> p fc t", t=128),
                        w1t[:, fs, dc, :, :])
                w2t = w2qT_hbm.rearrange("(r fc dm) (p t) -> p r fc dm t", fc=NF8, dm=DCH, t=128)
                for fc in range(nfc):
                    dma(SY, dwF, w2b[:, fc * D:(fc + 1) * D].rearrange("p (dm t) -> p dm t", t=128),
                        w2t[:, fs, fc, :, :])
                dma(SY, dwF, bb1[0:1, 0:fsup], b1_hbm[0:1, fs * fsup:(fs + 1) * fsup])
                wsem = dwF.n
                for tt in range(ntt):
                    P.wait_ge(dwF.h, wsem)
                    P.wait_ge(dX.h, nbb2)
                    if fs == 0:
                        guard(P, "psBIG")
                        for dmc in range(NDM):
                            P.matmul(psBIG[:, dmc * 512:(dmc + 1) * 512], ones1[:],
                                     bb2[0:1, dmc * 512:(dmc + 1) * 512], start=True, stop=False)
                    def emit_h(fg):
                        psu, psn = ((psH0, "psH0"), (psH1, "psH1"))[fg % 2]
                        guard(P, psn)
                        P.matmul(psu[:], ones1[:], bb1[0:1, fg * 512:(fg + 1) * 512],
                                 start=True, stop=False)
                        for dc in range(DCH):
                            for xi in xis:
                                r = P.matmul(
                                    psu[:],
                                    xT_list[xi][:, (tt * DCH + dc) * 128:(tt * DCH + dc + 1) * 128],
                                    w1b[:, dc * fsup + fg * 512: dc * fsup + fg * 512 + 512],
                                    start=False, stop=(dc == DCH - 1 and xi == xis[-1]))
                        return sP.inc(r)

                    def emit_gchain(fg, npe):
                        psu, psn = ((psH0, "psH0"), (psH1, "psH1"))[fg % 2]
                        V.wait_ge(sP.h, npe)
                        if not l1:
                            guard(V, "fg0")
                            n = sV.inc(V.tensor_copy(fg0[:], psu[:]))
                            note(psn, sV, n)
                            A.wait_ge(sV.h, n)
                            na = sA.inc(A.activation(fg1[:], fg0[:], AF.Gelu_apprx_tanh))
                            note("fg0", sA, na)
                            V.wait_ge(sA.h, na)
                            guard(V, "fq2")
                            nq2 = emit_mx_quant(fg1[:], fq2[:], 512, fam, fex, fsc, fiv, fvt[:])
                            guard(V, "fqT")
                            emit_transposes(fq2[:], 0, fqT, 0, 4, True, nsrc=(sV, nq2))
                            note("fq2", sP, sP.n)
                        else:
                            guard(V, "fgf")
                            n = sV.inc(V.tensor_copy(fgf[:], psu[:]))
                            note(psn, sV, n)
                            A.wait_ge(sV.h, n)
                            na = sA.inc(A.activation(fgf[:], fgf[:], AF.Gelu))
                            V.wait_ge(sA.h, na)
                            sV.inc(V.tensor_copy(fgh[:], fgf[:]))
                            V.drain()
                            V.tensor_tensor(fvt[:], fgf[:], fgh[:], op=ALU.subtract)
                            V.drain()
                            n = sV.inc(V.tensor_copy(fgl[:], fvt[:]))
                            note("fgf", sV, n)
                            guard(V, "fqT")
                            emit_transposes(fgh[:], 0, fqT, 0, 4, True, nsrc=(sV, n))
                            emit_transposes(fgl[:], 0, fqT, 4, 4, True)
                        return sV.n

                    def emit_w2(fg, nq):
                        P.wait_ge(sV.h, nq)
                        for dmc in range(NDM):
                            for sub in range(4):
                                fc = fg * 4 + sub
                                for xi in xis:
                                    first = (fs > 0 and fg == 0 and sub == 0 and xi == 0)
                                    last = (fg == nfg - 1 and sub == 3 and xi == xis[-1])
                                    r = P.matmul(
                                        psBIG[:, dmc * 512:(dmc + 1) * 512],
                                        fqT[:, (xi * 4 + sub) * 128:(xi * 4 + sub + 1) * 128],
                                        w2b[:, fc * D + dmc * 512: fc * D + dmc * 512 + 512],
                                        start=first, stop=last)
                        n = sP.inc(r)
                        note("fqT", sP, n)
                        return n

                    npe = emit_h(0)
                    nq = emit_gchain(0, npe)
                    for fg in range(1, nfg):
                        npe = emit_h(fg)
                        emit_w2(fg - 1, nq)
                        nq = emit_gchain(fg, npe)
                    if tt == ntt - 1:
                        note("fw1", sP, npe)
                    nw2 = emit_w2(nfg - 1, nq)
                    if tt == ntt - 1:
                        note("fw2", sP, nw2)
                        note("fbb1", sP, nw2)
                    V.wait_ge(sP.h, sP.n)
                    vsync()
                    if fs == 0:
                        n = sV.inc(V.tensor_copy(h2acc[:, tt * D:(tt + 1) * D], psBIG[:, 0:D]))
                    else:
                        n = sV.inc(V.tensor_tensor(h2acc[:, tt * D:(tt + 1) * D],
                                                   h2acc[:, tt * D:(tt + 1) * D],
                                                   psBIG[:, 0:D], op=ALU.add))
                    note("psBIG", sV, n)
            for tt in range(ntt):
                out_fn(tt, h2acc)
            barrier()

    with nc.sbuf_tensor("q1T", [128, SL * D], BF16) as q1T:
        with nc.sbuf_tensor("f2a", [128, D], F32) as f2a, \
             nc.sbuf_tensor("f2b", [128, D], F32) as f2b, \
             nc.sbuf_tensor("f2c", [128, D], F32) as f2c, \
             nc.sbuf_tensor("xbt", [128, D], BF16) as xbt, \
             nc.sbuf_tensor("q1b", [128, D], BF16) as q1b, \
             nc.sbuf_tensor("lnw", [128, D], F32) as lnw, \
             nc.sbuf_tensor("lnb", [128, D], F32) as lnb, \
             nc.sbuf_tensor("am2", [128, D // 32], F32) as am2, \
             nc.sbuf_tensor("ex2", [128, D // 32], I32) as ex2, \
             nc.sbuf_tensor("sc2", [128, D // 32], F32) as sc2, \
             nc.sbuf_tensor("iv2", [128, D // 32], F32) as iv2:
            dma(SY, dX, lnw[:], lnwh)
            nl = dma(SY, dX, lnb[:], lnbh)
            V.wait_ge(dX.h, nl)
            vack()
            for j in range(SL):
                SY.wait_ge(sV.h, sV.n)
                nl = dma(SY, dX, f2a[:], stream_h[j * 128:(j + 1) * 128, :])
                V.wait_ge(dX.h, nl)
                emit_ln(f2a[:], D, f2b[:], f2c[:], stat[:, 4:5], stat[:, 5:6])
                guard(V, "xbt")
                V.tensor_copy(xbt[:], f2c[:])
                V.drain()
                n = sV.inc(V.tensor_copy(f2c[:], xbt[:]))
                V.wait_ge(sV.h, n)
                G_.wait_ge(sV.h, n)
                nst = dma(G_, dG, xb_h[j * 128:(j + 1) * 128, :], xbt[:])
                note("xbt", dG, nst)
                emit_ln(f2c[:], D, f2a[:], f2b[:], stat[:, 6:7], stat[:, 7:8])
                V.tensor_tensor(f2b[:], f2b[:], lnw[:], op=ALU.mult)
                V.drain()
                V.tensor_tensor(f2b[:], f2b[:], lnb[:], op=ALU.add)
                V.drain()
                V.tensor_copy(q1b[:], f2b[:])
                V.drain()
                V.tensor_copy(f2b[:], q1b[:])
                V.drain()
                nq1 = emit_mx_quant(f2b[:], q1b[:], D, am2, ex2, sc2, iv2, f2a[:])
                emit_transposes(q1b[:], 0, q1T, j * DCH, DCH, True, nsrc=(sV, nq1))
            barrier()

        with nc.sbuf_tensor("o2a", [128, D], BF16) as o2a, \
             nc.sbuf_tensor("o2b", [128, D], BF16) as o2b, \
             nc.sbuf_tensor("o2c", [128, D], F32) as o2c:
            def out_l0(tt, h2acc):
                vsync()
                SY.wait_ge(sG.h, gack())
                nl = dma(SY, dX, o2a[:], xb_h[tt * 128:(tt + 1) * 128, :])
                nl2 = dma(SY, dX, o2c[:], stream_h[tt * 128:(tt + 1) * 128, :])
                V.wait_ge(dX.h, nl2)
                V.tensor_copy(o2b[:], h2acc[:, tt * D:(tt + 1) * D])
                V.drain()
                V.tensor_tensor(o2b[:], o2a[:], o2b[:], op=ALU.add)
                V.drain()
                n = sV.inc(V.tensor_tensor(o2c[:], o2c[:], o2b[:], op=ALU.add))
                G_.wait_ge(sV.h, n)
                dma(G_, dG, stream_h[tt * 128:(tt + 1) * 128, :], o2c[:])

            emit_ffn(w1qT0_h, w2qT0_h, b1h0, b2h0, [q1T], False, SL, out_l0, DFF // 8, cc_l0)

    if cfg.debug:
        G_.wait_ge(dG.h, dG.n)
        dma(G_, dG, dbg["dbg_s2"], stream_h)
        barrier()

    with nc.sbuf_tensor("xnT3", [128, SL * D], F32) as xnT3, \
         nc.sbuf_tensor("t3a", [128, D], F32) as t3a, \
         nc.sbuf_tensor("t3b", [128, D], F32) as t3b, \
         nc.sbuf_tensor("t3c", [128, D], F32) as t3c, \
         nc.sbuf_tensor("gam", [128, D], F32) as gam, \
         nc.sbuf_tensor("bet", [128, D], F32) as bet, \
         nc.sbuf_tensor("wb30", [128, DCH * 512], F32) as wb30, \
         nc.sbuf_tensor("wb31", [128, DCH * 512], F32) as wb31, \
         nc.sbuf_tensor("ev30", [128, 512], F32) as ev30, \
         nc.sbuf_tensor("ev31", [128, 512], F32) as ev31:
        dma(SY, dX, gam[:], gamh)
        nl = dma(SY, dX, bet[:], beth)
        V.wait_ge(dX.h, nl)
        vack()
        for j in range(SL):
            SY.wait_ge(sV.h, sV.n)
            nl = dma(SY, dX, t3a[:], stream_h[j * 128:(j + 1) * 128, :])
            V.wait_ge(dX.h, nl)
            emit_ln(t3a[:], D, t3b[:], t3c[:], mu3[:, j:j + 1], rstd3[:, j:j + 1])
            emit_ln(t3c[:], D, t3a[:], t3b[:], stat[:, 4:5], stat[:, 5:6])
            V.tensor_tensor(t3b[:], t3b[:], gam[:], op=ALU.mult)
            V.drain()
            n = sV.inc(V.tensor_tensor(t3b[:], t3b[:], bet[:], op=ALU.add))
            V.wait_ge(sV.h, n)
            emit_transposes(t3b[:], 0, xnT3, j * DCH, DCH, False, nsrc=(sV, n))
        wl = 0
        for sec, base in (('q', 0), ('k', KEY), ('v', 2 * KEY)):
            for oc in range(KEY // 512):
                wb, wbn = ((wb30, "wb30"), (wb31, "wb31"))[wl % 2]
                dwp = (dwA, dwB)[wl % 2]; wl += 1
                guard(SY, wbn)
                src = qkvw.rearrange("(dc p) n -> p dc n", p=128)[:, :, base + oc * 512: base + (oc + 1) * 512]
                wsem = dma(SY, dwp, wb[:].rearrange("p (dc n) -> p dc n", n=512), src)
                for j in range(SL):
                    psu, psn = ((psH0, "psH0"), (psH1, "psH1"))[j % 2]
                    P.wait_ge(dwp.h, wsem)
                    guard(P, psn)
                    for dc in range(DCH):
                        r = P.matmul(psu[:], xnT3[:, (j * DCH + dc) * 128:(j * DCH + dc + 1) * 128],
                                     wb[:, dc * 512:(dc + 1) * 512],
                                     start=(dc == 0), stop=(dc == DCH - 1))
                    n = sP.inc(r)
                    note(wbn, sP, n)
                    V.wait_ge(sP.h, n)
                    guard(V, "ev30")
                    if sec == 'q':
                        n = sV.inc(V.tensor_single_scalar(ev30[:, 0:512], psu[:], SC1, op=ALU.mult))
                    else:
                        n = sV.inc(V.tensor_copy(ev30[:, 0:512], psu[:]))
                    note(psn, sV, n)
                    if sec == 'v':
                        G_.wait_ge(sV.h, n)
                        nst = dma(G_, dSv, v1o[j * 128:(j + 1) * 128, oc * 512:(oc + 1) * 512], ev30[:, 0:512])
                        note("ev30", dSv, nst)
                    else:
                        V.wait_ge(sV.h, n)
                        guard(V, "ev31")
                        emit_transposes(ev30[:, 0:512], 0, ev31, 0, 4, False, nsrc=(sV, n))
                        note("ev30", sP, sP.n)
                        G_.wait_ge(sV.h, sV.n)
                        nst = spill_T(ev31[:, 0:512], 4, qT_h if sec == 'q' else kT1o, oc * 4, j * 128)
                        note("ev31", dG, nst)
        barrier()
        sCC.inc(G_.collective_compute(
            "AllGather", ALU.bypass, replica_groups=[[0, 1], [2, 3], [4, 5], [6, 7]],
            ins=[kT1o.tensor.ap().opt()], outs=[kT1g.tensor.ap().opt()]))
        sCC.inc(G_.collective_compute(
            "AllGather", ALU.bypass, replica_groups=[[0, 1], [2, 3], [4, 5], [6, 7]],
            ins=[v1o.tensor.ap().opt()], outs=[v1g.tensor.ap().opt()]))
        G_.wait_ge(sCC.h, sCC.n)
        barrier()

    with nc.sbuf_tensor("kb3a", [128, DCH * 512], F32) as kb3a, \
         nc.sbuf_tensor("kb3b", [128, DCH * 512], F32) as kb3b, \
         nc.sbuf_tensor("vb3", [128, 4 * KEY], F32) as vb3, \
         nc.sbuf_tensor("qT3", [128, D], F32) as qT3, \
         nc.sbuf_tensor("Pb3", [128, 2048], F32) as Pb3, \
         nc.sbuf_tensor("PT3", [128, 2048], F32) as PT3, \
         nc.sbuf_tensor("Ot3", [128, D], F32) as Ot3, \
         nc.sbuf_tensor("oTe3", [128, 512], F32) as oTe3:
        for j in range(SL):
            nkt = 2 * j + 2
            nk = nkt * 128
            SY.wait_ge(sP.h, sP.n)
            qsrc = qT_h.rearrange("(dc p) t -> p dc t", p=128)[:, 0:DCH, j * 128:(j + 1) * 128]
            qsem = dma(SY, dX, qT3[:].rearrange("p (dc t) -> p dc t", t=128), qsrc)
            for c in range((nk + 511) // 512):
                tiles = range(c * 4, min(c * 4 + 4, nkt))
                kb3, kbn, dkp = ((kb3a, "kb3a", dXA), (kb3b, "kb3b", dXB))[c % 2]
                guard(SY, kbn)
                for kt in tiles:
                    r_ = owner_rank(kt); sl = owner_slot(kt)
                    ksrc = kT1g.rearrange("(dc p) t -> p dc t", p=128)[
                        :, r_ * DCH:(r_ + 1) * DCH, sl * 128:(sl + 1) * 128]
                    kdst = kb3[:].rearrange("p (dc n) -> p dc n", n=512)[:, :, (kt - c * 4) * 128:(kt - c * 4 + 1) * 128]
                    dma(SY, dkp, kdst, ksrc)
                w = min(512, nk - c * 512)
                P.wait_ge(dkp.h, dkp.n)
                P.wait_ge(dX.h, qsem)
                P.wait_ge(sA.h, sA.n)
                P.wait_ge(sV.h, sV.n)
                for dc in range(DCH):
                    r = P.matmul(psBIG[:, c * 512:c * 512 + w],
                                 qT3[:, dc * 128:(dc + 1) * 128],
                                 kb3[:, dc * 512:dc * 512 + w],
                                 start=(dc == 0), stop=(dc == DCH - 1))
                npe = sP.inc(r)
                note(kbn, sP, npe)
            V.wait_ge(sP.h, sP.n)
            V.tensor_tensor(psBIG[:, nk - 256:nk], psBIG[:, nk - 256:nk],
                            masks[:, j * 256:(j + 1) * 256], op=ALU.add)
            V.drain()
            n = sV.inc(V.reduce_max(negm[:], psBIG[:, 0:nk], axis=AX.X, negate=True))
            A.wait_ge(sV.h, n)
            na = sA.inc(A.activation(Pb3[:, 0:nk], psBIG[:, 0:nk], AF.Exp,
                                     bias=negm[:], accum_out=ssum[:]))
            V.wait_ge(sA.h, na)
            n = sV.inc(V.reciprocal(rsum[:], ssum[:]))
            emit_transposes(Pb3[:], 0, PT3, 0, nkt, False, nsrc=(sA, na))
            for c in range((nk + 511) // 512):
                tiles = list(range(c * 4, min(c * 4 + 4, nkt)))
                SY.wait_ge(sP.h, sP.n)
                for i, kt in enumerate(tiles):
                    r_ = owner_rank(kt); sl = owner_slot(kt)
                    vsrc = v1g[(r_ * TOK + sl * 128):(r_ * TOK + sl * 128 + 128), :]
                    dma(SY, dK, vb3[:, i * KEY:(i + 1) * KEY], vsrc)
                P.wait_ge(dK.h, dK.n)
                if c == 0:
                    P.wait_ge(sA.h, sA.n)
                    P.wait_ge(sV.h, sV.n)
                for dmc in range(NDM):
                    for i, kt in enumerate(tiles):
                        r = P.matmul(psBIG[:, dmc * 512:(dmc + 1) * 512],
                                     PT3[:, kt * 128:(kt + 1) * 128],
                                     vb3[:, i * KEY + dmc * 512: i * KEY + dmc * 512 + 512],
                                     start=(kt == 0), stop=(kt == nkt - 1))
                sP.inc(r)
            n = sP.n
            V.wait_ge(sP.h, n)
            for dmc in range(NDM):
                sV.inc(V.tensor_scalar(Ot3[:, dmc * 512:(dmc + 1) * 512],
                                       psBIG[:, dmc * 512:(dmc + 1) * 512],
                                       rsum[:], None, op0=ALU.mult))
            done = 0
            while done < DCH:
                nb = min(4, DCH - done)
                P.wait_ge(sV.h, sV.n)
                for c2 in range(nb):
                    r = P.transpose(psA_[:, c2 * 128:(c2 + 1) * 128],
                                    Ot3[:, (done + c2) * 128:(done + c2 + 1) * 128], ident[:])
                n = sP.inc(r)
                V.wait_ge(sP.h, n)
                guard(V, "oTe3")
                n = sV.inc(V.tensor_copy(oTe3[:, 0:nb * 128], psA_[:, 0:nb * 128]))
                G_.wait_ge(sV.h, n)
                nst = spill_T(oTe3[:, 0:nb * 128], nb, oT_h, done, j * 128)
                note("oTe3", dG, nst)
                done += nb
        barrier()

    def resid_l1(j, oc, psu, npe, rt0, rt1, rt2):
        G_.wait_ge(sV.h, sV.n)
        nl = dma(G_, dGL, rt0[:], stream_h[j * 128:(j + 1) * 128, oc * 512:(oc + 1) * 512])
        V.wait_ge(dGL.h, nl)
        V.wait_ge(sP.h, npe)
        V.tensor_scalar(rt2[:], rt0[:], mu3[:, j:j + 1], None, op0=ALU.subtract)
        V.drain()
        V.tensor_scalar(rt2[:], rt2[:], rstd3[:, j:j + 1], None, op0=ALU.mult)
        V.drain()
        guard(V, "rt1")
        nps = sV.inc(V.tensor_tensor(rt1[:], rt2[:], psu[:], op=ALU.add))
        note("psH0" if psu is psH0 else "psH1", sV, nps)
        V.wait_ge(sV.h, nps)
        n = sV.inc(V.tensor_tensor(rt1[:], rt1[:], rt0[:], op=ALU.add))
        G_.wait_ge(sV.h, n)
        nst = dma(G_, dG, stream_h[j * 128:(j + 1) * 128, oc * 512:(oc + 1) * 512], rt1[:])
        note("rt1", dG, nst)

    emit_out_proj(oprw, DCH, resid_l1)

    if cfg.debug:
        G_.wait_ge(dG.h, dG.n)
        dma(G_, dG, dbg["dbg_s3"], stream_h)
        barrier()

    HT = SL // 2 if SL >= 2 else 1
    for half in range((SL + HT - 1) // HT):
        tts = list(range(half * HT, min((half + 1) * HT, SL)))
        with nc.sbuf_tensor("xhiT%d" % half, [128, len(tts) * D], BF16) as xhiT, \
             nc.sbuf_tensor("xloT%d" % half, [128, len(tts) * D], BF16) as xloT:
            with nc.sbuf_tensor("f4a%d" % half, [128, D], F32) as f4a, \
                 nc.sbuf_tensor("f4b%d" % half, [128, D], F32) as f4b, \
                 nc.sbuf_tensor("f4c%d" % half, [128, D], F32) as f4c, \
                 nc.sbuf_tensor("xhib%d" % half, [128, D], BF16) as xhib, \
                 nc.sbuf_tensor("xlob%d" % half, [128, D], BF16) as xlob:
                for i, tt in enumerate(tts):
                    SY.wait_ge(sV.h, sV.n)
                    nl = dma(SY, dX, f4a[:], stream_h[tt * 128:(tt + 1) * 128, :])
                    V.wait_ge(dX.h, nl)
                    emit_ln(f4a[:], D, f4b[:], f4c[:], stat[:, 4:5], stat[:, 5:6])
                    V.tensor_copy(xhib[:], f4c[:])
                    V.drain()
                    V.tensor_tensor(f4b[:], f4c[:], xhib[:], op=ALU.subtract)
                    V.drain()
                    n = sV.inc(V.tensor_copy(xlob[:], f4b[:]))
                    V.wait_ge(sV.h, n)
                    emit_transposes(xhib[:], 0, xhiT, i * DCH, DCH, True, nsrc=(sV, n))
                    emit_transposes(xlob[:], 0, xloT, i * DCH, DCH, True)
                barrier()

            with nc.sbuf_tensor("o4a%d" % half, [128, D], F32) as o4a:
                def out_l1(ttl, h2acc, _tts=tts):
                    tt = _tts[ttl]
                    vsync()
                    SY.wait_ge(sG.h, gack())
                    nl = dma(SY, dX, o4a[:], stream_h[tt * 128:(tt + 1) * 128, :])
                    V.wait_ge(dX.h, nl)
                    n = sV.inc(V.tensor_tensor(o4a[:], o4a[:], h2acc[:, ttl * D:(ttl + 1) * D], op=ALU.add))
                    G_.wait_ge(sV.h, n)
                    dma(G_, dG, y_own[tt * 128:(tt + 1) * 128, :], o4a[:])

                emit_ffn(w1qT1_h, w2qT1_h, b1h1, b2h1, [xhiT, xloT], True, len(tts), out_l1, DFF // 8, cc_l1)

    barrier()
    es.close()
    return nc



def _rope_tables(cfg, positions, scale):
    half = 64
    j64 = (np.arange(half, dtype=np.float32) / np.float32(half)).astype(np.float32)
    inv = (np.float32(1.0) / (np.float32(10000.0) ** j64)).astype(np.float32)
    ang = (positions[:, None].astype(np.float32) * inv[None, :]).astype(np.float32)
    cos = np.cos(ang.astype(np.float64)).astype(np.float32) * np.float32(scale)
    sin = np.sin(ang.astype(np.float64)).astype(np.float32) * np.float32(scale)
    return cos.astype(np.float32), sin.astype(np.float32)


def _host_inputs(cfg, I, core):
    seq, rank = core // 2, core % 2
    T, D, DFF, DK0, SL = cfg.T, cfg.D, cfg.DFF, cfg.DK0, cfg.SL
    rows = np.concatenate([np.arange(g_of(j, rank) * 128, g_of(j, rank) * 128 + 128)
                           for j in range(SL)])
    x2 = np.asarray(I["L0_attn_x"]).reshape(cfg.B * T, D)
    m = {}
    m["x_own"] = np.ascontiguousarray(x2[seq * T + rows])
    m["wqT"] = np.ascontiguousarray(np.asarray(I["L0_attn_Wq"]).T)
    m["wkT"] = np.ascontiguousarray(np.asarray(I["L0_attn_Wk"]).T)
    m["wvT"] = np.ascontiguousarray(np.asarray(I["L0_attn_Wv"]).T)
    m["wpT"] = np.ascontiguousarray(np.asarray(I["L0_attn_Wp"]).T)
    m["w1n0"] = np.ascontiguousarray(np.asarray(I["L0_ffn_W1"]))
    m["w2n0"] = np.ascontiguousarray(np.asarray(I["L0_ffn_W2"]))
    m["b1h0"] = np.ascontiguousarray(np.asarray(I["L0_ffn_b1"]).reshape(1, DFF))
    m["b2h0"] = np.ascontiguousarray(np.asarray(I["L0_ffn_b2"]).reshape(1, D))
    m["lnwh"] = np.ascontiguousarray(np.broadcast_to(np.asarray(I["L0_ffn_ln_w"]), (128, D)))
    m["lnbh"] = np.ascontiguousarray(np.broadcast_to(np.asarray(I["L0_ffn_ln_b"]), (128, D)))
    m["gamh"] = np.ascontiguousarray(np.broadcast_to(np.asarray(I["L1_attn_gamma"]), (128, D)))
    m["beth"] = np.ascontiguousarray(np.broadcast_to(np.asarray(I["L1_attn_beta"]), (128, D)))
    m["qkvw"] = np.ascontiguousarray(np.asarray(I["L1_attn_qkv"]))
    m["oprw"] = np.ascontiguousarray(np.asarray(I["L1_attn_o_proj"]))
    m["w1n1"] = np.ascontiguousarray(np.asarray(I["L1_ffn_w1"]))
    m["w2n1"] = np.ascontiguousarray(np.asarray(I["L1_ffn_w2"]))
    fsh = DFF // 8
    m["w1s0"] = np.ascontiguousarray(np.asarray(I["L0_ffn_W1"])[core * fsh:(core + 1) * fsh, :])
    m["w2s0"] = np.ascontiguousarray(np.asarray(I["L0_ffn_W2"])[:, core * fsh:(core + 1) * fsh])
    m["w1s1"] = np.ascontiguousarray(np.asarray(I["L1_ffn_w1"])[core * fsh:(core + 1) * fsh, :])
    m["w2s1"] = np.ascontiguousarray(np.asarray(I["L1_ffn_w2"])[:, core * fsh:(core + 1) * fsh])
    m["b1h1"] = np.ascontiguousarray(np.asarray(I["L1_ffn_b1"]).reshape(1, DFF))
    m["b2h1"] = np.ascontiguousarray(np.asarray(I["L1_ffn_b2"]).reshape(1, D))
    qs = 1.0 / np.sqrt(np.float32(cfg.DH))
    cq = np.empty((128, SL * 64), np.float32); sq = np.empty((128, SL * 64), np.float32)
    ck = np.empty((128, SL * 64), np.float32); sk = np.empty((128, SL * 64), np.float32)
    for j in range(SL):
        pos = np.arange(g_of(j, rank) * 128, g_of(j, rank) * 128 + 128)
        c1, s1 = _rope_tables(cfg, pos, qs)
        c2, s2 = _rope_tables(cfg, pos, 1.0)
        cq[:, j * 64:(j + 1) * 64] = c1; sq[:, j * 64:(j + 1) * 64] = s1
        ck[:, j * 64:(j + 1) * 64] = c2; sk[:, j * 64:(j + 1) * 64] = s2
    m["cosqh"] = cq; m["sinqh"] = sq; m["coskh"] = ck; m["sinkh"] = sk
    tri = np.triu(np.full((128, 128), NEG_BIG, np.float32), 1)
    full = np.full((128, 128), NEG_BIG, np.float32)
    zero = np.zeros((128, 128), np.float32)
    mk = np.empty((128, SL * 256), np.float32)
    for j in range(SL):
        g = g_of(j, rank)
        if g == 2 * j:
            mk[:, j * 256: j * 256 + 128] = tri
            mk[:, j * 256 + 128: j * 256 + 256] = full
        else:
            mk[:, j * 256: j * 256 + 128] = zero
            mk[:, j * 256 + 128: j * 256 + 256] = tri
    m["masksh"] = mk
    m["identh"] = np.eye(128, dtype=np.float32)
    return m, rows


def kernel(**inputs):
    cfg = Cfg()
    nc = build(cfg)
    in_maps = []
    row_list = []
    for core in range(8):
        m, rows = _host_inputs(cfg, inputs, core)
        in_maps.append(m)
        row_list.append((core // 2, rows))
    res = run_bass_kernel_spmd(nc, in_maps, core_ids=list(range(8)))
    out = np.empty((cfg.B, cfg.T, cfg.D), np.float32)
    for core in range(8):
        seq, rows = row_list[core]
        out.reshape(cfg.B * cfg.T, cfg.D)[seq * cfg.T + rows] = res.results[core]["y_own"]
    return out
